# version 05
# baseline (speedup 1.0000x reference)
import numpy as np
import ml_dtypes
from contextlib import ExitStack

import concourse.bass as bass
import concourse.mybir as mybir
from concourse.bass_utils import run_bass_kernel_spmd

F32 = mybir.dt.float32
BF16 = mybir.dt.bfloat16
I32 = mybir.dt.int32
AX = mybir.AxisListType.X
OP = mybir.AluOpType
AF = mybir.ActivationFunctionType

NCORES = 8
D = 2048
DC = 16
NH = 16
HD = 128
JB = 512
T = 1024
LN2C = 0.6931471805599453
INV_LN2 = 1.4426950408889634
ISQ128 = float(1.0 / np.sqrt(128.0))
NEG = -1.0e30

DBG = None



def _lhsT5(w):
    K, F = w.shape
    return np.ascontiguousarray(
        w.reshape(K // 128, 128, F // 128, 128).transpose(2, 1, 0, 3))


def _stage_shared(inp):
    s = {}
    s["wq5"] = _lhsT5(inp["L0_attn_wq"])
    s["wk5"] = _lhsT5(inp["L0_attn_wk"])
    s["wv5r"] = np.ascontiguousarray(inp["L0_attn_wv"].reshape(DC, 128, D))
    s["wo5"] = _lhsT5(inp["L0_attn_wo"])
    s["wf5"] = _lhsT5(np.ascontiguousarray(np.asarray(inp["L0_ffn_weight"]).T))
    s["ffn_bias_t"] = np.ascontiguousarray(
        np.asarray(inp["L0_ffn_bias"]).reshape(DC, 128).T).astype(np.float32)
    s["ffn_ws"] = np.asarray(inp["L0_ffn_weight_scale"]).reshape(1, 1).astype(np.float32)

    wqkv = np.asarray(inp["L1_attn_w_qkv"])
    perm = np.concatenate([
        np.concatenate([np.arange(384 * h + 128 * k, 384 * h + 128 * k + 128)
                        for h in range(NH)])
        for k in range(3)])
    s["wqkv5"] = _lhsT5(np.ascontiguousarray(wqkv[:, perm]))
    s["wd5"] = _lhsT5(np.asarray(inp["L1_attn_w_dense"]))

    s["gw_l"] = np.ascontiguousarray(
        np.asarray(inp["L1_ffn_gate_w"]).T.reshape(DC, 128, 8)).astype(np.float32)
    s["gate_b"] = np.asarray(inp["L1_ffn_gate_b"]).reshape(1, 8).astype(np.float32)
    s["gate_bias"] = np.asarray(inp["L1_ffn_bias"]).reshape(1, 8).astype(np.float32)
    w1 = np.asarray(inp["L1_ffn_w1"])
    s["w15"] = np.ascontiguousarray(
        w1.transpose(0, 2, 1).reshape(8, DC, 128, 11, 128).transpose(0, 3, 2, 1, 4))
    s["b1r"] = np.ascontiguousarray(np.asarray(inp["L1_ffn_b1"]).reshape(88, 128).T).astype(np.float32)
    w2 = np.asarray(inp["L1_ffn_w2"])
    s["w25r"] = np.ascontiguousarray(w2.transpose(0, 2, 1).reshape(8, 11, 128, D))
    s["b2"] = np.ascontiguousarray(np.asarray(inp["L1_ffn_b2"]))

    s["ident_f"] = np.eye(128, dtype=np.float32)
    s["ident_b"] = np.eye(128, dtype=np.float32).astype(ml_dtypes.bfloat16)
    s["ones_f"] = np.ones((128, 512), dtype=np.float32)
    s["ones_b"] = np.ones((128, 128), dtype=np.float32).astype(ml_dtypes.bfloat16)
    oh = np.zeros((NH, 128, NH), dtype=np.float32)
    for h in range(NH):
        oh[h, :, h] = 1.0
    s["onehot_b"] = oh.astype(ml_dtypes.bfloat16)
    inv1 = (1.0 / (np.float32(10000.0) **
                   (np.arange(0, 128, 2, dtype=np.float32) / np.float32(128))))
    emb1 = np.concatenate([inv1, inv1]).astype(np.float32)
    s["cos1"] = np.cos(emb1).astype(np.float32).reshape(128, 1)
    s["sin1"] = np.sin(emb1).astype(np.float32).reshape(128, 1)
    return s


def _stage_core(inp, c):
    d = {}
    x = np.asarray(inp["L0_attn_hidden_states"])
    sl = x[:, JB * c: JB * (c + 1), :].reshape(T, D)
    d["x0T"] = np.ascontiguousarray(sl.T).astype(np.float32)

    pos = np.arange(JB * c, JB * (c + 1), dtype=np.float32)
    inv = (1.0 / (np.float32(10000.0) **
                  (np.arange(0, HD, 2, dtype=np.float32) / np.float32(HD))))
    freqs = pos[:, None] * inv[None, :]
    cosT = np.cos(freqs).astype(np.float32).T
    sinT = np.sin(freqs).astype(np.float32).T
    d["cosT"] = np.ascontiguousarray(np.concatenate([cosT, cosT], axis=1))
    d["sinT"] = np.ascontiguousarray(np.concatenate([sinT, sinT], axis=1))

    kt = np.arange(JB)[:, None]
    qt = np.arange(JB)[None, :]
    m0 = np.where(kt > qt, 0.0, NEG).astype(np.float32)
    m1 = np.zeros((JB, JB), dtype=np.float32)
    m2 = np.where(kt <= qt, 0.0, NEG).astype(np.float32)
    if c < 2:
        m0 = np.full((JB, JB), NEG, dtype=np.float32)
    if c < 1:
        m1 = np.full((JB, JB), NEG, dtype=np.float32)
    mm = np.stack([m0, m1, m2]).reshape(3, 4, 128, JB)
    d["negmask"] = np.ascontiguousarray(mm).astype(ml_dtypes.bfloat16)
    return d




_DVE_DATA_OPS = {
    "tensor_tensor", "tensor_scalar", "tensor_copy", "tensor_reduce",
    "reduce_max", "reduce_sum", "reciprocal", "select", "memset",
    "scalar_tensor_tensor", "tensor_add", "tensor_sub", "tensor_mul",
    "activation", "copy", "add", "mul", "sign", "sqrt", "square", "transpose",
}


class EG:

    def __init__(self, eng):
        self._e = eng
        self.last = None

    def __getattr__(self, name):
        f = getattr(self._e, name)
        if name in _DVE_DATA_OPS:
            def g(*a, **k):
                self._e.drain()
                self.last = f(*a, **k)
                return self.last
            return g
        return f


class Ctx:
    def __init__(self, nc, es):
        self.nc = nc
        self.sem = {}
        for name in ["wld", "kv", "mm", "mm2", "act", "vec", "fre", "fr2",
                     "sto", "cc", "gld", "gwl", "gwl2", "kva", "kvb",
                     "sta", "stb", "tra", "trb", "sva", "svb", "gw3", "gw4"]:
            self.sem[name] = [es.enter_context(nc.semaphore(name)), 0]

    def h(self, name):
        return self.sem[name][0]

    def v(self, name):
        return self.sem[name][1]

    def inc(self, name, n=1):
        self.sem[name][1] += n
        return self.sem[name][1]


def build_nc():
    nc = bass.Bass("TRN2", target_bir_lowering=False)
    es = ExitStack()
    RG = [list(range(NCORES))]

    x_in = nc.dram_tensor("x0T", [D, T], F32, kind="ExternalInput")
    out = nc.dram_tensor("out", [T, D], F32, kind="ExternalOutput")

    def din(name, shape, dt=F32):
        return nc.dram_tensor(name, list(shape), dt, kind="ExternalInput")

    wq5 = din("wq5", (NH, 128, DC, 128))
    wk5 = din("wk5", (NH, 128, DC, 128))
    wv5r = din("wv5r", (DC, 128, D))
    wo5 = din("wo5", (DC, 128, DC, 128))
    wf5 = din("wf5", (DC, 128, DC, 128))
    ffn_bias_t = din("ffn_bias_t", (128, DC))
    ffn_ws = din("ffn_ws", (1, 1))
    wqkv5 = din("wqkv5", (48, 128, DC, 128))
    wd5 = din("wd5", (DC, 128, DC, 128))
    gw_l = din("gw_l", (DC, 128, 8))
    gate_b_in = din("gate_b", (1, 8))
    gate_bias_in = din("gate_bias", (1, 8))
    w15 = din("w15", (8, 11, 128, DC, 128))
    b1r = din("b1r", (128, 88))
    w25r = din("w25r", (8, 11, 128, D))
    b2_in = din("b2", (8, D))
    ident_f_in = din("ident_f", (128, 128))
    ident_b_in = din("ident_b", (128, 128), BF16)
    ones_f_in = din("ones_f", (128, 512))
    ones_b_in = din("ones_b", (128, 128), BF16)
    onehot_in = din("onehot_b", (NH, 128, NH), BF16)
    cos1_in = din("cos1", (128, 1))
    sin1_in = din("sin1", (128, 1))
    cosT_in = din("cosT", (64, T))
    sinT_in = din("sinT", (64, T))
    negm_in = din("negmask", (3, 4, 128, JB), BF16)

    k_send = nc.dram_tensor("k_send", [NH, 128, T], BF16)
    v_send = nc.dram_tensor("v_send", [NH, 8, 128, 128], BF16)
    agk = nc.dram_tensor("agk", [NCORES, NH, 128, T], BF16)
    agv = nc.dram_tensor("agv", [NCORES, NH, 8, 128, 128], BF16)
    band_k = nc.dram_tensor("band_k", [2, NH, 128, T], BF16)
    band_v = nc.dram_tensor("band_v", [2, NH, 8, 128, 128], BF16)
    ar_in = nc.dram_tensor("ar_in", [1, 1], F32)
    ar_out = nc.dram_tensor("ar_out", [1, 1], F32)
    dbg = None
    if DBG is not None:
        dbg = nc.dram_tensor("dbg", [128, DC * T], F32, kind="ExternalOutput")

    with es:
        cx = Ctx(nc, es)

        def sb(name, shape, dt):
            return es.enter_context(nc.sbuf_tensor("s_" + name, list(shape), dt))

        X = sb("X", (128, DC * T), F32)
        ident_f = sb("ident_f", (128, 128), F32)
        ident_b = sb("ident_b", (128, 128), BF16)
        ones_f = sb("ones_f", (128, 512), F32)
        ones_b = sb("ones_b", (128, 128), BF16)
        onehot = sb("onehot", (128, NH * NH), BF16)
        cos1 = sb("cos1", (128, 1), F32)
        sin1 = sb("sin1", (128, 1), F32)
        stat = sb("stat", (128, 1024), F32)
        gbias = sb("gbias", (128, 24), F32)
        WR = sb("WR", (128, 2 * D), BF16)
        sq_ring = sb("sq_ring", (128, 2 * 512), BF16)
        xb_ring = sb("xb_ring", (128, 2 * 512), BF16)
        row_sc = sb("row_sc", (1, 3 * 512), F32)
        bc_sc = sb("bc_sc", (128, 512), F32)

        pb = [es.enter_context(nc.psum_tensor(f"pb{i}", [128, 512], F32))
              for i in range(8)]

        ph_q = ExitStack()
        qbuf = ph_q.enter_context(nc.sbuf_tensor("qbuf", [128, NH * T], BF16))
        ph = ExitStack()
        cosT = ph.enter_context(nc.sbuf_tensor("s_cosT", [64, T], F32))
        sinT = ph.enter_context(nc.sbuf_tensor("s_sinT", [64, T], F32))

        def dbg_dump(src, width=DC * T):
            with nc.Block() as blk:
                @blk.vector
                def _(v):
                    v = EG(v)
                    for nm in ["mm", "mm2", "act", "vec", "sto", "wld", "kv"]:
                        v.wait_ge(cx.h(nm), cx.v(nm))
                    step = 2048
                    for i in range(0, width, step):
                        w = min(step, width - i)
                        v.tensor_copy(stat[:, 0:512], stat[:, 0:512])
                        break
                @blk.gpsimd
                def _(g):
                    for nm in ["mm", "mm2", "act", "vec", "sto", "wld", "kv"]:
                        g.wait_ge(cx.h(nm), cx.v(nm))
                    g.dma_start(out=dbg[:, 0:width], in_=src[:, 0:width]
                                ).then_inc(cx.h("gld"), 16)
                    g.wait_ge(cx.h("gld"), cx.v("gld") + 16)
                    cx.inc("gld", 16)

        with nc.Block() as blk:
            @blk.sync
            def _(e):
                e.dma_start(
                    out=X[:, :].rearrange("p (c t) -> p c t", t=T),
                    in_=x_in.ap().rearrange("(c p) t -> p c t", p=128),
                ).then_inc(cx.h("wld"), 16)
                cx.inc("wld", 16)
                for dst, src in [(ident_f, ident_f_in), (ident_b, ident_b_in),
                                 (ones_f, ones_f_in), (ones_b, ones_b_in),
                                 (cos1, cos1_in), (sin1, sin1_in),
                                 (cosT, cosT_in), (sinT, sinT_in)]:
                    e.dma_start(out=dst[:, :], in_=src[:, :]).then_inc(cx.h("wld"), 16)
                    cx.inc("wld", 16)
                e.dma_start(
                    out=onehot[:, :].rearrange("p (h f) -> p h f", h=NH),
                    in_=onehot_in.ap().rearrange("h p f -> p h f"),
                ).then_inc(cx.h("wld"), 16)
                cx.inc("wld", 16)
                e.dma_start(out=gbias[:, 0:DC], in_=ffn_bias_t[:, :]).then_inc(cx.h("wld"), 16)
                cx.inc("wld", 16)
                e.dma_start(out=gbias[0:1, 16:17], in_=ffn_ws[:, :]).then_inc(cx.h("wld"), 16)
                cx.inc("wld", 16)
                e.wait_ge(cx.h("wld"), cx.v("wld"))

            @blk.vector
            def _(v):
                v = EG(v)
                v.memset(gbias[0:1, 17:18], 1e-5)

        def emit_ln(dst, fp32_cb=None):
            for tt in range(2):
                b_mm = cx.v("mm")
                b_act = cx.v("act")
                b_vec = cx.v("vec")
                with nc.Block() as blk:
                    @blk.vector
                    def _(v, tt=tt):
                        v = EG(v)
                        for dc in range(DC):
                            if dc >= 2:
                                v.wait_ge(cx.h("mm"), b_mm + (dc - 1))
                            v.tensor_copy(
                                xb_ring[:, (dc % 2) * 512:(dc % 2) * 512 + 512],
                                X[:, dc * T + tt * 512: dc * T + tt * 512 + 512]
                            ).then_inc(cx.h("vec"), 1)
                        v.wait_ge(cx.h("mm"), b_mm + DC)
                        v.tensor_scalar(row_sc[0:1, 0:512], pb[6][0:1, 0:512],
                                        1.0 / 2048.0, None, OP.mult)
                        v.tensor_scalar(row_sc[0:1, 512:1024], pb[7][0:1, 0:512],
                                        1.0 / 2048.0, None, OP.mult)
                        v.tensor_tensor(row_sc[0:1, 1024:1536],
                                        row_sc[0:1, 0:512], row_sc[0:1, 0:512],
                                        OP.mult)
                        v.tensor_tensor(row_sc[0:1, 1024:1536],
                                        row_sc[0:1, 512:1024],
                                        row_sc[0:1, 1024:1536],
                                        OP.subtract).then_inc(cx.h("vec"), 1)
                        v.wait_ge(cx.h("act"), b_act + DC + 1)
                        v.reciprocal(row_sc[0:1, 512:1024],
                                     row_sc[0:1, 512:1024]).then_inc(cx.h("vec"), 1)
                        v.wait_ge(cx.h("mm"), b_mm + DC + 1)
                        v.tensor_copy(bc_sc[:, 0:512], pb[4][:, 0:512])
                        for dc in range(DC):
                            col = dc * T + tt * 512
                            v.tensor_tensor(stat[:, 0:512], X[:, col:col + 512],
                                            pb[5][:, 0:512], OP.subtract)
                            if fp32_cb is not None:
                                fp32_cb(v, dc, tt)
                            v.tensor_tensor(dst[:, col:col + 512],
                                            stat[:, 0:512], bc_sc[:, 0:512],
                                            OP.mult).then_inc(cx.h("vec"), 1)
                    cx.inc("vec", 2 * DC + 2)

                    @blk.scalar
                    def _(s, tt=tt):
                        s = EG(s)
                        for dc in range(DC):
                            s.wait_ge(cx.h("vec"), b_vec + dc + 1)
                            if dc >= 2:
                                s.wait_ge(cx.h("mm"), b_mm + (dc - 1))
                            s.activation(
                                sq_ring[:, (dc % 2) * 512:(dc % 2) * 512 + 512],
                                xb_ring[:, (dc % 2) * 512:(dc % 2) * 512 + 512],
                                AF.Square).then_inc(cx.h("act"), 1)
                        s.wait_ge(cx.h("vec"), b_vec + DC + 1)
                        s.activation(row_sc[0:1, 512:1024],
                                     row_sc[0:1, 1024:1536], AF.Sqrt,
                                     bias=gbias[0:1, 17:18]).then_inc(cx.h("act"), 1)
                    cx.inc("act", DC + 1)

                    @blk.tensor
                    def _(t, tt=tt):
                        for dc in range(DC):
                            t.wait_ge(cx.h("act"), b_act + dc + 1)
                            t.matmul(pb[6][0:1, 0:512], ones_b[:, 0:1],
                                     xb_ring[:, (dc % 2) * 512:(dc % 2) * 512 + 512],
                                     start=(dc == 0), stop=(dc == DC - 1))
                            t.matmul(pb[7][0:1, 0:512], ones_b[:, 0:1],
                                     sq_ring[:, (dc % 2) * 512:(dc % 2) * 512 + 512],
                                     start=(dc == 0), stop=(dc == DC - 1)
                                     ).then_inc(cx.h("mm"), 1)
                        t.wait_ge(cx.h("vec"), b_vec + DC + 2)
                        t.matmul(pb[5][:, 0:512], ones_f[0:1, 0:128],
                                 row_sc[0:1, 0:512], start=True, stop=True)
                        t.matmul(pb[4][:, 0:512], ones_f[0:1, 0:128],
                                 row_sc[0:1, 512:1024], start=True,
                                 stop=True).then_inc(cx.h("mm"), 1)
                    cx.inc("mm", DC + 1)

        def emit_proj(w5, n_fj, rhs, evac_fn, n_evac_per_unit):
            b_w = {0: cx.v("gwl"), 1: cx.v("gwl2")}
            b_fr2 = cx.v("fr2")
            b_mm = cx.v("mm")
            with nc.Block() as blk:
                @blk.gpsimd
                def _(g):
                    for fj in range(n_fj):
                        nm = "gwl" if fj % 2 == 0 else "gwl2"
                        if fj >= 2:
                            g.wait_ge(cx.h("mm"), b_mm + 2 * (fj - 1))
                        g.dma_start(out=WR[:, (fj % 2) * D:(fj % 2) * D + D],
                                    in_=w5[fj, :, :, :]).then_inc(cx.h(nm), 16)
                cx.inc("gwl", 16 * ((n_fj + 1) // 2))
                cx.inc("gwl2", 16 * (n_fj // 2))

                @blk.tensor
                def _(t):
                    for fj in range(n_fj):
                        nm = "gwl" if fj % 2 == 0 else "gwl2"
                        t.wait_ge(cx.h(nm), b_w[fj % 2] + 16 * (fj // 2 + 1))
                        for tt in range(2):
                            u = fj * 2 + tt
                            if u >= 2:
                                t.wait_ge(cx.h("fr2"), b_fr2 + (u - 1))
                            for c in range(DC):
                                mi = t.matmul(pb[u % 2][:, 0:512],
                                              WR[:, (fj % 2) * D + c * 128:
                                                 (fj % 2) * D + c * 128 + 128],
                                              rhs[:, c * T + tt * 512:
                                                  c * T + tt * 512 + 512],
                                              start=(c == 0), stop=(c == DC - 1))
                            mi.then_inc(cx.h("mm"), 1)
                cx.inc("mm", n_fj * 2)
                evac_fn(blk, b_mm)

        ln1_bf = ph.enter_context(nc.sbuf_tensor("ln1_bf", [128, DC * T], BF16))

        emit_ln(ln1_bf)
        if DBG == "ln1":
            dbg_dump(ln1_bf)
            ph.close(); ph_q.close()
            _audit(nc, cx)
            return nc, es

        rp_t1 = ph.enter_context(nc.sbuf_tensor("rp_t1", [64, 512], F32))
        rp_t2 = ph.enter_context(nc.sbuf_tensor("rp_t2", [64, 512], F32))
        kr = ph.enter_context(nc.sbuf_tensor("kr", [128, 2 * 512], BF16))
        vft = ph.enter_context(nc.sbuf_tensor("vft", [128, 8 * 512], BF16))

        def rope_evac(v, ps, dst_lo, dst_hi, tt):
            cs = slice(tt * 512, tt * 512 + 512)
            v.tensor_tensor(rp_t1[:, :], ps[0:64, 0:512], cosT[:, cs], OP.mult)
            v.tensor_tensor(rp_t2[:, :], ps[64:128, 0:512], sinT[:, cs], OP.mult)
            v.tensor_tensor(dst_lo, rp_t1[:, :], rp_t2[:, :], OP.subtract)
            v.tensor_tensor(rp_t1[:, :], ps[64:128, 0:512], cosT[:, cs], OP.mult)
            pbr = v.tensor_tensor(rp_t2[:, :], ps[0:64, 0:512], sinT[:, cs],
                                  OP.mult)
            fin = v.tensor_tensor(dst_hi, rp_t1[:, :], rp_t2[:, :], OP.add)
            return pbr, fin

        def evac_q(blk, b_mm):
            @blk.vector
            def _(v):
                v = EG(v)
                for fj in range(NH):
                    for tt in range(2):
                        u = fj * 2 + tt
                        v.wait_ge(cx.h("mm"), b_mm + u + 1)
                        col = fj * T + tt * 512
                        pbr, fin = rope_evac(v, pb[u % 2],
                                             qbuf[0:64, col:col + 512],
                                             qbuf[64:128, col:col + 512], tt)
                        fin.then_inc(cx.h("fr2"), 1)
            cx.inc("fr2", NH * 2)

        emit_proj(wq5, NH, ln1_bf, evac_q, 1)

        def evac_k(blk, b_mm):
            b_vec = cx.v("vec")
            b_st = {0: cx.v("sta"), 1: cx.v("stb")}
            @blk.vector
            def _(v):
                v = EG(v)
                for fj in range(NH):
                    for tt in range(2):
                        u = fj * 2 + tt
                        v.wait_ge(cx.h("mm"), b_mm + u + 1)
                        if u >= 2:
                            nm = "sta" if u % 2 == 0 else "stb"
                            v.wait_ge(cx.h(nm), b_st[u % 2] + 16 * ((u - 2) // 2 + 1))
                        kc_ = slice((u % 2) * 512, (u % 2) * 512 + 512)
                        pbr, fin = rope_evac(v, pb[u % 2], kr[0:64, kc_],
                                             kr[64:128, kc_], tt)
                        pbr.then_inc(cx.h("fr2"), 1)
                        fin.then_inc(cx.h("vec"), 1)
            cx.inc("fr2", NH * 2)
            cx.inc("vec", NH * 2)

            @blk.sync
            def _(e):
                for fj in range(NH):
                    for tt in range(2):
                        u = fj * 2 + tt
                        nm = "sta" if u % 2 == 0 else "stb"
                        e.wait_ge(cx.h("vec"), b_vec + u + 1)
                        e.dma_start(
                            out=k_send[fj, :, tt * 512:tt * 512 + 512],
                            in_=kr[:, (u % 2) * 512:(u % 2) * 512 + 512],
                        ).then_inc(cx.h(nm), 16)
            cx.inc("sta", 16 * NH)
            cx.inc("stb", 16 * NH)

        emit_proj(wk5, NH, ln1_bf, evac_k, 2)

        for dht in range(4):
            b_w = {0: cx.v("gwl"), 1: cx.v("gwl2")}
            b_fre = cx.v("fre")
            b_vec = cx.v("vec")
            with nc.Block() as blk:
                @blk.gpsimd
                def _(g, dht=dht):
                    for dc in range(DC):
                        nm = "gwl" if dc % 2 == 0 else "gwl2"
                        if dc >= 2:
                            g.wait_ge(cx.h("fre"), b_fre + (dc - 1))
                        g.dma_start(
                            out=WR[:, (dc % 2) * D:(dc % 2) * D + 512],
                            in_=wv5r[dc, :, dht * 512:dht * 512 + 512],
                        ).then_inc(cx.h(nm), 16)
                cx.inc("gwl", 16 * (DC // 2))
                cx.inc("gwl2", 16 * (DC // 2))

                @blk.tensor
                def _(t, dht=dht):
                    for dc in range(DC):
                        nm = "gwl" if dc % 2 == 0 else "gwl2"
                        t.wait_ge(cx.h(nm), b_w[dc % 2] + 16 * (dc // 2 + 1))
                        for ktc in range(8):
                            mi = t.matmul(pb[ktc][:, 0:512],
                                          ln1_bf[:, dc * T + ktc * 128:
                                                 dc * T + ktc * 128 + 128],
                                          WR[:, (dc % 2) * D:(dc % 2) * D + 512],
                                          start=(dc == 0), stop=(dc == DC - 1))
                        mi.then_inc(cx.h("fre"), 1)
                cx.inc("fre", DC)

                b_sv = {0: cx.v("sva"), 1: cx.v("svb")}

                @blk.vector
                def _(v, dht=dht):
                    v = EG(v)
                    v.wait_ge(cx.h("sva"), b_sv[0])
                    v.wait_ge(cx.h("svb"), b_sv[1])
                    v.wait_ge(cx.h("fre"), b_fre + DC)
                    for ktc in range(8):
                        v.tensor_copy(vft[:, ktc * 512:ktc * 512 + 512],
                                      pb[ktc][:, 0:512]).then_inc(cx.h("vec"), 1)
                cx.inc("vec", 8)

                @blk.sync
                def _(e, dht=dht):
                    for ktc in range(8):
                        p = ktc % 2
                        snm = "sva" if p == 0 else "svb"
                        e.wait_ge(cx.h("vec"), b_vec + ktc + 1)
                        for sub in range(4):
                            hc = dht * 4 + sub
                            e.dma_start(out=v_send[hc, ktc, :, :],
                                        in_=vft[:, ktc * 512 + sub * 128:
                                                ktc * 512 + sub * 128 + 128],
                                        ).then_inc(cx.h(snm), 16)
                cx.inc("sva", 16 * 16)
                cx.inc("svb", 16 * 16)

        if DBG == "qkv":
            dbg_dump(qbuf, NH * T)
            ph.close(); ph_q.close()
            _audit(nc, cx)
            return nc, es

        with nc.Block() as blk:
            @blk.gpsimd
            def _(g):
                for nm in ["sta", "stb", "sva", "svb"]:
                    g.wait_ge(cx.h(nm), cx.v(nm))
                g.collective_compute("AllGather", OP.bypass, replica_groups=RG,
                                     ins=[k_send.ap()], outs=[agk.ap()]
                                     ).then_inc(cx.h("cc"), 1)
                g.collective_compute("AllGather", OP.bypass, replica_groups=RG,
                                     ins=[v_send.ap()], outs=[agv.ap()]
                                     ).then_inc(cx.h("cc"), 1)
                cx.inc("cc", 2)

        ph.close()
        ph = ExitStack()
        ctx_bf = ph.enter_context(nc.sbuf_tensor("ctx_bf", [128, NH * T], BF16))
        kband = ph.enter_context(nc.sbuf_tensor("kband", [128, 2 * 1536], BF16))
        vband = ph.enter_context(nc.sbuf_tensor("vband", [128, 2 * 1536], BF16))
        negm = ph.enter_context(nc.sbuf_tensor("negm", [128, 12 * 512], BF16))
        ptile = ph.enter_context(nc.sbuf_tensor("ptile", [128, 3 * 512], BF16))
        ksq = ph.enter_context(nc.sbuf_tensor("ksq", [128, 1536], F32))
        mneg = ph.enter_context(nc.sbuf_tensor("mneg", [128, 1], F32))
        rcp_row = ph.enter_context(nc.sbuf_tensor("rcp_row", [1, 512], F32))

        with nc.Block() as blk:
            @blk.sync
            def _(e):
                e.dma_start(
                    out=negm[:, :].rearrange("p (m q j) -> p m q j", m=3, q=4),
                    in_=negm_in.ap().rearrange("m q p j -> p m q j"),
                ).then_inc(cx.h("kv"), 16)
                cx.inc("kv", 16)
                e.wait_ge(cx.h("cc"), cx.v("cc"))
                r1 = e.alloc_register()
                r2 = e.alloc_register()
                e.cc_rank_ld(r1, replica_groups=RG)
                e.reg_add(r2, r1, 6)
                e.reg_mod(r2, r2, 8)
                e.reg_add(r1, r1, 7)
                e.reg_mod(r1, r1, 8)
                s2 = e.snap(r2)
                s1 = e.snap(r1)
                e.dma_start(out=band_k[0, :, :, :],
                            in_=agk[bass.ds(s2, 1), :, :, :]).then_inc(cx.h("kv"), 16)
                e.dma_start(out=band_v[0, :, :, :, :],
                            in_=agv[bass.ds(s2, 1), :, :, :, :]).then_inc(cx.h("kv"), 16)
                e.dma_start(out=band_k[1, :, :, :],
                            in_=agk[bass.ds(s1, 1), :, :, :]).then_inc(cx.h("kv"), 16)
                e.dma_start(out=band_v[1, :, :, :, :],
                            in_=agv[bass.ds(s1, 1), :, :, :, :]).then_inc(cx.h("kv"), 16)
                cx.inc("kv", 64)

        mm2_hist = []
        for bh in range(2 * NH):
            b, h = bh // NH, bh % NH
            buf = bh % 2
            knm = "kva" if buf == 0 else "kvb"
            b_kv = cx.v(knm)
            b_kv0 = cx.v("kv")
            b_mm = cx.v("mm")
            b_mm2 = cx.v("mm2")
            b_act = cx.v("act")
            b_vec = cx.v("vec")
            mm2_hist.append(b_mm2)
            qcol = slice(h * T + b * 512, h * T + b * 512 + 512)
            with nc.Block() as blk:
                @blk.sync
                def _(e, b=b, h=h, buf=buf, bh=bh, knm=knm):
                    if bh < 2:
                        e.wait_ge(cx.h("kv"), b_kv0)
                    else:
                        e.wait_ge(cx.h("mm2"), mm2_hist[bh - 1])
                    for m in range(3):
                        if m < 2:
                            ksrc = band_k[m, h, :, b * 512:b * 512 + 512]
                        else:
                            ksrc = k_send[h, :, b * 512:b * 512 + 512]
                        e.dma_start(
                            out=kband[:, buf * 1536 + m * 512:
                                      buf * 1536 + m * 512 + 512],
                            in_=ksrc).then_inc(cx.h(knm), 16)
                        for sub in range(4):
                            if m < 2:
                                vsrc = band_v[m, h, b * 4 + sub, :, :]
                            else:
                                vsrc = v_send[h, b * 4 + sub, :, :]
                            e.dma_start(
                                out=vband[:, buf * 1536 + m * 512 + sub * 128:
                                          buf * 1536 + m * 512 + sub * 128 + 128],
                                in_=vsrc).then_inc(cx.h(knm), 16)
                cx.inc(knm, 16 * 15)

                @blk.scalar
                def _(s, b=b, h=h, buf=buf, b_kv=b_kv, b_mm=b_mm,
                      b_mm2=b_mm2, b_act=b_act, b_vec=b_vec, qcol=qcol):
                    s = EG(s)
                    s.wait_ge(cx.h(knm), b_kv + 16 * 15)
                    s.activation(ksq[:, :], kband[:, buf * 1536:buf * 1536 + 1536],
                                 AF.Square).then_inc(cx.h("act"), 1)
                    s.activation(stat[:, 512:1024], qbuf[:, qcol],
                                 AF.Square).then_inc(cx.h("act"), 1)
                    s.wait_ge(cx.h("vec"), b_vec + 2)
                    s.activation(stat[0:1, 2:3], stat[0:1, 1:2], AF.Sqrt,
                                 scale=1.0 / 128.0).then_inc(cx.h("act"), 1)
                    s.wait_ge(cx.h("vec"), b_vec + 4)
                    for j in range(12):
                        s.wait_ge(cx.h("mm"), b_mm + j + 1)
                        if j >= 3:
                            s.wait_ge(cx.h("mm2"), b_mm2 + 2 + 2 * (j - 2))
                        s.activation(ptile[:, (j % 3) * 512:(j % 3) * 512 + 512],
                                     pb[j % 2][:, 0:512], AF.Exp,
                                     bias=mneg[:, 0:1],
                                     scale=ISQ128).then_inc(cx.h("act"), 1)
                cx.inc("act", 15)

                @blk.vector
                def _(v, b=b, h=h, b_kv=b_kv, b_mm=b_mm, b_mm2=b_mm2,
                      b_act=b_act, b_vec=b_vec, qcol=qcol):
                    v = EG(v)
                    v.wait_ge(cx.h("act"), b_act + 1)
                    v.reduce_max(stat[:, 0:1], ksq[:, :], axis=AX)
                    v.last.then_inc(cx.h("vec"), 1)
                    v.wait_ge(cx.h("mm2"), b_mm2 + 1)
                    v.reduce_max(stat[0:1, 0:1], pb[6][0:1, 0:512], axis=AX)
                    v.tensor_tensor(stat[0:1, 1:2], stat[0:1, 0:1],
                                    pb[7][0:1, 0:1], OP.mult)
                    v.last.then_inc(cx.h("vec"), 1)
                    v.wait_ge(cx.h("act"), b_act + 3)
                    v.tensor_scalar(stat[0:1, 3:4], stat[0:1, 2:3], -1.0, None,
                                    OP.mult).then_inc(cx.h("vec"), 1)
                    v.wait_ge(cx.h("mm2"), b_mm2 + 2)
                    v.tensor_copy(mneg[:, 0:1], pb[7][:, 4:5]).then_inc(cx.h("vec"), 1)
                    v.wait_ge(cx.h("mm2"), b_mm2 + 2 + 24)
                    v.reciprocal(rcp_row[0:1, :], pb[3][0:1, 0:512])
                    v.last.then_inc(cx.h("vec"), 1)
                    v.wait_ge(cx.h("mm2"), b_mm2 + 27)
                    v.tensor_copy(bc_sc[:, 0:512], pb[5][:, 0:512])
                    v.tensor_tensor(ctx_bf[:, qcol], pb[2][:, 0:512],
                                    bc_sc[:, 0:512], OP.mult
                                    ).then_inc(cx.h("vec"), 1)
                cx.inc("vec", 6)

                @blk.tensor
                def _(t, b=b, h=h, buf=buf, b_kv=b_kv, b_mm=b_mm,
                      b_mm2=b_mm2, b_act=b_act, b_vec=b_vec, qcol=qcol,
                      knm=knm):
                    t.wait_ge(cx.h("vec"), b_vec + 1)
                    t.matmul(pb[7][0:1, 0:1], ones_f[:, 0:1], stat[:, 0:1],
                             start=True, stop=True)
                    t.wait_ge(cx.h("act"), b_act + 2)
                    t.matmul(pb[6][0:1, 0:512], ones_f[:, 0:1],
                             stat[:, 512:1024], start=True,
                             stop=True).then_inc(cx.h("mm2"), 1)
                    t.wait_ge(cx.h("vec"), b_vec + 3)
                    t.matmul(pb[7][:, 4:5], ones_f[0:1, 0:128], stat[0:1, 3:4],
                             start=True, stop=True).then_inc(cx.h("mm2"), 1)
                    t.wait_ge(cx.h(knm), b_kv + 16 * 15)
                    for j in range(13):
                        if j < 12:
                            if j >= 2:
                                t.wait_ge(cx.h("act"), b_act + 3 + (j - 2) + 1)
                            t.matmul(pb[j % 2][:, 0:512],
                                     kband[:, buf * 1536 + j * 128:
                                           buf * 1536 + j * 128 + 128],
                                     qbuf[:, qcol], start=True, stop=False)
                            t.matmul(pb[j % 2][:, 0:512], ident_b[:, :],
                                     negm[:, j * 512:j * 512 + 512],
                                     start=False, stop=True).then_inc(cx.h("mm"), 1)
                        if j >= 1:
                            jj = j - 1
                            t.wait_ge(cx.h("act"), b_act + 3 + jj + 1)
                            t.matmul(pb[2][:, 0:512],
                                     vband[:, buf * 1536 + jj * 128:
                                           buf * 1536 + jj * 128 + 128],
                                     ptile[:, (jj % 3) * 512:(jj % 3) * 512 + 512],
                                     start=(jj == 0), stop=(jj == 11))
                            t.matmul(pb[3][0:1, 0:512], ones_b[:, 0:1],
                                     ptile[:, (jj % 3) * 512:(jj % 3) * 512 + 512],
                                     start=(jj == 0), stop=(jj == 11)
                                     ).then_inc(cx.h("mm2"), 2)
                    t.wait_ge(cx.h("vec"), b_vec + 5)
                    t.matmul(pb[5][:, 0:512], ones_f[0:1, 0:128], rcp_row[0:1, :],
                             start=True, stop=True).then_inc(cx.h("mm2"), 1)
                cx.inc("mm", 12)
                cx.inc("mm2", 27)

        if DBG == "attn0":
            dbg_dump(ctx_bf, NH * T)
            ph.close(); ph_q.close()
            _audit(nc, cx)
            return nc, es

        def evac_resid(blk, b_mm):
            @blk.vector
            def _(v):
                v = EG(v)
                for fj in range(DC):
                    for tt in range(2):
                        u = fj * 2 + tt
                        v.wait_ge(cx.h("mm"), b_mm + u + 1)
                        col = fj * T + tt * 512
                        v.tensor_tensor(X[:, col:col + 512], pb[u % 2][:, 0:512],
                                        X[:, col:col + 512], OP.add
                                        ).then_inc(cx.h("fr2"), 1)
            cx.inc("fr2", DC * 2)

        emit_proj(wo5, DC, ctx_bf, evac_resid, 1)
        if DBG == "x1":
            dbg_dump(X)
            ph.close(); ph_q.close()
            _audit(nc, cx)
            return nc, es

        ph.close()
        ph_q.close()
        ph = ExitStack()
        ln2 = ph.enter_context(nc.sbuf_tensor("ln2", [128, DC * T], F32))
        xq_bf = ph.enter_context(nc.sbuf_tensor("xq_bf", [128, DC * T], BF16))

        emit_ln(ln2)
        if DBG == "ln2":
            dbg_dump(ln2)
            ph.close()
            _audit(nc, cx)
            return nc, es

        amax = ph.enter_context(nc.sbuf_tensor("amax", [128, 1], F32))
        b_vec = cx.v("vec")
        b_mm2 = cx.v("mm2")
        with nc.Block() as blk:
            @blk.vector
            def _(v):
                v = EG(v)
                for dc in range(DC):
                    v.tensor_reduce(stat[:, 0:1], ln2[:, dc * T:dc * T + T],
                                    axis=AX, op=OP.max,
                                    apply_absolute_value=True)
                    if dc == 0:
                        v.tensor_copy(amax[:, :], stat[:, 0:1])
                    else:
                        v.tensor_tensor(amax[:, :], amax[:, :], stat[:, 0:1],
                                        OP.max)
                v.last.then_inc(cx.h("vec"), 1)
                v.wait_ge(cx.h("mm2"), b_mm2 + 1)
                v.reduce_max(stat[0:1, 0:1], pb[6][0:1, 0:128], axis=AX)
                v.last.then_inc(cx.h("vec"), 1)
            cx.inc("vec", 2)

            @blk.tensor
            def _(t):
                t.wait_ge(cx.h("vec"), b_vec + 1)
                t.transpose(pb[6][0:1, 0:128], amax[:, 0:1],
                            ident_f[:, :]).then_inc(cx.h("mm2"), 1)
            cx.inc("mm2", 1)

            @blk.gpsimd
            def _(g):
                g.wait_ge(cx.h("vec"), b_vec + 2)
                g.dma_start(out=ar_in[:, :], in_=stat[0:1, 0:1]).then_inc(cx.h("gld"), 16)
                g.wait_ge(cx.h("gld"), cx.v("gld") + 16)
                g.collective_compute("AllReduce", OP.max, replica_groups=RG,
                                     ins=[ar_in.ap()], outs=[ar_out.ap()]
                                     ).then_inc(cx.h("cc"), 1)
                g.wait_ge(cx.h("cc"), cx.v("cc") + 1)
                g.dma_start(out=stat[0:1, 4:5], in_=ar_out[:, :]).then_inc(cx.h("gld"), 16)
                cx.inc("gld", 32)
                cx.inc("cc", 1)

        b_gld = cx.v("gld")
        b_vec = cx.v("vec")
        b_mm2 = cx.v("mm2")
        with nc.Block() as blk:
            @blk.vector
            def _(v):
                v = EG(v)
                v.wait_ge(cx.h("gld"), b_gld)
                v.tensor_scalar(stat[0:1, 5:6], stat[0:1, 4:5], 1.0 / 448.0,
                                1e-7, OP.mult, OP.max)
                v.reciprocal(stat[0:1, 6:7], stat[0:1, 5:6])
                v.last.then_inc(cx.h("vec"), 1)
                v.wait_ge(cx.h("mm2"), b_mm2 + 1)
                v.tensor_copy(gbias[:, 18:19], pb[6][:, 0:1])
                v.tensor_copy(gbias[:, 19:20], pb[6][:, 1:2])
                v.last.then_inc(cx.h("vec"), 1)
            cx.inc("vec", 2)

            @blk.tensor
            def _(t):
                t.wait_ge(cx.h("vec"), b_vec + 1)
                t.matmul(pb[6][:, 0:1], ones_f[0:1, 0:128], stat[0:1, 5:6],
                         start=True, stop=True)
                t.matmul(pb[6][:, 1:2], ones_f[0:1, 0:128], stat[0:1, 6:7],
                         start=True, stop=True).then_inc(cx.h("mm2"), 1)
            cx.inc("mm2", 1)

        qs1 = ph.enter_context(nc.sbuf_tensor("qs1", [128, T], F32))
        qs2 = ph.enter_context(nc.sbuf_tensor("qs2", [128, T], F32))
        qs3 = ph.enter_context(nc.sbuf_tensor("qs3", [128, T], F32))
        qsi = ph.enter_context(nc.sbuf_tensor("qsi", [128, T], I32))

        for dc in range(DC):
            b_act = cx.v("act")
            b_vec = cx.v("vec")
            with nc.Block() as blk:
                @blk.scalar
                def _(s, dc=dc, b_act=b_act, b_vec=b_vec):
                    s = EG(s)
                    s.activation(qs1[:, :], ln2[:, dc * T:dc * T + T], AF.Abs,
                                 scale=gbias[:, 19:20]).then_inc(cx.h("act"), 1)
                    s.wait_ge(cx.h("vec"), b_vec + 1)
                    s.activation(qs2[:, :], qs1[:, :], AF.Ln).then_inc(cx.h("act"), 1)
                    s.wait_ge(cx.h("vec"), b_vec + 2)
                    s.activation(qs2[:, :], qs3[:, :], AF.Exp,
                                 scale=LN2C).then_inc(cx.h("act"), 1)
                    s.activation(qs3[:, :], qs3[:, :], AF.Exp,
                                 scale=-LN2C).then_inc(cx.h("act"), 1)
                    s.wait_ge(cx.h("vec"), b_vec + 3)
                    s.activation(qs3[:, :], ln2[:, dc * T:dc * T + T],
                                 AF.Sign).then_inc(cx.h("act"), 1)
                cx.inc("act", 5)

                @blk.vector
                def _(v, dc=dc, b_act=b_act, b_vec=b_vec):
                    v = EG(v)
                    v.wait_ge(cx.h("act"), b_act + 1)
                    v.tensor_scalar(qs1[:, :], qs1[:, :], 1e-30, None, OP.max)
                    v.last.then_inc(cx.h("vec"), 1)
                    v.wait_ge(cx.h("act"), b_act + 2)
                    v.tensor_scalar(qs2[:, :], qs2[:, :], INV_LN2, 131.0,
                                    OP.mult, OP.add)
                    v.tensor_copy(qsi[:, :], qs2[:, :])
                    v.tensor_copy(qs3[:, :], qsi[:, :])
                    v.tensor_tensor(stat[:, 0:512], qs3[:, 0:512],
                                    qs2[:, 0:512], OP.is_gt)
                    v.tensor_tensor(stat[:, 512:1024], qs3[:, 512:1024],
                                    qs2[:, 512:1024], OP.is_gt)
                    v.tensor_tensor(qs3[:, 0:512], qs3[:, 0:512],
                                    stat[:, 0:512], OP.subtract)
                    v.tensor_tensor(qs3[:, 512:1024], qs3[:, 512:1024],
                                    stat[:, 512:1024], OP.subtract)
                    v.tensor_scalar(qs3[:, :], qs3[:, :], -134.0, -9.0,
                                    OP.add, OP.max)
                    v.last.then_inc(cx.h("vec"), 1)
                    v.wait_ge(cx.h("act"), b_act + 4)
                    v.tensor_tensor(stat[:, 0:1024], qs1[:, :],
                                    qs3[:, :], OP.mult)
                    v.last.then_inc(cx.h("vec"), 1)
                    v.tensor_copy(qsi[:, :], stat[:, 0:1024])
                    v.tensor_copy(stat[:, 0:1024], qsi[:, :])
                    v.tensor_tensor(stat[:, 0:1024], stat[:, 0:1024],
                                    qs2[:, :], OP.mult)
                    v.tensor_tensor(qs1[:, :], qs1[:, :], stat[:, 0:1024],
                                    OP.subtract)
                    v.tensor_scalar(qs2[:, :], qs2[:, :], 0.5, None, OP.mult)
                    v.tensor_tensor(qs1[:, :], qs1[:, :], qs2[:, :], OP.is_gt)
                    v.tensor_scalar(qs2[:, :], qs2[:, :], 2.0, None, OP.mult)
                    v.tensor_tensor(qs1[:, :], qs1[:, :], qs2[:, :], OP.mult)
                    v.tensor_tensor(stat[:, 0:1024], stat[:, 0:1024],
                                    qs1[:, :], OP.add)
                    v.wait_ge(cx.h("act"), b_act + 5)
                    v.tensor_tensor(xq_bf[:, dc * T:dc * T + T],
                                    stat[:, 0:1024], qs3[:, :],
                                    OP.mult).then_inc(cx.h("vec"), 1)
                cx.inc("vec", 4)

        if DBG == "xq":
            dbg_dump(xq_bf)
            ph.close()
            _audit(nc, cx)
            return nc, es

        b_vec = cx.v("vec")
        b_mm2 = cx.v("mm2")
        with nc.Block() as blk:
            @blk.tensor
            def _(t):
                t.matmul(pb[6][:, 3:4], ones_f[0:1, 0:128], gbias[0:1, 16:17],
                         start=True, stop=True).then_inc(cx.h("mm2"), 1)
            cx.inc("mm2", 1)

            @blk.vector
            def _(v):
                v = EG(v)
                v.wait_ge(cx.h("mm2"), b_mm2 + 1)
                v.tensor_copy(gbias[:, 20:21], pb[6][:, 3:4]).then_inc(cx.h("vec"), 1)
            cx.inc("vec", 1)

        def evac_ffn(blk, b_mm):
            b_act = cx.v("act")
            @blk.scalar
            def _(s):
                s = EG(s)
                for fj in range(DC):
                    for tt in range(2):
                        u = fj * 2 + tt
                        s.wait_ge(cx.h("mm"), b_mm + u + 1)
                        if u >= 1:
                            s.wait_ge(cx.h("fr2"), cx_fr2_base + u)
                        s.activation(bc_sc[:, 0:512], pb[u % 2][:, 0:512],
                                     AF.Identity, bias=gbias[:, fj:fj + 1],
                                     scale=gbias[:, 20:21]).then_inc(cx.h("act"), 1)
            cx.inc("act", DC * 2)

            @blk.vector
            def _(v):
                v = EG(v)
                for fj in range(DC):
                    for tt in range(2):
                        u = fj * 2 + tt
                        v.wait_ge(cx.h("act"), b_act + u + 1)
                        col = fj * T + tt * 512
                        v.tensor_tensor(X[:, col:col + 512], bc_sc[:, 0:512],
                                        X[:, col:col + 512], OP.add
                                        ).then_inc(cx.h("fr2"), 1)
            cx.inc("fr2", DC * 2)

        cx_fr2_base = cx.v("fr2")
        emit_proj(wf5, DC, xq_bf, evac_ffn, 1)
        if DBG == "x2":
            dbg_dump(X)
            ph.close()
            _audit(nc, cx)
            return nc, es

        ph.close()
        ph = ExitStack()
        qbuf1 = ph.enter_context(nc.sbuf_tensor("qbuf1", [128, NH * 512], BF16))
        kbuf1 = ph.enter_context(nc.sbuf_tensor("kbuf1", [128, NH * T], BF16))
        vbuf1 = ph.enter_context(nc.sbuf_tensor("vbuf1", [128, NH * T], BF16))
        rp1 = ph.enter_context(nc.sbuf_tensor("rp1", [64, 2 * 512], F32))
        ph3 = ExitStack()
        ln3_bf = ph3.enter_context(nc.sbuf_tensor("ln3_bf", [128, DC * T], BF16))

        emit_ln(ln3_bf)
        if DBG == "ln3":
            dbg_dump(ln3_bf)
            ph3.close(); ph.close()
            _audit(nc, cx)
            return nc, es

        units = []
        upref = {}
        for fj in range(48):
            upref[fj] = len(units)
            tts = (1,) if fj < NH else (0, 1)
            for tt in tts:
                units.append((fj, tt))
        b_w = {0: cx.v("gwl"), 1: cx.v("gwl2")}
        b_mm = cx.v("mm")
        b_fr2 = cx.v("fr2")
        with nc.Block() as blk:
            @blk.gpsimd
            def _(g):
                for fj in range(48):
                    nm = "gwl" if fj % 2 == 0 else "gwl2"
                    if fj >= 2:
                        g.wait_ge(cx.h("mm"), b_mm + upref[fj - 1])
                    g.dma_start(out=WR[:, (fj % 2) * D:(fj % 2) * D + D],
                                in_=wqkv5[fj, :, :, :]).then_inc(cx.h(nm), 16)
            cx.inc("gwl", 16 * 24)
            cx.inc("gwl2", 16 * 24)

            @blk.tensor
            def _(t):
                for u, (fj, tt) in enumerate(units):
                    nm = "gwl" if fj % 2 == 0 else "gwl2"
                    t.wait_ge(cx.h(nm), b_w[fj % 2] + 16 * (fj // 2 + 1))
                    if u >= 2:
                        t.wait_ge(cx.h("fr2"), b_fr2 + (u - 1))
                    for c in range(DC):
                        mi = t.matmul(pb[u % 2][:, 0:512],
                                      WR[:, (fj % 2) * D + c * 128:
                                         (fj % 2) * D + c * 128 + 128],
                                      ln3_bf[:, c * T + tt * 512:
                                             c * T + tt * 512 + 512],
                                      start=(c == 0), stop=(c == DC - 1))
                    mi.then_inc(cx.h("mm"), 1)
            cx.inc("mm", len(units))

            @blk.vector
            def _(v):
                v = EG(v)

                def rope1(ps, dst_lo, dst_hi):
                    c1 = cos1[0:64, 0:1]
                    s1_ = sin1[0:64, 0:1]
                    v.tensor_scalar(rp1[:, 0:512], ps[0:64, 0:512], c1, None,
                                    OP.mult)
                    v.tensor_scalar(rp1[:, 512:1024], ps[64:128, 0:512], s1_,
                                    None, OP.mult)
                    v.tensor_tensor(dst_lo, rp1[:, 0:512], rp1[:, 512:1024],
                                    OP.subtract)
                    v.tensor_scalar(rp1[:, 0:512], ps[64:128, 0:512], c1, None,
                                    OP.mult)
                    v.tensor_scalar(rp1[:, 512:1024], ps[0:64, 0:512], s1_,
                                    None, OP.mult)
                    return v.tensor_tensor(dst_hi, rp1[:, 0:512],
                                           rp1[:, 512:1024], OP.add)

                for u, (fj, tt) in enumerate(units):
                    v.wait_ge(cx.h("mm"), b_mm + u + 1)
                    ps = pb[u % 2]
                    if fj < NH:
                        col = fj * 512
                        fin = rope1(ps, qbuf1[0:64, col:col + 512],
                                    qbuf1[64:128, col:col + 512])
                    elif fj < 2 * NH:
                        h = fj - NH
                        col = h * T + tt * 512
                        if tt == 0:
                            fin = v.tensor_copy(kbuf1[:, col:col + 512],
                                                ps[:, 0:512])
                        else:
                            fin = rope1(ps, kbuf1[0:64, col:col + 512],
                                        kbuf1[64:128, col:col + 512])
                    else:
                        h = fj - 2 * NH
                        col = h * T + tt * 512
                        fin = v.tensor_copy(vbuf1[:, col:col + 512],
                                            ps[:, 0:512])
                    fin.then_inc(cx.h("fr2"), 1)
            cx.inc("fr2", len(units))

        ph3.close()
        if DBG == "qkv1":
            dbg_dump(kbuf1, NH * T)
            ph.close()
            _audit(nc, cx)
            return nc, es

        zb = ph.enter_context(nc.sbuf_tensor("zb", [128, 2 * 1024], BF16))
        pp = ph.enter_context(nc.sbuf_tensor("pp", [1, 2 * 512], F32))
        sc0 = ph.enter_context(nc.sbuf_tensor("sc0", [1, 3 * 512], F32))
        ct1 = ph.enter_context(nc.sbuf_tensor("ct1", [128, 2 * 512], F32))

        b_mm2 = cx.v("mm2")
        b_vec = cx.v("vec")
        b_act = cx.v("act")
        with nc.Block() as blk:
            @blk.vector
            def _(v):
                v = EG(v)
                for h in range(NH):
                    if h >= 2:
                        v.wait_ge(cx.h("mm2"), b_mm2 + 4 * (h - 2) + 2)
                    v.tensor_tensor(zb[:, (h % 2) * 1024:(h % 2) * 1024 + 512],
                                    qbuf1[:, h * 512:h * 512 + 512],
                                    kbuf1[:, h * T:h * T + 512], OP.mult)
                    v.tensor_tensor(zb[:, (h % 2) * 1024 + 512:
                                       (h % 2) * 1024 + 1024],
                                    qbuf1[:, h * 512:h * 512 + 512],
                                    kbuf1[:, h * T + 512:h * T + 1024],
                                    OP.mult).then_inc(cx.h("vec"), 1)
                    v.wait_ge(cx.h("mm2"), b_mm2 + 4 * h + 2)
                    v.tensor_copy(sc0[0:1, 0:512],
                                  pb[4][0:1, 0:512])
                    v.tensor_tensor(sc0[0:1, 512:1024], sc0[0:1, 0:512],
                                    pb[5][0:1, 0:512], OP.max)
                    v.tensor_tensor(sc0[0:1, 1024:1536], sc0[0:1, 0:512],
                                    sc0[0:1, 512:1024], OP.subtract)
                    v.tensor_tensor(sc0[0:1, 0:512], pb[5][0:1, 0:512],
                                    sc0[0:1, 512:1024],
                                    OP.subtract).then_inc(cx.h("vec"), 1)
                    v.wait_ge(cx.h("act"), b_act + 2 * h + 2)
                    v.tensor_tensor(sc0[0:1, 512:1024], pp[0:1, 0:512],
                                    pp[0:1, 512:1024], OP.add)
                    v.reciprocal(sc0[0:1, 512:1024], sc0[0:1, 512:1024])
                    v.tensor_tensor(pp[0:1, 0:512], pp[0:1, 0:512],
                                    sc0[0:1, 512:1024], OP.mult)
                    v.tensor_tensor(pp[0:1, 512:1024], pp[0:1, 512:1024],
                                    sc0[0:1, 512:1024],
                                    OP.mult).then_inc(cx.h("vec"), 1)
                    v.wait_ge(cx.h("mm2"), b_mm2 + 4 * h + 4)
                    v.tensor_tensor(ct1[:, 0:512], pb[6][:, 0:512],
                                    vbuf1[:, h * T:h * T + 512], OP.mult)
                    v.tensor_tensor(ct1[:, 512:1024], pb[7][:, 0:512],
                                    vbuf1[:, h * T + 512:h * T + 1024],
                                    OP.mult)
                    v.tensor_tensor(vbuf1[:, h * T + 512:h * T + 1024],
                                    ct1[:, 0:512], ct1[:, 512:1024],
                                    OP.add).then_inc(cx.h("vec"), 1)
            cx.inc("vec", 4 * NH)

            @blk.tensor
            def _(t):
                for h in range(NH):
                    t.wait_ge(cx.h("vec"), b_vec + 4 * h + 1)
                    t.matmul(pb[4][0:16, 0:512],
                             onehot[:, h * NH:(h + 1) * NH],
                             zb[:, (h % 2) * 1024:(h % 2) * 1024 + 512],
                             start=True, stop=True).then_inc(cx.h("mm2"), 1)
                    t.matmul(pb[5][0:16, 0:512],
                             onehot[:, h * NH:(h + 1) * NH],
                             zb[:, (h % 2) * 1024 + 512:(h % 2) * 1024 + 1024],
                             start=True, stop=True).then_inc(cx.h("mm2"), 1)
                    t.wait_ge(cx.h("vec"), b_vec + 4 * h + 3)
                    t.matmul(pb[6][:, 0:512], ones_f[0:1, 0:128],
                             pp[0:1, 0:512], start=True,
                             stop=True).then_inc(cx.h("mm2"), 1)
                    t.matmul(pb[7][:, 0:512], ones_f[0:1, 0:128],
                             pp[0:1, 512:1024], start=True,
                             stop=True).then_inc(cx.h("mm2"), 1)
            cx.inc("mm2", 4 * NH)

            @blk.scalar
            def _(s):
                s = EG(s)
                for h in range(NH):
                    s.wait_ge(cx.h("vec"), b_vec + 4 * h + 2)
                    s.activation(pp[0:1, 0:512], sc0[0:1, 1024:1536], AF.Exp,
                                 scale=ISQ128).then_inc(cx.h("act"), 1)
                    s.activation(pp[0:1, 512:1024], sc0[0:1, 0:512], AF.Exp,
                                 scale=ISQ128).then_inc(cx.h("act"), 1)
            cx.inc("act", 2 * NH)

        if DBG == "attn1":
            dbg_dump(vbuf1, NH * T)
            ph.close()
            _audit(nc, cx)
            return nc, es

        emit_proj(wd5, DC, vbuf1, evac_resid, 1)
        if DBG == "x3":
            dbg_dump(X)
            ph.close()
            _audit(nc, cx)
            return nc, es

        ph.close()
        ph = ExitStack()
        ln4_bf = ph.enter_context(nc.sbuf_tensor("ln4_bf", [128, DC * T], BF16))
        comb = ph.enter_context(nc.sbuf_tensor("compb", [128, 8 * 8], F32))
        combT = ph.enter_context(nc.sbuf_tensor("combT", [8, T], BF16))
        ph4 = ExitStack()
        ln4r = ph4.enter_context(nc.sbuf_tensor("ln4r", [128, 2 * 512], F32))
        rsd4 = ph4.enter_context(nc.sbuf_tensor("rsd4", [1, T], F32))
        gsc = ph4.enter_context(nc.sbuf_tensor("gsc", [8, T], F32))
        gw_sb = ph4.enter_context(nc.sbuf_tensor("gw_sb", [128, DC * 8], F32))
        gb_sb = ph4.enter_context(nc.sbuf_tensor("gb_sb", [1, 24], F32))

        b_wld = cx.v("wld")
        with nc.Block() as blk:
            @blk.sync
            def _(e):
                e.dma_start(out=gw_sb[:, :].rearrange("p (c e) -> p c e", e=8),
                            in_=gw_l.ap().rearrange("c p e -> p c e")
                            ).then_inc(cx.h("wld"), 16)
                e.dma_start(out=gb_sb[0:1, 0:8], in_=gate_b_in[:, :]
                            ).then_inc(cx.h("wld"), 16)
                e.dma_start(out=gb_sb[0:1, 8:16], in_=gate_bias_in[:, :]
                            ).then_inc(cx.h("wld"), 16)
                cx.inc("wld", 48)

            @blk.vector
            def _(v):
                v = EG(v)
                v.wait_ge(cx.h("wld"), b_wld + 48)
                v.tensor_tensor(gb_sb[0:1, 16:24], gb_sb[0:1, 0:8],
                                gb_sb[0:1, 8:16], OP.add).then_inc(cx.h("vec"), 1)
            cx.inc("vec", 1)

        for tt in range(2):
            b_mm = cx.v("mm")
            b_act = cx.v("act")
            b_vec = cx.v("vec")
            b_mm2 = cx.v("mm2")
            with nc.Block() as blk:
                @blk.vector
                def _(v, tt=tt):
                    v = EG(v)
                    for dc in range(DC):
                        if dc >= 2:
                            v.wait_ge(cx.h("mm"), b_mm + (dc - 1))
                        v.tensor_copy(
                            xb_ring[:, (dc % 2) * 512:(dc % 2) * 512 + 512],
                            X[:, dc * T + tt * 512: dc * T + tt * 512 + 512]
                        ).then_inc(cx.h("vec"), 1)
                    v.wait_ge(cx.h("mm"), b_mm + DC)
                    v.tensor_scalar(row_sc[0:1, 0:512], pb[6][0:1, 0:512],
                                    1.0 / 2048.0, None, OP.mult)
                    v.tensor_scalar(row_sc[0:1, 512:1024], pb[7][0:1, 0:512],
                                    1.0 / 2048.0, None, OP.mult)
                    v.tensor_tensor(row_sc[0:1, 1024:1536],
                                    row_sc[0:1, 0:512], row_sc[0:1, 0:512],
                                    OP.mult)
                    v.tensor_tensor(row_sc[0:1, 1024:1536],
                                    row_sc[0:1, 512:1024],
                                    row_sc[0:1, 1024:1536],
                                    OP.subtract).then_inc(cx.h("vec"), 1)
                    v.wait_ge(cx.h("act"), b_act + DC + 1)
                    v.reciprocal(row_sc[0:1, 512:1024],
                                 row_sc[0:1, 512:1024]).then_inc(cx.h("vec"), 1)
                    v.tensor_copy(rsd4[0:1, tt * 512:tt * 512 + 512],
                                  row_sc[0:1, 512:1024])
                    v.wait_ge(cx.h("mm"), b_mm + DC + 1)
                    v.tensor_copy(bc_sc[:, 0:512], pb[4][:, 0:512])
                    for dc in range(DC):
                        col = dc * T + tt * 512
                        if dc >= 2:
                            v.wait_ge(cx.h("mm2"), b_mm2 + (dc - 1))
                        v.tensor_tensor(stat[:, 0:512], X[:, col:col + 512],
                                        pb[5][:, 0:512], OP.subtract)
                        v.tensor_tensor(ln4r[:, (dc % 2) * 512:
                                             (dc % 2) * 512 + 512],
                                        stat[:, 0:512], bc_sc[:, 0:512],
                                        OP.mult)
                        v.tensor_copy(ln4_bf[:, col:col + 512],
                                      ln4r[:, (dc % 2) * 512:
                                           (dc % 2) * 512 + 512]
                                      ).then_inc(cx.h("vec"), 1)
                cx.inc("vec", 2 * DC + 2)

                @blk.scalar
                def _(s, tt=tt):
                    s = EG(s)
                    for dc in range(DC):
                        s.wait_ge(cx.h("vec"), b_vec + dc + 1)
                        if dc >= 2:
                            s.wait_ge(cx.h("mm"), b_mm + (dc - 1))
                        s.activation(
                            sq_ring[:, (dc % 2) * 512:(dc % 2) * 512 + 512],
                            xb_ring[:, (dc % 2) * 512:(dc % 2) * 512 + 512],
                            AF.Square).then_inc(cx.h("act"), 1)
                    s.wait_ge(cx.h("vec"), b_vec + DC + 1)
                    s.activation(row_sc[0:1, 512:1024],
                                 row_sc[0:1, 1024:1536], AF.Sqrt,
                                 bias=gbias[0:1, 17:18]).then_inc(cx.h("act"), 1)
                cx.inc("act", DC + 1)

                @blk.tensor
                def _(t, tt=tt):
                    for dc in range(DC):
                        t.wait_ge(cx.h("act"), b_act + dc + 1)
                        t.matmul(pb[6][0:1, 0:512], ones_b[:, 0:1],
                                 xb_ring[:, (dc % 2) * 512:(dc % 2) * 512 + 512],
                                 start=(dc == 0), stop=(dc == DC - 1))
                        t.matmul(pb[7][0:1, 0:512], ones_b[:, 0:1],
                                 sq_ring[:, (dc % 2) * 512:(dc % 2) * 512 + 512],
                                 start=(dc == 0), stop=(dc == DC - 1)
                                 ).then_inc(cx.h("mm"), 1)
                    t.wait_ge(cx.h("vec"), b_vec + DC + 2)
                    t.matmul(pb[5][:, 0:512], ones_f[0:1, 0:128],
                             row_sc[0:1, 0:512], start=True, stop=True)
                    t.matmul(pb[4][:, 0:512], ones_f[0:1, 0:128],
                             row_sc[0:1, 512:1024], start=True,
                             stop=True).then_inc(cx.h("mm"), 1)
                    for dc in range(DC):
                        t.wait_ge(cx.h("vec"), b_vec + DC + 3 + dc)
                        t.matmul(pb[2 + tt][0:8, 0:512],
                                 gw_sb[:, dc * 8:(dc + 1) * 8],
                                 ln4r[:, (dc % 2) * 512:(dc % 2) * 512 + 512],
                                 start=(dc == 0), stop=False
                                 ).then_inc(cx.h("mm2"), 1)
                    t.matmul(pb[2 + tt][0:8, 0:512], gb_sb[0:1, 16:24],
                             ones_f[0:1, 0:512], start=False,
                             stop=True).then_inc(cx.h("mm2"), 1)
                cx.inc("mm", DC + 1)
                cx.inc("mm2", DC + 1)

        b_mm2 = cx.v("mm2")
        b_vec = cx.v("vec")
        b_act = cx.v("act")
        with nc.Block() as blk:
            @blk.vector
            def _(v):
                v = EG(v)
                for tt in range(2):
                    v.tensor_copy(gsc[0:8, tt * 512:tt * 512 + 512],
                                  pb[2 + tt][0:8, 0:512])
                v.last.then_inc(cx.h("vec"), 1)
                v.wait_ge(cx.h("mm2"), b_mm2 + 2)
                for tt in range(2):
                    v.tensor_tensor(gsc[0:8, tt * 512:tt * 512 + 512],
                                    gsc[0:8, tt * 512:tt * 512 + 512],
                                    pb[2 + tt][0:8, 0:512], OP.mult)
                v.last.then_inc(cx.h("vec"), 1)
                for tc in range(8):
                    v.wait_ge(cx.h("mm2"), b_mm2 + 2 + tc + 1)
                    g0 = stat[:, 0:8]
                    v.tensor_copy(g0, pb[tc % 2][0:128, 0:8])
                    v.tensor_reduce(stat[:, 8:9], g0, axis=AX, op=OP.max)
                    v.tensor_scalar(stat[:, 16:24], g0, stat[:, 8:9], None,
                                    OP.is_ge)
                    v.scalar_tensor_tensor(stat[:, 24:32], stat[:, 16:24],
                                           NEG, g0, OP.mult, OP.add)
                    v.tensor_reduce(stat[:, 9:10], stat[:, 24:32], axis=AX,
                                    op=OP.max)
                    v.tensor_scalar(stat[:, 32:40], stat[:, 24:32],
                                    stat[:, 9:10], None, OP.is_ge)
                    v.tensor_tensor(stat[:, 10:11], stat[:, 9:10],
                                    stat[:, 8:9],
                                    OP.subtract).then_inc(cx.h("vec"), 1)
                    v.wait_ge(cx.h("act"), b_act + tc + 1)
                    v.tensor_scalar(stat[:, 12:13], stat[:, 11:12], -1.0, 1.0,
                                    OP.mult, OP.add)
                    v.tensor_scalar(stat[:, 32:40], stat[:, 32:40],
                                    stat[:, 11:12], None, OP.mult)
                    v.scalar_tensor_tensor(comb[:, tc * 8:(tc + 1) * 8],
                                           stat[:, 16:24], stat[:, 12:13],
                                           stat[:, 32:40], OP.mult,
                                           OP.add).then_inc(cx.h("vec"), 1)
                for tc in range(8):
                    v.wait_ge(cx.h("mm2"), b_mm2 + 10 + tc + 1)
                    v.tensor_copy(combT[0:8, tc * 128:(tc + 1) * 128],
                                  pb[4 + tc % 2][0:8, 0:128]
                                  ).then_inc(cx.h("vec"), 1)
            cx.inc("vec", 2 + 16 + 8)

            @blk.tensor
            def _(t):
                t.wait_ge(cx.h("vec"), b_vec + 1)
                for tt in range(2):
                    t.matmul(pb[2 + tt][0:8, 0:512], ones_f[0:1, 0:8],
                             rsd4[0:1, tt * 512:tt * 512 + 512],
                             start=True, stop=True).then_inc(cx.h("mm2"), 1)
                for tc in range(8):
                    t.wait_ge(cx.h("vec"), b_vec + 2)
                    if tc >= 2:
                        t.wait_ge(cx.h("vec"), b_vec + 2 + 2 * (tc - 2) + 1)
                    tti = tc * 128 // 512
                    t.transpose(pb[tc % 2][0:128, 0:8],
                                gsc[0:8, tc * 128:(tc + 1) * 128],
                                ident_f[0:8, 0:8]).then_inc(cx.h("mm2"), 1)
                for tc in range(8):
                    t.wait_ge(cx.h("vec"), b_vec + 2 + 2 * tc + 2)
                    if tc >= 2:
                        t.wait_ge(cx.h("vec"), b_vec + 18 + (tc - 2) + 1)
                    t.transpose(pb[4 + tc % 2][0:8, 0:128],
                                comb[:, tc * 8:(tc + 1) * 8],
                                ident_f[:, :]).then_inc(cx.h("mm2"), 1)
            cx.inc("mm2", 18)

            @blk.scalar
            def _(s):
                s = EG(s)
                for tc in range(8):
                    s.wait_ge(cx.h("vec"), b_vec + 2 + 2 * tc + 1)
                    s.activation(stat[:, 11:12], stat[:, 10:11],
                                 AF.Sigmoid).then_inc(cx.h("act"), 1)
            cx.inc("act", 8)

        if DBG == "gates":
            dbg_dump(comb, 64)
            ph.close()
            _audit(nc, cx)
            return nc, es

        ph4.close()
        h_sb = ph.enter_context(nc.sbuf_tensor("h_sb", [128, 11 * T], BF16))
        w2_sb = ph.enter_context(nc.sbuf_tensor("w2_sb", [128, 11 * D], BF16))
        b1_sb = ph.enter_context(nc.sbuf_tensor("b1_sb", [128, 88], F32))
        b2_sb = ph.enter_context(nc.sbuf_tensor("b2_sb", [8, D], BF16))

        b_mm2 = cx.v("mm2")
        b_vec = cx.v("vec")
        b_wld = cx.v("wld")
        b_gld = cx.v("gld")
        with nc.Block() as blk:
            @blk.sync
            def _(e):
                e.dma_start(out=b1_sb[:, :], in_=b1r[:, :]
                            ).then_inc(cx.h("wld"), 16)
                cx.inc("wld", 16)
            wld_b1 = cx.v("wld")

            @blk.gpsimd
            def _(g):
                g.dma_start(out=b2_sb[:, :], in_=b2_in[:, :]
                            ).then_inc(cx.h("gld"), 16)
                cx.inc("gld", 16)

            @blk.tensor
            def _(t):
                for i in range(128):
                    dc, tc = i // 8, i % 8
                    if i >= 4:
                        t.wait_ge(cx.h("vec"), b_vec + (i - 4) + 1)
                    t.transpose(pb[i % 4][:, 0:128],
                                X[:, dc * T + tc * 128:dc * T + tc * 128 + 128],
                                ident_f[:, :]).then_inc(cx.h("mm2"), 1)
                t.wait_ge(cx.h("gld"), b_gld + 16)
                for u in range(32):
                    tc, dj = u // 4, u % 4
                    if u >= 2:
                        t.wait_ge(cx.h("vec"), b_vec + 128 + (u - 1))
                    t.matmul(pb[4 + u % 2][:, 0:512],
                             combT[0:8, tc * 128:(tc + 1) * 128],
                             b2_sb[:, dj * 512:dj * 512 + 512],
                             start=True, stop=True).then_inc(cx.h("mm2"), 1)
            cx.inc("mm2", 128 + 32)

            @blk.vector
            def _(v):
                v = EG(v)
                for i in range(128):
                    dc, tc = i // 8, i % 8
                    v.wait_ge(cx.h("mm2"), b_mm2 + i + 1)
                    v.tensor_copy(X[:, dc * T + tc * 128:dc * T + tc * 128 + 128],
                                  pb[i % 4][:, 0:128]).then_inc(cx.h("vec"), 1)
                for u in range(32):
                    tc, dj = u // 4, u % 4
                    v.wait_ge(cx.h("mm2"), b_mm2 + 128 + u + 1)
                    for q4 in range(4):
                        dc = dj * 4 + q4
                        v.scalar_tensor_tensor(
                            X[:, dc * T + tc * 128:dc * T + tc * 128 + 128],
                            pb[4 + u % 2][:, q4 * 128:q4 * 128 + 128],
                            1.0,
                            X[:, dc * T + tc * 128:dc * T + tc * 128 + 128],
                            OP.mult, OP.add)
                    v.last.then_inc(cx.h("vec"), 1)
            cx.inc("vec", 160)

        for e_ in range(8):
            b_w = {0: cx.v("gwl"), 1: cx.v("gwl2")}
            b_kvw = {0: cx.v("gw3"), 1: cx.v("gw4")}
            b_mm = cx.v("mm")
            b_act = cx.v("act")
            b_mm2 = cx.v("mm2")
            b_vec = cx.v("vec")
            with nc.Block() as blk:
                @blk.gpsimd
                def _(g, e_=e_):
                    g.wait_ge(cx.h("mm"), b_mm)
                    g.wait_ge(cx.h("mm2"), b_mm2)
                    for hj in range(11):
                        nm = "gwl" if hj % 2 == 0 else "gwl2"
                        if hj >= 2:
                            g.wait_ge(cx.h("mm"), b_mm + 2 * (hj - 1))
                        g.dma_start(out=WR[:, (hj % 2) * D:(hj % 2) * D + D],
                                    in_=w15[e_, hj, :, :, :]
                                    ).then_inc(cx.h(nm), 16)
                    for hj in range(11):
                        nm = "gw3" if hj % 2 == 0 else "gw4"
                        g.dma_start(out=w2_sb[:, hj * D:(hj + 1) * D],
                                    in_=w25r[e_, hj, :, :]).then_inc(cx.h(nm), 16)
                cx.inc("gwl", 16 * 6)
                cx.inc("gwl2", 16 * 5)
                cx.inc("gw3", 16 * 6)
                cx.inc("gw4", 16 * 5)

                @blk.tensor
                def _(t, e_=e_):
                    for hj in range(11):
                        nm = "gwl" if hj % 2 == 0 else "gwl2"
                        t.wait_ge(cx.h(nm), b_w[hj % 2] + 16 * (hj // 2 + 1))
                        for tt in range(2):
                            u = hj * 2 + tt
                            if u >= 2:
                                t.wait_ge(cx.h("act"), b_act + (u - 2) + 1)
                            for c in range(DC):
                                mi = t.matmul(pb[u % 2][:, 0:512],
                                              WR[:, (hj % 2) * D + c * 128:
                                                 (hj % 2) * D + c * 128 + 128],
                                              ln4_bf[:, c * T + tt * 512:
                                                     c * T + tt * 512 + 512],
                                              start=(c == 0),
                                              stop=(c == DC - 1))
                            mi.then_inc(cx.h("mm"), 1)
                    for hj in range(11):
                        nm = "gw3" if hj % 2 == 0 else "gw4"
                        t.wait_ge(cx.h(nm), b_kvw[hj % 2] + 16 * (hj // 2 + 1))
                    t.wait_ge(cx.h("act"), b_act + 22)
                    for u2 in range(32):
                        tc, dj = u2 // 4, u2 % 4
                        if u2 >= 2:
                            t.wait_ge(cx.h("vec"), b_vec + (u2 - 1))
                        for hj in range(11):
                            mi = t.matmul(
                                pb[2 + u2 % 2][:, 0:512],
                                h_sb[:, hj * T + tc * 128:
                                     hj * T + tc * 128 + 128],
                                w2_sb[:, hj * D + dj * 512:
                                      hj * D + dj * 512 + 512],
                                start=(hj == 0), stop=(hj == 10))
                        mi.then_inc(cx.h("mm2"), 1)
                cx.inc("mm", 22)
                cx.inc("mm2", 32)

                @blk.scalar
                def _(s, e_=e_):
                    s = EG(s)
                    s.wait_ge(cx.h("wld"), wld_b1)
                    for hj in range(11):
                        for tt in range(2):
                            u = hj * 2 + tt
                            s.wait_ge(cx.h("mm"), b_mm + u + 1)
                            s.activation(h_sb[:, hj * T + tt * 512:
                                              hj * T + tt * 512 + 512],
                                         pb[u % 2][:, 0:512], AF.Silu,
                                         bias=b1_sb[:, e_ * 11 + hj:
                                                    e_ * 11 + hj + 1]
                                         ).then_inc(cx.h("act"), 1)
                cx.inc("act", 22)

                @blk.vector
                def _(v, e_=e_):
                    v = EG(v)
                    for u2 in range(32):
                        tc, dj = u2 // 4, u2 % 4
                        v.wait_ge(cx.h("mm2"), b_mm2 + u2 + 1)
                        for q4 in range(4):
                            dc = dj * 4 + q4
                            v.scalar_tensor_tensor(
                                X[:, dc * T + tc * 128:dc * T + tc * 128 + 128],
                                pb[2 + u2 % 2][:, q4 * 128:q4 * 128 + 128],
                                comb[:, tc * 8 + e_:tc * 8 + e_ + 1],
                                X[:, dc * T + tc * 128:dc * T + tc * 128 + 128],
                                OP.mult, OP.add)
                        v.last.then_inc(cx.h("vec"), 1)
                cx.inc("vec", 32)

        with nc.Block() as blk:
            @blk.sync
            def _(e):
                for nm in ["mm", "mm2", "act", "vec"]:
                    e.wait_ge(cx.h(nm), cx.v(nm))
                for tc in range(8):
                    e.dma_start(
                        out=out[tc * 128:(tc + 1) * 128, :],
                        in_=X[:, :].rearrange("p (c x t) -> p c x t",
                                              c=DC, x=8)[:, :, tc, :],
                    ).then_inc(cx.h("sto"), 16)
                e.wait_ge(cx.h("sto"), cx.v("sto") + 128)
                cx.inc("sto", 128)

        ph.close()
        _audit(nc, cx)

    return nc, es




def _audit(nc, cx):
    totals = {}
    max_wait = {}
    for inst in nc.all_instructions():
        si = getattr(inst, "sync_info", None)
        if si is None:
            continue
        for u in si.on_update:
            if u.sync_type == "semaphore" and u.update_mode in ("sem-inc", "sem-add-imm"):
                totals[u.ant_name] = totals.get(u.ant_name, 0) + u.update_value
        for w in si.on_wait:
            if w.sync_type == "semaphore" and w.wait_mode == "sem-ge-imm":
                max_wait[w.ant_name] = max(max_wait.get(w.ant_name, 0), w.wait_value)
    errs = []
    for name, (h, v) in cx.sem.items():
        t = totals.get(h.name, 0)
        if t != v:
            errs.append(f"sem {name}: python counter {v} != emitted {t}")
        mw = max_wait.get(h.name, 0)
        if mw > t:
            errs.append(f"sem {name}: wait {mw} exceeds total {t} (hang)")
    if errs:
        raise AssertionError("SEM AUDIT FAILED:\n" + "\n".join(errs))


def kernel(**inputs):
    shared = _stage_shared(inputs)
    cores = [_stage_core(inputs, c) for c in range(NCORES)]
    nc, _es = build_nc()
    in_maps = []
    for c in range(NCORES):
        m = {k: shared[k] for k in
             ["wq5", "wk5", "wv5r", "wo5", "wf5", "ffn_bias_t", "ffn_ws",
              "wqkv5", "wd5", "gw_l", "gate_b", "gate_bias", "w15", "b1r",
              "w25r", "b2", "ident_f", "ident_b", "ones_f", "ones_b",
              "onehot_b", "cos1", "sin1"]}
        m.update(cores[c])
        in_maps.append(m)
    res = run_bass_kernel_spmd(nc, in_maps, core_ids=list(range(NCORES)))
    full = np.empty((2, 4096, D), dtype=np.float32)
    for c in range(NCORES):
        o = np.asarray(res.results[c]["out"])
        full[0, JB * c:JB * (c + 1), :] = o[0:JB]
        full[1, JB * c:JB * (c + 1), :] = o[JB:T]
    return full


# version 06
# speedup vs baseline: 1.0125x; 1.0125x over previous
import numpy as np
import ml_dtypes
from contextlib import ExitStack

import concourse.bass as bass
import concourse.mybir as mybir
from concourse.bass_utils import run_bass_kernel_spmd

F32 = mybir.dt.float32
BF16 = mybir.dt.bfloat16
I32 = mybir.dt.int32
AX = mybir.AxisListType.X
OP = mybir.AluOpType
AF = mybir.ActivationFunctionType

NCORES = 8
D = 2048
DC = 16
NH = 16
HD = 128
JB = 512
T = 1024
LN2C = 0.6931471805599453
INV_LN2 = 1.4426950408889634
ISQ128 = float(1.0 / np.sqrt(128.0))
NEG = -1.0e30

DBG = None



def _lhsT5(w):
    K, F = w.shape
    return np.ascontiguousarray(
        w.reshape(K // 128, 128, F // 128, 128).transpose(2, 1, 0, 3))


def _stage_shared(inp):
    s = {}
    s["wq5"] = _lhsT5(inp["L0_attn_wq"])
    s["wk5"] = _lhsT5(inp["L0_attn_wk"])
    s["wv5r"] = np.ascontiguousarray(inp["L0_attn_wv"].reshape(DC, 128, D))
    s["wo5"] = _lhsT5(inp["L0_attn_wo"])
    s["wf5"] = _lhsT5(np.ascontiguousarray(np.asarray(inp["L0_ffn_weight"]).T))
    s["ffn_bias_t"] = np.ascontiguousarray(
        np.asarray(inp["L0_ffn_bias"]).reshape(DC, 128).T).astype(np.float32)
    s["ffn_ws"] = np.asarray(inp["L0_ffn_weight_scale"]).reshape(1, 1).astype(np.float32)

    wqkv = np.asarray(inp["L1_attn_w_qkv"])
    perm = np.concatenate([
        np.concatenate([np.arange(384 * h + 128 * k, 384 * h + 128 * k + 128)
                        for h in range(NH)])
        for k in range(3)])
    s["wqkv5"] = _lhsT5(np.ascontiguousarray(wqkv[:, perm]))
    s["wd5"] = _lhsT5(np.asarray(inp["L1_attn_w_dense"]))

    s["gw_l"] = np.ascontiguousarray(
        np.asarray(inp["L1_ffn_gate_w"]).T.reshape(DC, 128, 8)).astype(np.float32)
    s["gate_b"] = np.asarray(inp["L1_ffn_gate_b"]).reshape(1, 8).astype(np.float32)
    s["gate_bias"] = np.asarray(inp["L1_ffn_bias"]).reshape(1, 8).astype(np.float32)
    w1 = np.asarray(inp["L1_ffn_w1"])
    s["w15"] = np.ascontiguousarray(
        w1.transpose(0, 2, 1).reshape(8, DC, 128, 11, 128).transpose(0, 3, 2, 1, 4))
    s["b1r"] = np.ascontiguousarray(np.asarray(inp["L1_ffn_b1"]).reshape(88, 128).T).astype(np.float32)
    w2 = np.asarray(inp["L1_ffn_w2"])
    s["w25r"] = np.ascontiguousarray(w2.transpose(0, 2, 1).reshape(8, 11, 128, D))
    s["b2"] = np.ascontiguousarray(np.asarray(inp["L1_ffn_b2"]))

    s["ident_f"] = np.eye(128, dtype=np.float32)
    s["ident_b"] = np.eye(128, dtype=np.float32).astype(ml_dtypes.bfloat16)
    s["ones_f"] = np.ones((128, 512), dtype=np.float32)
    s["ones_b"] = np.ones((128, 128), dtype=np.float32).astype(ml_dtypes.bfloat16)
    oh = np.zeros((NH, 128, NH), dtype=np.float32)
    for h in range(NH):
        oh[h, :, h] = 1.0
    s["onehot_b"] = oh.astype(ml_dtypes.bfloat16)
    inv1 = (1.0 / (np.float32(10000.0) **
                   (np.arange(0, 128, 2, dtype=np.float32) / np.float32(128))))
    emb1 = np.concatenate([inv1, inv1]).astype(np.float32)
    s["cos1"] = np.cos(emb1).astype(np.float32).reshape(128, 1)
    s["sin1"] = np.sin(emb1).astype(np.float32).reshape(128, 1)
    return s


def _stage_core(inp, c):
    d = {}
    x = np.asarray(inp["L0_attn_hidden_states"])
    sl = x[:, JB * c: JB * (c + 1), :].reshape(T, D)
    d["x0T"] = np.ascontiguousarray(sl.T).astype(np.float32)

    pos = np.arange(JB * c, JB * (c + 1), dtype=np.float32)
    inv = (1.0 / (np.float32(10000.0) **
                  (np.arange(0, HD, 2, dtype=np.float32) / np.float32(HD))))
    freqs = pos[:, None] * inv[None, :]
    cosT = np.cos(freqs).astype(np.float32).T
    sinT = np.sin(freqs).astype(np.float32).T
    d["cosT"] = np.ascontiguousarray(np.concatenate([cosT, cosT], axis=1))
    d["sinT"] = np.ascontiguousarray(np.concatenate([sinT, sinT], axis=1))

    kt = np.arange(JB)[:, None]
    qt = np.arange(JB)[None, :]
    m0 = np.where(kt > qt, 0.0, NEG).astype(np.float32)
    m1 = np.zeros((JB, JB), dtype=np.float32)
    m2 = np.where(kt <= qt, 0.0, NEG).astype(np.float32)
    if c < 2:
        m0 = np.full((JB, JB), NEG, dtype=np.float32)
    if c < 1:
        m1 = np.full((JB, JB), NEG, dtype=np.float32)
    mm = np.stack([m0, m1, m2]).reshape(3, 4, 128, JB)
    d["negmask"] = np.ascontiguousarray(mm).astype(ml_dtypes.bfloat16)
    return d




_DVE_DATA_OPS = {
    "tensor_tensor", "tensor_scalar", "tensor_copy", "tensor_reduce",
    "reduce_max", "reduce_sum", "reciprocal", "select", "memset",
    "scalar_tensor_tensor", "tensor_add", "tensor_sub", "tensor_mul",
    "activation", "copy", "add", "mul", "sign", "sqrt", "square", "transpose",
}


class EG:

    def __init__(self, eng):
        self._e = eng
        self.last = None

    def __getattr__(self, name):
        f = getattr(self._e, name)
        if name in _DVE_DATA_OPS:
            def g(*a, **k):
                self._e.drain()
                self.last = f(*a, **k)
                return self.last
            return g
        return f


class Ctx:
    def __init__(self, nc, es):
        self.nc = nc
        self.sem = {}
        for name in ["wld", "kv", "mm", "mm2", "act", "vec", "fre", "fr2",
                     "sto", "cc", "gld", "gwl", "gwl2", "kva", "kvb",
                     "sta", "stb", "tra", "trb", "sva", "svb", "gw3", "gw4"]:
            self.sem[name] = [es.enter_context(nc.semaphore(name)), 0]

    def h(self, name):
        return self.sem[name][0]

    def v(self, name):
        return self.sem[name][1]

    def inc(self, name, n=1):
        self.sem[name][1] += n
        return self.sem[name][1]


def build_nc():
    nc = bass.Bass("TRN2", target_bir_lowering=False)
    es = ExitStack()
    RG = [list(range(NCORES))]

    x_in = nc.dram_tensor("x0T", [D, T], F32, kind="ExternalInput")
    out = nc.dram_tensor("out", [T, D], F32, kind="ExternalOutput")

    def din(name, shape, dt=F32):
        return nc.dram_tensor(name, list(shape), dt, kind="ExternalInput")

    wq5 = din("wq5", (NH, 128, DC, 128))
    wk5 = din("wk5", (NH, 128, DC, 128))
    wv5r = din("wv5r", (DC, 128, D))
    wo5 = din("wo5", (DC, 128, DC, 128))
    wf5 = din("wf5", (DC, 128, DC, 128))
    ffn_bias_t = din("ffn_bias_t", (128, DC))
    ffn_ws = din("ffn_ws", (1, 1))
    wqkv5 = din("wqkv5", (48, 128, DC, 128))
    wd5 = din("wd5", (DC, 128, DC, 128))
    gw_l = din("gw_l", (DC, 128, 8))
    gate_b_in = din("gate_b", (1, 8))
    gate_bias_in = din("gate_bias", (1, 8))
    w15 = din("w15", (8, 11, 128, DC, 128))
    b1r = din("b1r", (128, 88))
    w25r = din("w25r", (8, 11, 128, D))
    b2_in = din("b2", (8, D))
    ident_f_in = din("ident_f", (128, 128))
    ident_b_in = din("ident_b", (128, 128), BF16)
    ones_f_in = din("ones_f", (128, 512))
    ones_b_in = din("ones_b", (128, 128), BF16)
    onehot_in = din("onehot_b", (NH, 128, NH), BF16)
    cos1_in = din("cos1", (128, 1))
    sin1_in = din("sin1", (128, 1))
    cosT_in = din("cosT", (64, T))
    sinT_in = din("sinT", (64, T))
    negm_in = din("negmask", (3, 4, 128, JB), BF16)

    k_send = nc.dram_tensor("k_send", [NH, 128, T], BF16)
    v_send = nc.dram_tensor("v_send", [NH, 8, 128, 128], BF16)
    agk = nc.dram_tensor("agk", [NCORES, NH, 128, T], BF16)
    agv = nc.dram_tensor("agv", [NCORES, NH, 8, 128, 128], BF16)
    band_k = nc.dram_tensor("band_k", [2, NH, 128, T], BF16)
    band_v = nc.dram_tensor("band_v", [2, NH, 8, 128, 128], BF16)
    ar_in = nc.dram_tensor("ar_in", [1, 1], F32)
    ar_out = nc.dram_tensor("ar_out", [1, 1], F32)
    dbg = None
    if DBG is not None:
        dbg = nc.dram_tensor("dbg", [128, DC * T], F32, kind="ExternalOutput")

    with es:
        cx = Ctx(nc, es)

        def sb(name, shape, dt):
            return es.enter_context(nc.sbuf_tensor("s_" + name, list(shape), dt))

        X = sb("X", (128, DC * T), F32)
        ident_f = sb("ident_f", (128, 128), F32)
        ident_b = sb("ident_b", (128, 128), BF16)
        ones_f = sb("ones_f", (128, 512), F32)
        ones_b = sb("ones_b", (128, 128), BF16)
        onehot = sb("onehot", (128, NH * NH), BF16)
        cos1 = sb("cos1", (128, 1), F32)
        sin1 = sb("sin1", (128, 1), F32)
        stat = sb("stat", (128, 1024), F32)
        gbias = sb("gbias", (128, 24), F32)
        WR = sb("WR", (128, 2 * D), BF16)
        sq_ring = sb("sq_ring", (128, 2 * 512), BF16)
        xb_ring = sb("xb_ring", (128, 2 * 512), BF16)
        row_sc = sb("row_sc", (1, 3 * 512), F32)
        bc_sc = sb("bc_sc", (128, 512), F32)

        pb = [es.enter_context(nc.psum_tensor(f"pb{i}", [128, 512], F32))
              for i in range(8)]

        ph_q = ExitStack()
        qbuf = ph_q.enter_context(nc.sbuf_tensor("qbuf", [128, NH * T], BF16))
        ph = ExitStack()
        cosT = ph.enter_context(nc.sbuf_tensor("s_cosT", [64, T], F32))
        sinT = ph.enter_context(nc.sbuf_tensor("s_sinT", [64, T], F32))

        def dbg_dump(src, width=DC * T):
            with nc.Block() as blk:
                @blk.vector
                def _(v):
                    v = EG(v)
                    for nm in ["mm", "mm2", "act", "vec", "sto", "wld", "kv"]:
                        v.wait_ge(cx.h(nm), cx.v(nm))
                    step = 2048
                    for i in range(0, width, step):
                        w = min(step, width - i)
                        v.tensor_copy(stat[:, 0:512], stat[:, 0:512])
                        break
                @blk.gpsimd
                def _(g):
                    for nm in ["mm", "mm2", "act", "vec", "sto", "wld", "kv"]:
                        g.wait_ge(cx.h(nm), cx.v(nm))
                    g.dma_start(out=dbg[:, 0:width], in_=src[:, 0:width]
                                ).then_inc(cx.h("gld"), 16)
                    g.wait_ge(cx.h("gld"), cx.v("gld") + 16)
                    cx.inc("gld", 16)

        with nc.Block() as blk:
            @blk.sync
            def _(e):
                e.dma_start(
                    out=X[:, :].rearrange("p (c t) -> p c t", t=T),
                    in_=x_in.ap().rearrange("(c p) t -> p c t", p=128),
                ).then_inc(cx.h("wld"), 16)
                cx.inc("wld", 16)
                for dst, src in [(ident_f, ident_f_in), (ident_b, ident_b_in),
                                 (ones_f, ones_f_in), (ones_b, ones_b_in),
                                 (cos1, cos1_in), (sin1, sin1_in),
                                 (cosT, cosT_in), (sinT, sinT_in)]:
                    e.dma_start(out=dst[:, :], in_=src[:, :]).then_inc(cx.h("wld"), 16)
                    cx.inc("wld", 16)
                e.dma_start(
                    out=onehot[:, :].rearrange("p (h f) -> p h f", h=NH),
                    in_=onehot_in.ap().rearrange("h p f -> p h f"),
                ).then_inc(cx.h("wld"), 16)
                cx.inc("wld", 16)
                e.dma_start(out=gbias[:, 0:DC], in_=ffn_bias_t[:, :]).then_inc(cx.h("wld"), 16)
                cx.inc("wld", 16)
                e.dma_start(out=gbias[0:1, 16:17], in_=ffn_ws[:, :]).then_inc(cx.h("wld"), 16)
                cx.inc("wld", 16)
                e.wait_ge(cx.h("wld"), cx.v("wld"))

            @blk.vector
            def _(v):
                v = EG(v)
                v.memset(gbias[0:1, 17:18], 1e-5)

        def emit_ln(dst, fp32_cb=None):
            for tt in range(2):
                b_mm = cx.v("mm")
                b_act = cx.v("act")
                b_vec = cx.v("vec")
                with nc.Block() as blk:
                    @blk.vector
                    def _(v, tt=tt):
                        v = EG(v)
                        for dc in range(DC):
                            if dc >= 2:
                                v.wait_ge(cx.h("mm"), b_mm + (dc - 1))
                            v.tensor_copy(
                                xb_ring[:, (dc % 2) * 512:(dc % 2) * 512 + 512],
                                X[:, dc * T + tt * 512: dc * T + tt * 512 + 512]
                            ).then_inc(cx.h("vec"), 1)
                        v.wait_ge(cx.h("mm"), b_mm + DC)
                        v.tensor_scalar(row_sc[0:1, 0:512], pb[6][0:1, 0:512],
                                        1.0 / 2048.0, None, OP.mult)
                        v.tensor_scalar(row_sc[0:1, 512:1024], pb[7][0:1, 0:512],
                                        1.0 / 2048.0, None, OP.mult)
                        v.tensor_tensor(row_sc[0:1, 1024:1536],
                                        row_sc[0:1, 0:512], row_sc[0:1, 0:512],
                                        OP.mult)
                        v.tensor_tensor(row_sc[0:1, 1024:1536],
                                        row_sc[0:1, 512:1024],
                                        row_sc[0:1, 1024:1536],
                                        OP.subtract).then_inc(cx.h("vec"), 1)
                        v.wait_ge(cx.h("act"), b_act + DC + 1)
                        v.reciprocal(row_sc[0:1, 512:1024],
                                     row_sc[0:1, 512:1024]).then_inc(cx.h("vec"), 1)
                        v.wait_ge(cx.h("mm"), b_mm + DC + 1)
                        v.tensor_copy(bc_sc[:, 0:512], pb[4][:, 0:512])
                        for dc in range(DC):
                            col = dc * T + tt * 512
                            v.tensor_tensor(stat[:, 0:512], X[:, col:col + 512],
                                            pb[5][:, 0:512], OP.subtract)
                            if fp32_cb is not None:
                                fp32_cb(v, dc, tt)
                            v.tensor_tensor(dst[:, col:col + 512],
                                            stat[:, 0:512], bc_sc[:, 0:512],
                                            OP.mult).then_inc(cx.h("vec"), 1)
                    cx.inc("vec", 2 * DC + 2)

                    @blk.scalar
                    def _(s, tt=tt):
                        s = EG(s)
                        for dc in range(DC):
                            s.wait_ge(cx.h("vec"), b_vec + dc + 1)
                            if dc >= 2:
                                s.wait_ge(cx.h("mm"), b_mm + (dc - 1))
                            s.activation(
                                sq_ring[:, (dc % 2) * 512:(dc % 2) * 512 + 512],
                                xb_ring[:, (dc % 2) * 512:(dc % 2) * 512 + 512],
                                AF.Square).then_inc(cx.h("act"), 1)
                        s.wait_ge(cx.h("vec"), b_vec + DC + 1)
                        s.activation(row_sc[0:1, 512:1024],
                                     row_sc[0:1, 1024:1536], AF.Sqrt,
                                     bias=gbias[0:1, 17:18]).then_inc(cx.h("act"), 1)
                    cx.inc("act", DC + 1)

                    @blk.tensor
                    def _(t, tt=tt):
                        for dc in range(DC):
                            t.wait_ge(cx.h("act"), b_act + dc + 1)
                            t.matmul(pb[6][0:1, 0:512], ones_b[:, 0:1],
                                     xb_ring[:, (dc % 2) * 512:(dc % 2) * 512 + 512],
                                     start=(dc == 0), stop=(dc == DC - 1))
                            t.matmul(pb[7][0:1, 0:512], ones_b[:, 0:1],
                                     sq_ring[:, (dc % 2) * 512:(dc % 2) * 512 + 512],
                                     start=(dc == 0), stop=(dc == DC - 1)
                                     ).then_inc(cx.h("mm"), 1)
                        t.wait_ge(cx.h("vec"), b_vec + DC + 2)
                        t.matmul(pb[5][:, 0:512], ones_f[0:1, 0:128],
                                 row_sc[0:1, 0:512], start=True, stop=True)
                        t.matmul(pb[4][:, 0:512], ones_f[0:1, 0:128],
                                 row_sc[0:1, 512:1024], start=True,
                                 stop=True).then_inc(cx.h("mm"), 1)
                    cx.inc("mm", DC + 1)

        def emit_proj(w5, n_fj, rhs, evac_fn, n_evac_per_unit):
            b_w = {0: cx.v("gwl"), 1: cx.v("gwl2")}
            b_fr2 = cx.v("fr2")
            b_mm = cx.v("mm")
            with nc.Block() as blk:
                @blk.gpsimd
                def _(g):
                    for fj in range(n_fj):
                        nm = "gwl" if fj % 2 == 0 else "gwl2"
                        if fj >= 2:
                            g.wait_ge(cx.h("mm"), b_mm + 2 * (fj - 1))
                        g.dma_start(out=WR[:, (fj % 2) * D:(fj % 2) * D + D],
                                    in_=w5[fj, :, :, :]).then_inc(cx.h(nm), 16)
                cx.inc("gwl", 16 * ((n_fj + 1) // 2))
                cx.inc("gwl2", 16 * (n_fj // 2))

                @blk.tensor
                def _(t):
                    for fj in range(n_fj):
                        nm = "gwl" if fj % 2 == 0 else "gwl2"
                        t.wait_ge(cx.h(nm), b_w[fj % 2] + 16 * (fj // 2 + 1))
                        for tt in range(2):
                            u = fj * 2 + tt
                            if u >= 2:
                                t.wait_ge(cx.h("fr2"), b_fr2 + (u - 1))
                            for c in range(DC):
                                mi = t.matmul(pb[u % 2][:, 0:512],
                                              WR[:, (fj % 2) * D + c * 128:
                                                 (fj % 2) * D + c * 128 + 128],
                                              rhs[:, c * T + tt * 512:
                                                  c * T + tt * 512 + 512],
                                              start=(c == 0), stop=(c == DC - 1))
                            mi.then_inc(cx.h("mm"), 1)
                cx.inc("mm", n_fj * 2)
                evac_fn(blk, b_mm)

        ln1_bf = ph.enter_context(nc.sbuf_tensor("ln1_bf", [128, DC * T], BF16))

        emit_ln(ln1_bf)
        if DBG == "ln1":
            dbg_dump(ln1_bf)
            ph.close(); ph_q.close()
            _audit(nc, cx)
            return nc, es

        rp_t1 = ph.enter_context(nc.sbuf_tensor("rp_t1", [64, 512], F32))
        rp_t2 = ph.enter_context(nc.sbuf_tensor("rp_t2", [64, 512], F32))
        kr = ph.enter_context(nc.sbuf_tensor("kr", [128, 2 * 512], BF16))
        vft = ph.enter_context(nc.sbuf_tensor("vft", [128, 8 * 512], BF16))

        def rope_evac(v, ps, dst_lo, dst_hi, tt):
            cs = slice(tt * 512, tt * 512 + 512)
            v.tensor_tensor(rp_t1[:, :], ps[0:64, 0:512], cosT[:, cs], OP.mult)
            v.tensor_tensor(rp_t2[:, :], ps[64:128, 0:512], sinT[:, cs], OP.mult)
            v.tensor_tensor(dst_lo, rp_t1[:, :], rp_t2[:, :], OP.subtract)
            v.tensor_tensor(rp_t1[:, :], ps[64:128, 0:512], cosT[:, cs], OP.mult)
            pbr = v.tensor_tensor(rp_t2[:, :], ps[0:64, 0:512], sinT[:, cs],
                                  OP.mult)
            fin = v.tensor_tensor(dst_hi, rp_t1[:, :], rp_t2[:, :], OP.add)
            return pbr, fin

        def evac_q(blk, b_mm):
            @blk.vector
            def _(v):
                v = EG(v)
                for fj in range(NH):
                    for tt in range(2):
                        u = fj * 2 + tt
                        v.wait_ge(cx.h("mm"), b_mm + u + 1)
                        col = fj * T + tt * 512
                        pbr, fin = rope_evac(v, pb[u % 2],
                                             qbuf[0:64, col:col + 512],
                                             qbuf[64:128, col:col + 512], tt)
                        fin.then_inc(cx.h("fr2"), 1)
            cx.inc("fr2", NH * 2)

        emit_proj(wq5, NH, ln1_bf, evac_q, 1)

        def evac_k(blk, b_mm):
            b_vec = cx.v("vec")
            b_st = {0: cx.v("sta"), 1: cx.v("stb")}
            @blk.vector
            def _(v):
                v = EG(v)
                for fj in range(NH):
                    for tt in range(2):
                        u = fj * 2 + tt
                        v.wait_ge(cx.h("mm"), b_mm + u + 1)
                        if u >= 2:
                            nm = "sta" if u % 2 == 0 else "stb"
                            v.wait_ge(cx.h(nm), b_st[u % 2] + 16 * ((u - 2) // 2 + 1))
                        kc_ = slice((u % 2) * 512, (u % 2) * 512 + 512)
                        pbr, fin = rope_evac(v, pb[u % 2], kr[0:64, kc_],
                                             kr[64:128, kc_], tt)
                        pbr.then_inc(cx.h("fr2"), 1)
                        fin.then_inc(cx.h("vec"), 1)
            cx.inc("fr2", NH * 2)
            cx.inc("vec", NH * 2)

            @blk.sync
            def _(e):
                for fj in range(NH):
                    for tt in range(2):
                        u = fj * 2 + tt
                        nm = "sta" if u % 2 == 0 else "stb"
                        e.wait_ge(cx.h("vec"), b_vec + u + 1)
                        e.dma_start(
                            out=k_send[fj, :, tt * 512:tt * 512 + 512],
                            in_=kr[:, (u % 2) * 512:(u % 2) * 512 + 512],
                        ).then_inc(cx.h(nm), 16)
            cx.inc("sta", 16 * NH)
            cx.inc("stb", 16 * NH)

        emit_proj(wk5, NH, ln1_bf, evac_k, 2)

        for dht in range(4):
            b_w = {0: cx.v("gwl"), 1: cx.v("gwl2")}
            b_fre = cx.v("fre")
            b_vec = cx.v("vec")
            with nc.Block() as blk:
                @blk.gpsimd
                def _(g, dht=dht):
                    for dc in range(DC):
                        nm = "gwl" if dc % 2 == 0 else "gwl2"
                        if dc >= 2:
                            g.wait_ge(cx.h("fre"), b_fre + (dc - 1))
                        g.dma_start(
                            out=WR[:, (dc % 2) * D:(dc % 2) * D + 512],
                            in_=wv5r[dc, :, dht * 512:dht * 512 + 512],
                        ).then_inc(cx.h(nm), 16)
                cx.inc("gwl", 16 * (DC // 2))
                cx.inc("gwl2", 16 * (DC // 2))

                @blk.tensor
                def _(t, dht=dht):
                    for dc in range(DC):
                        nm = "gwl" if dc % 2 == 0 else "gwl2"
                        t.wait_ge(cx.h(nm), b_w[dc % 2] + 16 * (dc // 2 + 1))
                        for ktc in range(8):
                            mi = t.matmul(pb[ktc][:, 0:512],
                                          ln1_bf[:, dc * T + ktc * 128:
                                                 dc * T + ktc * 128 + 128],
                                          WR[:, (dc % 2) * D:(dc % 2) * D + 512],
                                          start=(dc == 0), stop=(dc == DC - 1))
                        mi.then_inc(cx.h("fre"), 1)
                cx.inc("fre", DC)

                b_sv = {0: cx.v("sva"), 1: cx.v("svb")}

                @blk.vector
                def _(v, dht=dht):
                    v = EG(v)
                    v.wait_ge(cx.h("sva"), b_sv[0])
                    v.wait_ge(cx.h("svb"), b_sv[1])
                    v.wait_ge(cx.h("fre"), b_fre + DC)
                    for ktc in range(8):
                        v.tensor_copy(vft[:, ktc * 512:ktc * 512 + 512],
                                      pb[ktc][:, 0:512]).then_inc(cx.h("vec"), 1)
                cx.inc("vec", 8)

                @blk.sync
                def _(e, dht=dht):
                    for ktc in range(8):
                        p = ktc % 2
                        snm = "sva" if p == 0 else "svb"
                        e.wait_ge(cx.h("vec"), b_vec + ktc + 1)
                        for sub in range(4):
                            hc = dht * 4 + sub
                            e.dma_start(out=v_send[hc, ktc, :, :],
                                        in_=vft[:, ktc * 512 + sub * 128:
                                                ktc * 512 + sub * 128 + 128],
                                        ).then_inc(cx.h(snm), 16)
                cx.inc("sva", 16 * 16)
                cx.inc("svb", 16 * 16)

        if DBG == "qkv":
            dbg_dump(qbuf, NH * T)
            ph.close(); ph_q.close()
            _audit(nc, cx)
            return nc, es

        with nc.Block() as blk:
            @blk.gpsimd
            def _(g):
                for nm in ["sta", "stb", "sva", "svb"]:
                    g.wait_ge(cx.h(nm), cx.v(nm))
                g.collective_compute("AllGather", OP.bypass, replica_groups=RG,
                                     ins=[k_send.ap()], outs=[agk.ap()]
                                     ).then_inc(cx.h("cc"), 1)
                g.collective_compute("AllGather", OP.bypass, replica_groups=RG,
                                     ins=[v_send.ap()], outs=[agv.ap()]
                                     ).then_inc(cx.h("cc"), 1)
                cx.inc("cc", 2)

        ph.close()
        ph = ExitStack()
        ctx_bf = ph.enter_context(nc.sbuf_tensor("ctx_bf", [128, NH * T], BF16))
        kband = ph.enter_context(nc.sbuf_tensor("kband", [128, 2 * 1536], BF16))
        vband = ph.enter_context(nc.sbuf_tensor("vband", [128, 2 * 1536], BF16))
        negm = ph.enter_context(nc.sbuf_tensor("negm", [128, 12 * 512], BF16))
        ptile = ph.enter_context(nc.sbuf_tensor("ptile", [128, 3 * 512], BF16))
        ksq = ph.enter_context(nc.sbuf_tensor("ksq", [128, 1536], F32))
        mneg = ph.enter_context(nc.sbuf_tensor("mneg", [128, 1], F32))
        rcp_row = ph.enter_context(nc.sbuf_tensor("rcp_row", [1, 512], F32))

        with nc.Block() as blk:
            @blk.sync
            def _(e):
                e.dma_start(
                    out=negm[:, :].rearrange("p (m q j) -> p m q j", m=3, q=4),
                    in_=negm_in.ap().rearrange("m q p j -> p m q j"),
                ).then_inc(cx.h("kv"), 16)
                cx.inc("kv", 16)
                e.wait_ge(cx.h("cc"), cx.v("cc"))
                r1 = e.alloc_register()
                r2 = e.alloc_register()
                e.cc_rank_ld(r1, replica_groups=RG)
                e.reg_add(r2, r1, 6)
                e.reg_mod(r2, r2, 8)
                e.reg_add(r1, r1, 7)
                e.reg_mod(r1, r1, 8)
                s2 = e.snap(r2)
                s1 = e.snap(r1)
                e.dma_start(out=band_k[0, :, :, :],
                            in_=agk[bass.ds(s2, 1), :, :, :]).then_inc(cx.h("kv"), 16)
                e.dma_start(out=band_v[0, :, :, :, :],
                            in_=agv[bass.ds(s2, 1), :, :, :, :]).then_inc(cx.h("kv"), 16)
                e.dma_start(out=band_k[1, :, :, :],
                            in_=agk[bass.ds(s1, 1), :, :, :]).then_inc(cx.h("kv"), 16)
                e.dma_start(out=band_v[1, :, :, :, :],
                            in_=agv[bass.ds(s1, 1), :, :, :, :]).then_inc(cx.h("kv"), 16)
                cx.inc("kv", 64)

        mm2_hist = []
        for bh in range(2 * NH):
            b, h = bh // NH, bh % NH
            buf = bh % 2
            knm = "kva" if buf == 0 else "kvb"
            b_kv = cx.v(knm)
            b_kv0 = cx.v("kv")
            b_mm = cx.v("mm")
            b_mm2 = cx.v("mm2")
            b_act = cx.v("act")
            b_vec = cx.v("vec")
            mm2_hist.append(b_mm2)
            qcol = slice(h * T + b * 512, h * T + b * 512 + 512)
            with nc.Block() as blk:
                @blk.sync
                def _(e, b=b, h=h, buf=buf, bh=bh, knm=knm):
                    if bh < 2:
                        e.wait_ge(cx.h("kv"), b_kv0)
                    else:
                        e.wait_ge(cx.h("mm2"), mm2_hist[bh - 1])
                    for m in range(3):
                        if m < 2:
                            ksrc = band_k[m, h, :, b * 512:b * 512 + 512]
                        else:
                            ksrc = k_send[h, :, b * 512:b * 512 + 512]
                        e.dma_start(
                            out=kband[:, buf * 1536 + m * 512:
                                      buf * 1536 + m * 512 + 512],
                            in_=ksrc).then_inc(cx.h(knm), 16)
                        for sub in range(4):
                            if m < 2:
                                vsrc = band_v[m, h, b * 4 + sub, :, :]
                            else:
                                vsrc = v_send[h, b * 4 + sub, :, :]
                            e.dma_start(
                                out=vband[:, buf * 1536 + m * 512 + sub * 128:
                                          buf * 1536 + m * 512 + sub * 128 + 128],
                                in_=vsrc).then_inc(cx.h(knm), 16)
                cx.inc(knm, 16 * 15)

                @blk.scalar
                def _(s, b=b, h=h, buf=buf, b_kv=b_kv, b_mm=b_mm,
                      b_mm2=b_mm2, b_act=b_act, b_vec=b_vec, qcol=qcol):
                    s = EG(s)
                    s.wait_ge(cx.h(knm), b_kv + 16 * 15)
                    s.activation(ksq[:, :], kband[:, buf * 1536:buf * 1536 + 1536],
                                 AF.Square).then_inc(cx.h("act"), 1)
                    s.activation(stat[:, 512:1024], qbuf[:, qcol],
                                 AF.Square).then_inc(cx.h("act"), 1)
                    s.wait_ge(cx.h("vec"), b_vec + 2)
                    s.activation(stat[0:1, 2:3], stat[0:1, 1:2], AF.Sqrt,
                                 scale=1.0 / 128.0).then_inc(cx.h("act"), 1)
                    s.wait_ge(cx.h("vec"), b_vec + 4)
                    for j in range(12):
                        s.wait_ge(cx.h("mm"), b_mm + j + 1)
                        if j >= 3:
                            s.wait_ge(cx.h("mm2"), b_mm2 + 2 + 2 * (j - 2))
                        s.activation(ptile[:, (j % 3) * 512:(j % 3) * 512 + 512],
                                     pb[j % 2][:, 0:512], AF.Exp,
                                     bias=mneg[:, 0:1],
                                     scale=ISQ128).then_inc(cx.h("act"), 1)
                cx.inc("act", 15)

                @blk.vector
                def _(v, b=b, h=h, b_kv=b_kv, b_mm=b_mm, b_mm2=b_mm2,
                      b_act=b_act, b_vec=b_vec, qcol=qcol):
                    v = EG(v)
                    v.wait_ge(cx.h("act"), b_act + 1)
                    v.reduce_max(stat[:, 0:1], ksq[:, :], axis=AX)
                    v.last.then_inc(cx.h("vec"), 1)
                    v.wait_ge(cx.h("mm2"), b_mm2 + 1)
                    v.reduce_max(stat[0:1, 0:1], pb[6][0:1, 0:512], axis=AX)
                    v.tensor_tensor(stat[0:1, 1:2], stat[0:1, 0:1],
                                    pb[7][0:1, 0:1], OP.mult)
                    v.last.then_inc(cx.h("vec"), 1)
                    v.wait_ge(cx.h("act"), b_act + 3)
                    v.tensor_scalar(stat[0:1, 3:4], stat[0:1, 2:3], -1.0, None,
                                    OP.mult).then_inc(cx.h("vec"), 1)
                    v.wait_ge(cx.h("mm2"), b_mm2 + 2)
                    v.tensor_copy(mneg[:, 0:1], pb[7][:, 4:5]).then_inc(cx.h("vec"), 1)
                    v.wait_ge(cx.h("mm2"), b_mm2 + 2 + 24)
                    v.reciprocal(rcp_row[0:1, :], pb[3][0:1, 0:512])
                    v.last.then_inc(cx.h("vec"), 1)
                    v.wait_ge(cx.h("mm2"), b_mm2 + 27)
                    v.tensor_copy(bc_sc[:, 0:512], pb[5][:, 0:512])
                    v.tensor_tensor(ctx_bf[:, qcol], pb[2][:, 0:512],
                                    bc_sc[:, 0:512], OP.mult
                                    ).then_inc(cx.h("vec"), 1)
                cx.inc("vec", 6)

                @blk.tensor
                def _(t, b=b, h=h, buf=buf, b_kv=b_kv, b_mm=b_mm,
                      b_mm2=b_mm2, b_act=b_act, b_vec=b_vec, qcol=qcol,
                      knm=knm):
                    t.wait_ge(cx.h("vec"), b_vec + 1)
                    t.matmul(pb[7][0:1, 0:1], ones_f[:, 0:1], stat[:, 0:1],
                             start=True, stop=True)
                    t.wait_ge(cx.h("act"), b_act + 2)
                    t.matmul(pb[6][0:1, 0:512], ones_f[:, 0:1],
                             stat[:, 512:1024], start=True,
                             stop=True).then_inc(cx.h("mm2"), 1)
                    t.wait_ge(cx.h("vec"), b_vec + 3)
                    t.matmul(pb[7][:, 4:5], ones_f[0:1, 0:128], stat[0:1, 3:4],
                             start=True, stop=True).then_inc(cx.h("mm2"), 1)
                    t.wait_ge(cx.h(knm), b_kv + 16 * 15)
                    for j in range(13):
                        if j < 12:
                            if j >= 2:
                                t.wait_ge(cx.h("act"), b_act + 3 + (j - 2) + 1)
                            t.matmul(pb[j % 2][:, 0:512],
                                     kband[:, buf * 1536 + j * 128:
                                           buf * 1536 + j * 128 + 128],
                                     qbuf[:, qcol], start=True, stop=False)
                            t.matmul(pb[j % 2][:, 0:512], ident_b[:, :],
                                     negm[:, j * 512:j * 512 + 512],
                                     start=False, stop=True).then_inc(cx.h("mm"), 1)
                        if j >= 1:
                            jj = j - 1
                            t.wait_ge(cx.h("act"), b_act + 3 + jj + 1)
                            t.matmul(pb[2][:, 0:512],
                                     vband[:, buf * 1536 + jj * 128:
                                           buf * 1536 + jj * 128 + 128],
                                     ptile[:, (jj % 3) * 512:(jj % 3) * 512 + 512],
                                     start=(jj == 0), stop=(jj == 11))
                            t.matmul(pb[3][0:1, 0:512], ones_b[:, 0:1],
                                     ptile[:, (jj % 3) * 512:(jj % 3) * 512 + 512],
                                     start=(jj == 0), stop=(jj == 11)
                                     ).then_inc(cx.h("mm2"), 2)
                    t.wait_ge(cx.h("vec"), b_vec + 5)
                    t.matmul(pb[5][:, 0:512], ones_f[0:1, 0:128], rcp_row[0:1, :],
                             start=True, stop=True).then_inc(cx.h("mm2"), 1)
                cx.inc("mm", 12)
                cx.inc("mm2", 27)

        if DBG == "attn0":
            dbg_dump(ctx_bf, NH * T)
            ph.close(); ph_q.close()
            _audit(nc, cx)
            return nc, es

        def evac_resid(blk, b_mm):
            @blk.vector
            def _(v):
                v = EG(v)
                for fj in range(DC):
                    for tt in range(2):
                        u = fj * 2 + tt
                        v.wait_ge(cx.h("mm"), b_mm + u + 1)
                        col = fj * T + tt * 512
                        v.tensor_tensor(X[:, col:col + 512], pb[u % 2][:, 0:512],
                                        X[:, col:col + 512], OP.add
                                        ).then_inc(cx.h("fr2"), 1)
            cx.inc("fr2", DC * 2)

        emit_proj(wo5, DC, ctx_bf, evac_resid, 1)
        if DBG == "x1":
            dbg_dump(X)
            ph.close(); ph_q.close()
            _audit(nc, cx)
            return nc, es

        ph.close()
        ph_q.close()
        ph = ExitStack()
        ln2 = ph.enter_context(nc.sbuf_tensor("ln2", [128, DC * T], F32))
        xq_bf = ph.enter_context(nc.sbuf_tensor("xq_bf", [128, DC * T], BF16))

        emit_ln(ln2)
        if DBG == "ln2":
            dbg_dump(ln2)
            ph.close()
            _audit(nc, cx)
            return nc, es

        amax = ph.enter_context(nc.sbuf_tensor("amax", [128, 1], F32))
        b_vec = cx.v("vec")
        b_mm2 = cx.v("mm2")
        with nc.Block() as blk:
            @blk.vector
            def _(v):
                v = EG(v)
                for dc in range(DC):
                    v.tensor_reduce(stat[:, 0:1], ln2[:, dc * T:dc * T + T],
                                    axis=AX, op=OP.max,
                                    apply_absolute_value=True)
                    if dc == 0:
                        v.tensor_copy(amax[:, :], stat[:, 0:1])
                    else:
                        v.tensor_tensor(amax[:, :], amax[:, :], stat[:, 0:1],
                                        OP.max)
                v.last.then_inc(cx.h("vec"), 1)
                v.wait_ge(cx.h("mm2"), b_mm2 + 1)
                v.reduce_max(stat[0:1, 0:1], pb[6][0:1, 0:128], axis=AX)
                v.last.then_inc(cx.h("vec"), 1)
            cx.inc("vec", 2)

            @blk.tensor
            def _(t):
                t.wait_ge(cx.h("vec"), b_vec + 1)
                t.transpose(pb[6][0:1, 0:128], amax[:, 0:1],
                            ident_f[:, :]).then_inc(cx.h("mm2"), 1)
            cx.inc("mm2", 1)

            @blk.gpsimd
            def _(g):
                g.wait_ge(cx.h("vec"), b_vec + 2)
                g.dma_start(out=ar_in[:, :], in_=stat[0:1, 0:1]).then_inc(cx.h("gld"), 16)
                g.wait_ge(cx.h("gld"), cx.v("gld") + 16)
                g.collective_compute("AllReduce", OP.max, replica_groups=RG,
                                     ins=[ar_in.ap()], outs=[ar_out.ap()]
                                     ).then_inc(cx.h("cc"), 1)
                g.wait_ge(cx.h("cc"), cx.v("cc") + 1)
                g.dma_start(out=stat[0:1, 4:5], in_=ar_out[:, :]).then_inc(cx.h("gld"), 16)
                cx.inc("gld", 32)
                cx.inc("cc", 1)

        b_gld = cx.v("gld")
        b_vec = cx.v("vec")
        b_mm2 = cx.v("mm2")
        with nc.Block() as blk:
            @blk.vector
            def _(v):
                v = EG(v)
                v.wait_ge(cx.h("gld"), b_gld)
                v.tensor_scalar(stat[0:1, 5:6], stat[0:1, 4:5], 1.0 / 448.0,
                                1e-7, OP.mult, OP.max)
                v.reciprocal(stat[0:1, 6:7], stat[0:1, 5:6])
                v.last.then_inc(cx.h("vec"), 1)
                v.wait_ge(cx.h("mm2"), b_mm2 + 1)
                v.tensor_copy(gbias[:, 18:19], pb[6][:, 0:1])
                v.tensor_copy(gbias[:, 19:20], pb[6][:, 1:2])
                v.last.then_inc(cx.h("vec"), 1)
            cx.inc("vec", 2)

            @blk.tensor
            def _(t):
                t.wait_ge(cx.h("vec"), b_vec + 1)
                t.matmul(pb[6][:, 0:1], ones_f[0:1, 0:128], stat[0:1, 5:6],
                         start=True, stop=True)
                t.matmul(pb[6][:, 1:2], ones_f[0:1, 0:128], stat[0:1, 6:7],
                         start=True, stop=True).then_inc(cx.h("mm2"), 1)
            cx.inc("mm2", 1)

        qs1 = ph.enter_context(nc.sbuf_tensor("qs1", [128, T], F32))
        qs2 = ph.enter_context(nc.sbuf_tensor("qs2", [128, T], F32))
        qs3 = ph.enter_context(nc.sbuf_tensor("qs3", [128, T], F32))
        qsi = ph.enter_context(nc.sbuf_tensor("qsi", [128, T], I32))

        for dc in range(DC):
            b_act = cx.v("act")
            b_vec = cx.v("vec")
            with nc.Block() as blk:
                @blk.scalar
                def _(s, dc=dc, b_act=b_act, b_vec=b_vec):
                    s = EG(s)
                    s.activation(qs1[:, :], ln2[:, dc * T:dc * T + T], AF.Abs,
                                 scale=gbias[:, 19:20]).then_inc(cx.h("act"), 1)
                    s.wait_ge(cx.h("vec"), b_vec + 1)
                    s.activation(qs2[:, :], qs1[:, :], AF.Ln).then_inc(cx.h("act"), 1)
                    s.wait_ge(cx.h("vec"), b_vec + 2)
                    s.activation(qs2[:, :], qs3[:, :], AF.Exp,
                                 scale=LN2C).then_inc(cx.h("act"), 1)
                    s.activation(qs3[:, :], qs3[:, :], AF.Exp,
                                 scale=-LN2C).then_inc(cx.h("act"), 1)
                    s.wait_ge(cx.h("vec"), b_vec + 3)
                    s.activation(qs3[:, :], ln2[:, dc * T:dc * T + T],
                                 AF.Sign).then_inc(cx.h("act"), 1)
                cx.inc("act", 5)

                @blk.vector
                def _(v, dc=dc, b_act=b_act, b_vec=b_vec):
                    v = EG(v)
                    v.wait_ge(cx.h("act"), b_act + 1)
                    v.tensor_scalar(qs1[:, :], qs1[:, :], 1e-30, None, OP.max)
                    v.last.then_inc(cx.h("vec"), 1)
                    v.wait_ge(cx.h("act"), b_act + 2)
                    v.tensor_scalar(qs2[:, :], qs2[:, :], INV_LN2, 131.0,
                                    OP.mult, OP.add)
                    v.tensor_copy(qsi[:, :], qs2[:, :])
                    v.tensor_copy(qs3[:, :], qsi[:, :])
                    v.tensor_tensor(stat[:, 0:512], qs3[:, 0:512],
                                    qs2[:, 0:512], OP.is_gt)
                    v.tensor_tensor(stat[:, 512:1024], qs3[:, 512:1024],
                                    qs2[:, 512:1024], OP.is_gt)
                    v.tensor_tensor(qs3[:, 0:512], qs3[:, 0:512],
                                    stat[:, 0:512], OP.subtract)
                    v.tensor_tensor(qs3[:, 512:1024], qs3[:, 512:1024],
                                    stat[:, 512:1024], OP.subtract)
                    v.tensor_scalar(qs3[:, :], qs3[:, :], -134.0, -9.0,
                                    OP.add, OP.max)
                    v.last.then_inc(cx.h("vec"), 1)
                    v.wait_ge(cx.h("act"), b_act + 4)
                    v.tensor_tensor(stat[:, 0:1024], qs1[:, :],
                                    qs3[:, :], OP.mult)
                    v.last.then_inc(cx.h("vec"), 1)
                    v.tensor_copy(qsi[:, :], stat[:, 0:1024])
                    v.tensor_copy(stat[:, 0:1024], qsi[:, :])
                    v.tensor_tensor(stat[:, 0:1024], stat[:, 0:1024],
                                    qs2[:, :], OP.mult)
                    v.tensor_tensor(qs1[:, :], qs1[:, :], stat[:, 0:1024],
                                    OP.subtract)
                    v.tensor_scalar(qs2[:, :], qs2[:, :], 0.5, None, OP.mult)
                    v.tensor_tensor(qs1[:, :], qs1[:, :], qs2[:, :], OP.is_gt)
                    v.tensor_scalar(qs2[:, :], qs2[:, :], 2.0, None, OP.mult)
                    v.tensor_tensor(qs1[:, :], qs1[:, :], qs2[:, :], OP.mult)
                    v.tensor_tensor(stat[:, 0:1024], stat[:, 0:1024],
                                    qs1[:, :], OP.add)
                    v.wait_ge(cx.h("act"), b_act + 5)
                    v.tensor_tensor(xq_bf[:, dc * T:dc * T + T],
                                    stat[:, 0:1024], qs3[:, :],
                                    OP.mult).then_inc(cx.h("vec"), 1)
                cx.inc("vec", 4)

        if DBG == "xq":
            dbg_dump(xq_bf)
            ph.close()
            _audit(nc, cx)
            return nc, es

        b_vec = cx.v("vec")
        b_mm2 = cx.v("mm2")
        with nc.Block() as blk:
            @blk.tensor
            def _(t):
                t.matmul(pb[6][:, 3:4], ones_f[0:1, 0:128], gbias[0:1, 16:17],
                         start=True, stop=True).then_inc(cx.h("mm2"), 1)
            cx.inc("mm2", 1)

            @blk.vector
            def _(v):
                v = EG(v)
                v.wait_ge(cx.h("mm2"), b_mm2 + 1)
                v.tensor_copy(gbias[:, 20:21], pb[6][:, 3:4]).then_inc(cx.h("vec"), 1)
            cx.inc("vec", 1)

        def evac_ffn(blk, b_mm):
            b_act = cx.v("act")
            @blk.scalar
            def _(s):
                s = EG(s)
                for fj in range(DC):
                    for tt in range(2):
                        u = fj * 2 + tt
                        s.wait_ge(cx.h("mm"), b_mm + u + 1)
                        if u >= 1:
                            s.wait_ge(cx.h("fr2"), cx_fr2_base + u)
                        s.activation(bc_sc[:, 0:512], pb[u % 2][:, 0:512],
                                     AF.Identity, bias=gbias[:, fj:fj + 1],
                                     scale=gbias[:, 20:21]).then_inc(cx.h("act"), 1)
            cx.inc("act", DC * 2)

            @blk.vector
            def _(v):
                v = EG(v)
                for fj in range(DC):
                    for tt in range(2):
                        u = fj * 2 + tt
                        v.wait_ge(cx.h("act"), b_act + u + 1)
                        col = fj * T + tt * 512
                        v.tensor_tensor(X[:, col:col + 512], bc_sc[:, 0:512],
                                        X[:, col:col + 512], OP.add
                                        ).then_inc(cx.h("fr2"), 1)
            cx.inc("fr2", DC * 2)

        cx_fr2_base = cx.v("fr2")
        emit_proj(wf5, DC, xq_bf, evac_ffn, 1)
        if DBG == "x2":
            dbg_dump(X)
            ph.close()
            _audit(nc, cx)
            return nc, es

        ph.close()
        ph = ExitStack()
        qbuf1 = ph.enter_context(nc.sbuf_tensor("qbuf1", [128, NH * 512], BF16))
        kbuf1 = ph.enter_context(nc.sbuf_tensor("kbuf1", [128, NH * T], BF16))
        vbuf1 = ph.enter_context(nc.sbuf_tensor("vbuf1", [128, NH * T], BF16))
        rp1 = ph.enter_context(nc.sbuf_tensor("rp1", [64, 2 * 512], F32))
        ph3 = ExitStack()
        ln3_bf = ph3.enter_context(nc.sbuf_tensor("ln3_bf", [128, DC * T], BF16))

        emit_ln(ln3_bf)
        if DBG == "ln3":
            dbg_dump(ln3_bf)
            ph3.close(); ph.close()
            _audit(nc, cx)
            return nc, es

        units = []
        upref = {}
        for fj in range(48):
            upref[fj] = len(units)
            tts = (1,) if fj < NH else (0, 1)
            for tt in tts:
                units.append((fj, tt))
        b_w = {0: cx.v("gwl"), 1: cx.v("gwl2")}
        b_mm = cx.v("mm")
        b_fr2 = cx.v("fr2")
        with nc.Block() as blk:
            @blk.gpsimd
            def _(g):
                for fj in range(48):
                    nm = "gwl" if fj % 2 == 0 else "gwl2"
                    if fj >= 2:
                        g.wait_ge(cx.h("mm"), b_mm + upref[fj - 1])
                    g.dma_start(out=WR[:, (fj % 2) * D:(fj % 2) * D + D],
                                in_=wqkv5[fj, :, :, :]).then_inc(cx.h(nm), 16)
            cx.inc("gwl", 16 * 24)
            cx.inc("gwl2", 16 * 24)

            @blk.tensor
            def _(t):
                for u, (fj, tt) in enumerate(units):
                    nm = "gwl" if fj % 2 == 0 else "gwl2"
                    t.wait_ge(cx.h(nm), b_w[fj % 2] + 16 * (fj // 2 + 1))
                    if u >= 2:
                        t.wait_ge(cx.h("fr2"), b_fr2 + (u - 1))
                    for c in range(DC):
                        mi = t.matmul(pb[u % 2][:, 0:512],
                                      WR[:, (fj % 2) * D + c * 128:
                                         (fj % 2) * D + c * 128 + 128],
                                      ln3_bf[:, c * T + tt * 512:
                                             c * T + tt * 512 + 512],
                                      start=(c == 0), stop=(c == DC - 1))
                    mi.then_inc(cx.h("mm"), 1)
            cx.inc("mm", len(units))

            @blk.vector
            def _(v):
                v = EG(v)

                def rope1(ps, dst_lo, dst_hi):
                    c1 = cos1[0:64, 0:1]
                    s1_ = sin1[0:64, 0:1]
                    v.tensor_scalar(rp1[:, 0:512], ps[0:64, 0:512], c1, None,
                                    OP.mult)
                    v.tensor_scalar(rp1[:, 512:1024], ps[64:128, 0:512], s1_,
                                    None, OP.mult)
                    v.tensor_tensor(dst_lo, rp1[:, 0:512], rp1[:, 512:1024],
                                    OP.subtract)
                    v.tensor_scalar(rp1[:, 0:512], ps[64:128, 0:512], c1, None,
                                    OP.mult)
                    v.tensor_scalar(rp1[:, 512:1024], ps[0:64, 0:512], s1_,
                                    None, OP.mult)
                    return v.tensor_tensor(dst_hi, rp1[:, 0:512],
                                           rp1[:, 512:1024], OP.add)

                for u, (fj, tt) in enumerate(units):
                    v.wait_ge(cx.h("mm"), b_mm + u + 1)
                    ps = pb[u % 2]
                    if fj < NH:
                        col = fj * 512
                        fin = rope1(ps, qbuf1[0:64, col:col + 512],
                                    qbuf1[64:128, col:col + 512])
                    elif fj < 2 * NH:
                        h = fj - NH
                        col = h * T + tt * 512
                        if tt == 0:
                            fin = v.tensor_copy(kbuf1[:, col:col + 512],
                                                ps[:, 0:512])
                        else:
                            fin = rope1(ps, kbuf1[0:64, col:col + 512],
                                        kbuf1[64:128, col:col + 512])
                    else:
                        h = fj - 2 * NH
                        col = h * T + tt * 512
                        fin = v.tensor_copy(vbuf1[:, col:col + 512],
                                            ps[:, 0:512])
                    fin.then_inc(cx.h("fr2"), 1)
            cx.inc("fr2", len(units))

        ph3.close()
        if DBG == "qkv1":
            dbg_dump(kbuf1, NH * T)
            ph.close()
            _audit(nc, cx)
            return nc, es

        zb = ph.enter_context(nc.sbuf_tensor("zb", [128, 2 * 1024], BF16))
        pp = ph.enter_context(nc.sbuf_tensor("pp", [1, 2 * 512], F32))
        sc0 = ph.enter_context(nc.sbuf_tensor("sc0", [1, 3 * 512], F32))
        ct1 = ph.enter_context(nc.sbuf_tensor("ct1", [128, 2 * 512], F32))

        b_mm2 = cx.v("mm2")
        b_vec = cx.v("vec")
        b_act = cx.v("act")
        with nc.Block() as blk:
            @blk.vector
            def _(v):
                v = EG(v)
                for h in range(NH):
                    if h >= 2:
                        v.wait_ge(cx.h("mm2"), b_mm2 + 4 * (h - 2) + 2)
                    v.tensor_tensor(zb[:, (h % 2) * 1024:(h % 2) * 1024 + 512],
                                    qbuf1[:, h * 512:h * 512 + 512],
                                    kbuf1[:, h * T:h * T + 512], OP.mult)
                    v.tensor_tensor(zb[:, (h % 2) * 1024 + 512:
                                       (h % 2) * 1024 + 1024],
                                    qbuf1[:, h * 512:h * 512 + 512],
                                    kbuf1[:, h * T + 512:h * T + 1024],
                                    OP.mult).then_inc(cx.h("vec"), 1)
                    v.wait_ge(cx.h("mm2"), b_mm2 + 4 * h + 2)
                    v.tensor_copy(sc0[0:1, 0:512],
                                  pb[4][0:1, 0:512])
                    v.tensor_tensor(sc0[0:1, 512:1024], sc0[0:1, 0:512],
                                    pb[5][0:1, 0:512], OP.max)
                    v.tensor_tensor(sc0[0:1, 1024:1536], sc0[0:1, 0:512],
                                    sc0[0:1, 512:1024], OP.subtract)
                    v.tensor_tensor(sc0[0:1, 0:512], pb[5][0:1, 0:512],
                                    sc0[0:1, 512:1024],
                                    OP.subtract).then_inc(cx.h("vec"), 1)
                    v.wait_ge(cx.h("act"), b_act + 2 * h + 2)
                    v.tensor_tensor(sc0[0:1, 512:1024], pp[0:1, 0:512],
                                    pp[0:1, 512:1024], OP.add)
                    v.reciprocal(sc0[0:1, 512:1024], sc0[0:1, 512:1024])
                    v.tensor_tensor(pp[0:1, 0:512], pp[0:1, 0:512],
                                    sc0[0:1, 512:1024], OP.mult)
                    v.tensor_tensor(pp[0:1, 512:1024], pp[0:1, 512:1024],
                                    sc0[0:1, 512:1024],
                                    OP.mult).then_inc(cx.h("vec"), 1)
                    v.wait_ge(cx.h("mm2"), b_mm2 + 4 * h + 4)
                    v.tensor_tensor(ct1[:, 0:512], pb[6][:, 0:512],
                                    vbuf1[:, h * T:h * T + 512], OP.mult)
                    v.tensor_tensor(ct1[:, 512:1024], pb[7][:, 0:512],
                                    vbuf1[:, h * T + 512:h * T + 1024],
                                    OP.mult)
                    v.tensor_tensor(vbuf1[:, h * T + 512:h * T + 1024],
                                    ct1[:, 0:512], ct1[:, 512:1024],
                                    OP.add).then_inc(cx.h("vec"), 1)
            cx.inc("vec", 4 * NH)

            @blk.tensor
            def _(t):
                for h in range(NH):
                    t.wait_ge(cx.h("vec"), b_vec + 4 * h + 1)
                    t.matmul(pb[4][0:16, 0:512],
                             onehot[:, h * NH:(h + 1) * NH],
                             zb[:, (h % 2) * 1024:(h % 2) * 1024 + 512],
                             start=True, stop=True).then_inc(cx.h("mm2"), 1)
                    t.matmul(pb[5][0:16, 0:512],
                             onehot[:, h * NH:(h + 1) * NH],
                             zb[:, (h % 2) * 1024 + 512:(h % 2) * 1024 + 1024],
                             start=True, stop=True).then_inc(cx.h("mm2"), 1)
                    t.wait_ge(cx.h("vec"), b_vec + 4 * h + 3)
                    t.matmul(pb[6][:, 0:512], ones_f[0:1, 0:128],
                             pp[0:1, 0:512], start=True,
                             stop=True).then_inc(cx.h("mm2"), 1)
                    t.matmul(pb[7][:, 0:512], ones_f[0:1, 0:128],
                             pp[0:1, 512:1024], start=True,
                             stop=True).then_inc(cx.h("mm2"), 1)
            cx.inc("mm2", 4 * NH)

            @blk.scalar
            def _(s):
                s = EG(s)
                for h in range(NH):
                    s.wait_ge(cx.h("vec"), b_vec + 4 * h + 2)
                    s.activation(pp[0:1, 0:512], sc0[0:1, 1024:1536], AF.Exp,
                                 scale=ISQ128).then_inc(cx.h("act"), 1)
                    s.activation(pp[0:1, 512:1024], sc0[0:1, 0:512], AF.Exp,
                                 scale=ISQ128).then_inc(cx.h("act"), 1)
            cx.inc("act", 2 * NH)

        if DBG == "attn1":
            dbg_dump(vbuf1, NH * T)
            ph.close()
            _audit(nc, cx)
            return nc, es

        emit_proj(wd5, DC, vbuf1, evac_resid, 1)
        if DBG == "x3":
            dbg_dump(X)
            ph.close()
            _audit(nc, cx)
            return nc, es

        ph.close()
        ph = ExitStack()
        ln4_bf = ph.enter_context(nc.sbuf_tensor("ln4_bf", [128, DC * T], BF16))
        comb = ph.enter_context(nc.sbuf_tensor("compb", [128, 8 * 8], F32))
        combT = ph.enter_context(nc.sbuf_tensor("combT", [8, T], BF16))
        ph4 = ExitStack()
        ln4r = ph4.enter_context(nc.sbuf_tensor("ln4r", [128, 2 * 512], F32))
        rsd4 = ph4.enter_context(nc.sbuf_tensor("rsd4", [1, T], F32))
        gsc = ph4.enter_context(nc.sbuf_tensor("gsc", [8, T], F32))
        gw_sb = ph4.enter_context(nc.sbuf_tensor("gw_sb", [128, DC * 8], F32))
        gb_sb = ph4.enter_context(nc.sbuf_tensor("gb_sb", [1, 24], F32))

        b_wld = cx.v("wld")
        with nc.Block() as blk:
            @blk.sync
            def _(e):
                e.dma_start(out=gw_sb[:, :].rearrange("p (c e) -> p c e", e=8),
                            in_=gw_l.ap().rearrange("c p e -> p c e")
                            ).then_inc(cx.h("wld"), 16)
                e.dma_start(out=gb_sb[0:1, 0:8], in_=gate_b_in[:, :]
                            ).then_inc(cx.h("wld"), 16)
                e.dma_start(out=gb_sb[0:1, 8:16], in_=gate_bias_in[:, :]
                            ).then_inc(cx.h("wld"), 16)
                cx.inc("wld", 48)

            @blk.vector
            def _(v):
                v = EG(v)
                v.wait_ge(cx.h("wld"), b_wld + 48)
                v.tensor_tensor(gb_sb[0:1, 16:24], gb_sb[0:1, 0:8],
                                gb_sb[0:1, 8:16], OP.add).then_inc(cx.h("vec"), 1)
            cx.inc("vec", 1)

        for tt in range(2):
            b_mm = cx.v("mm")
            b_act = cx.v("act")
            b_vec = cx.v("vec")
            b_mm2 = cx.v("mm2")
            with nc.Block() as blk:
                @blk.vector
                def _(v, tt=tt):
                    v = EG(v)
                    for dc in range(DC):
                        if dc >= 2:
                            v.wait_ge(cx.h("mm"), b_mm + (dc - 1))
                        v.tensor_copy(
                            xb_ring[:, (dc % 2) * 512:(dc % 2) * 512 + 512],
                            X[:, dc * T + tt * 512: dc * T + tt * 512 + 512]
                        ).then_inc(cx.h("vec"), 1)
                    v.wait_ge(cx.h("mm"), b_mm + DC)
                    v.tensor_scalar(row_sc[0:1, 0:512], pb[6][0:1, 0:512],
                                    1.0 / 2048.0, None, OP.mult)
                    v.tensor_scalar(row_sc[0:1, 512:1024], pb[7][0:1, 0:512],
                                    1.0 / 2048.0, None, OP.mult)
                    v.tensor_tensor(row_sc[0:1, 1024:1536],
                                    row_sc[0:1, 0:512], row_sc[0:1, 0:512],
                                    OP.mult)
                    v.tensor_tensor(row_sc[0:1, 1024:1536],
                                    row_sc[0:1, 512:1024],
                                    row_sc[0:1, 1024:1536],
                                    OP.subtract).then_inc(cx.h("vec"), 1)
                    v.wait_ge(cx.h("act"), b_act + DC + 1)
                    v.reciprocal(row_sc[0:1, 512:1024],
                                 row_sc[0:1, 512:1024]).then_inc(cx.h("vec"), 1)
                    v.tensor_copy(rsd4[0:1, tt * 512:tt * 512 + 512],
                                  row_sc[0:1, 512:1024])
                    v.wait_ge(cx.h("mm"), b_mm + DC + 1)
                    v.tensor_copy(bc_sc[:, 0:512], pb[4][:, 0:512])
                    for dc in range(DC):
                        col = dc * T + tt * 512
                        if dc >= 2:
                            v.wait_ge(cx.h("mm2"), b_mm2 + (dc - 1))
                        v.tensor_tensor(stat[:, 0:512], X[:, col:col + 512],
                                        pb[5][:, 0:512], OP.subtract)
                        v.tensor_tensor(ln4r[:, (dc % 2) * 512:
                                             (dc % 2) * 512 + 512],
                                        stat[:, 0:512], bc_sc[:, 0:512],
                                        OP.mult)
                        v.tensor_copy(ln4_bf[:, col:col + 512],
                                      ln4r[:, (dc % 2) * 512:
                                           (dc % 2) * 512 + 512]
                                      ).then_inc(cx.h("vec"), 1)
                cx.inc("vec", 2 * DC + 2)

                @blk.scalar
                def _(s, tt=tt):
                    s = EG(s)
                    for dc in range(DC):
                        s.wait_ge(cx.h("vec"), b_vec + dc + 1)
                        if dc >= 2:
                            s.wait_ge(cx.h("mm"), b_mm + (dc - 1))
                        s.activation(
                            sq_ring[:, (dc % 2) * 512:(dc % 2) * 512 + 512],
                            xb_ring[:, (dc % 2) * 512:(dc % 2) * 512 + 512],
                            AF.Square).then_inc(cx.h("act"), 1)
                    s.wait_ge(cx.h("vec"), b_vec + DC + 1)
                    s.activation(row_sc[0:1, 512:1024],
                                 row_sc[0:1, 1024:1536], AF.Sqrt,
                                 bias=gbias[0:1, 17:18]).then_inc(cx.h("act"), 1)
                cx.inc("act", DC + 1)

                @blk.tensor
                def _(t, tt=tt):
                    for dc in range(DC):
                        t.wait_ge(cx.h("act"), b_act + dc + 1)
                        t.matmul(pb[6][0:1, 0:512], ones_b[:, 0:1],
                                 xb_ring[:, (dc % 2) * 512:(dc % 2) * 512 + 512],
                                 start=(dc == 0), stop=(dc == DC - 1))
                        t.matmul(pb[7][0:1, 0:512], ones_b[:, 0:1],
                                 sq_ring[:, (dc % 2) * 512:(dc % 2) * 512 + 512],
                                 start=(dc == 0), stop=(dc == DC - 1)
                                 ).then_inc(cx.h("mm"), 1)
                    t.wait_ge(cx.h("vec"), b_vec + DC + 2)
                    t.matmul(pb[5][:, 0:512], ones_f[0:1, 0:128],
                             row_sc[0:1, 0:512], start=True, stop=True)
                    t.matmul(pb[4][:, 0:512], ones_f[0:1, 0:128],
                             row_sc[0:1, 512:1024], start=True,
                             stop=True).then_inc(cx.h("mm"), 1)
                    for dc in range(DC):
                        t.wait_ge(cx.h("vec"), b_vec + DC + 3 + dc)
                        t.matmul(pb[2 + tt][0:8, 0:512],
                                 gw_sb[:, dc * 8:(dc + 1) * 8],
                                 ln4r[:, (dc % 2) * 512:(dc % 2) * 512 + 512],
                                 start=(dc == 0), stop=False
                                 ).then_inc(cx.h("mm2"), 1)
                    t.matmul(pb[2 + tt][0:8, 0:512], gb_sb[0:1, 16:24],
                             ones_f[0:1, 0:512], start=False,
                             stop=True).then_inc(cx.h("mm2"), 1)
                cx.inc("mm", DC + 1)
                cx.inc("mm2", DC + 1)

        b_mm2 = cx.v("mm2")
        b_vec = cx.v("vec")
        b_act = cx.v("act")
        with nc.Block() as blk:
            @blk.vector
            def _(v):
                v = EG(v)
                for tt in range(2):
                    v.tensor_copy(gsc[0:8, tt * 512:tt * 512 + 512],
                                  pb[2 + tt][0:8, 0:512])
                v.last.then_inc(cx.h("vec"), 1)
                v.wait_ge(cx.h("mm2"), b_mm2 + 2)
                for tt in range(2):
                    v.tensor_tensor(gsc[0:8, tt * 512:tt * 512 + 512],
                                    gsc[0:8, tt * 512:tt * 512 + 512],
                                    pb[2 + tt][0:8, 0:512], OP.mult)
                v.last.then_inc(cx.h("vec"), 1)
                for tc in range(8):
                    v.wait_ge(cx.h("mm2"), b_mm2 + 2 + tc + 1)
                    g0 = stat[:, 0:8]
                    v.tensor_copy(g0, pb[tc % 2][0:128, 0:8])
                    v.tensor_reduce(stat[:, 8:9], g0, axis=AX, op=OP.max)
                    v.tensor_scalar(stat[:, 16:24], g0, stat[:, 8:9], None,
                                    OP.is_ge)
                    v.scalar_tensor_tensor(stat[:, 24:32], stat[:, 16:24],
                                           NEG, g0, OP.mult, OP.add)
                    v.tensor_reduce(stat[:, 9:10], stat[:, 24:32], axis=AX,
                                    op=OP.max)
                    v.tensor_scalar(stat[:, 32:40], stat[:, 24:32],
                                    stat[:, 9:10], None, OP.is_ge)
                    v.tensor_tensor(stat[:, 10:11], stat[:, 9:10],
                                    stat[:, 8:9],
                                    OP.subtract).then_inc(cx.h("vec"), 1)
                    v.wait_ge(cx.h("act"), b_act + tc + 1)
                    v.tensor_scalar(stat[:, 12:13], stat[:, 11:12], -1.0, 1.0,
                                    OP.mult, OP.add)
                    v.tensor_scalar(stat[:, 32:40], stat[:, 32:40],
                                    stat[:, 11:12], None, OP.mult)
                    v.scalar_tensor_tensor(comb[:, tc * 8:(tc + 1) * 8],
                                           stat[:, 16:24], stat[:, 12:13],
                                           stat[:, 32:40], OP.mult,
                                           OP.add).then_inc(cx.h("vec"), 1)
                for tc in range(8):
                    v.wait_ge(cx.h("mm2"), b_mm2 + 10 + tc + 1)
                    v.tensor_copy(combT[0:8, tc * 128:(tc + 1) * 128],
                                  pb[4 + tc % 2][0:8, 0:128]
                                  ).then_inc(cx.h("vec"), 1)
            cx.inc("vec", 2 + 16 + 8)

            @blk.tensor
            def _(t):
                t.wait_ge(cx.h("vec"), b_vec + 1)
                for tt in range(2):
                    t.matmul(pb[2 + tt][0:8, 0:512], ones_f[0:1, 0:8],
                             rsd4[0:1, tt * 512:tt * 512 + 512],
                             start=True, stop=True).then_inc(cx.h("mm2"), 1)
                for tc in range(8):
                    t.wait_ge(cx.h("vec"), b_vec + 2)
                    if tc >= 2:
                        t.wait_ge(cx.h("vec"), b_vec + 2 + 2 * (tc - 2) + 1)
                    tti = tc * 128 // 512
                    t.transpose(pb[tc % 2][0:128, 0:8],
                                gsc[0:8, tc * 128:(tc + 1) * 128],
                                ident_f[0:8, 0:8]).then_inc(cx.h("mm2"), 1)
                for tc in range(8):
                    t.wait_ge(cx.h("vec"), b_vec + 2 + 2 * tc + 2)
                    if tc >= 2:
                        t.wait_ge(cx.h("vec"), b_vec + 18 + (tc - 2) + 1)
                    t.transpose(pb[4 + tc % 2][0:8, 0:128],
                                comb[:, tc * 8:(tc + 1) * 8],
                                ident_f[:, :]).then_inc(cx.h("mm2"), 1)
            cx.inc("mm2", 18)

            @blk.scalar
            def _(s):
                s = EG(s)
                for tc in range(8):
                    s.wait_ge(cx.h("vec"), b_vec + 2 + 2 * tc + 1)
                    s.activation(stat[:, 11:12], stat[:, 10:11],
                                 AF.Sigmoid).then_inc(cx.h("act"), 1)
            cx.inc("act", 8)

        if DBG == "gates":
            dbg_dump(comb, 64)
            ph.close()
            _audit(nc, cx)
            return nc, es

        ph4.close()
        h_sb = ph.enter_context(nc.sbuf_tensor("h_sb", [128, 11 * T], BF16))
        w2_sb = ph.enter_context(nc.sbuf_tensor("w2_sb", [128, 11 * D], BF16))
        b1_sb = ph.enter_context(nc.sbuf_tensor("b1_sb", [128, 88], F32))
        b2_sb = ph.enter_context(nc.sbuf_tensor("b2_sb", [8, D], BF16))

        b_mm2 = cx.v("mm2")
        b_vec = cx.v("vec")
        b_wld = cx.v("wld")
        b_gld = cx.v("gld")
        with nc.Block() as blk:
            @blk.sync
            def _(e):
                e.dma_start(out=b1_sb[:, :], in_=b1r[:, :]
                            ).then_inc(cx.h("wld"), 16)
                cx.inc("wld", 16)
            wld_b1 = cx.v("wld")

            @blk.gpsimd
            def _(g):
                g.dma_start(out=b2_sb[:, :], in_=b2_in[:, :]
                            ).then_inc(cx.h("gld"), 16)
                cx.inc("gld", 16)

            @blk.tensor
            def _(t):
                for i in range(128):
                    dc, tc = i // 8, i % 8
                    if i >= 4:
                        t.wait_ge(cx.h("vec"), b_vec + (i - 4) + 1)
                    t.transpose(pb[i % 4][:, 0:128],
                                X[:, dc * T + tc * 128:dc * T + tc * 128 + 128],
                                ident_f[:, :]).then_inc(cx.h("mm2"), 1)
                t.wait_ge(cx.h("gld"), b_gld + 16)
                for u in range(32):
                    tc, dj = u // 4, u % 4
                    if u >= 2:
                        t.wait_ge(cx.h("vec"), b_vec + 128 + (u - 1))
                    t.matmul(pb[4 + u % 2][:, 0:512],
                             combT[0:8, tc * 128:(tc + 1) * 128],
                             b2_sb[:, dj * 512:dj * 512 + 512],
                             start=True, stop=True).then_inc(cx.h("mm2"), 1)
            cx.inc("mm2", 128 + 32)

            @blk.vector
            def _(v):
                v = EG(v)
                for i in range(128):
                    dc, tc = i // 8, i % 8
                    v.wait_ge(cx.h("mm2"), b_mm2 + i + 1)
                    v.tensor_copy(X[:, dc * T + tc * 128:dc * T + tc * 128 + 128],
                                  pb[i % 4][:, 0:128]).then_inc(cx.h("vec"), 1)
                for u in range(32):
                    tc, dj = u // 4, u % 4
                    v.wait_ge(cx.h("mm2"), b_mm2 + 128 + u + 1)
                    for q4 in range(4):
                        dc = dj * 4 + q4
                        v.scalar_tensor_tensor(
                            X[:, dc * T + tc * 128:dc * T + tc * 128 + 128],
                            pb[4 + u % 2][:, q4 * 128:q4 * 128 + 128],
                            1.0,
                            X[:, dc * T + tc * 128:dc * T + tc * 128 + 128],
                            OP.mult, OP.add)
                    v.last.then_inc(cx.h("vec"), 1)
            cx.inc("vec", 160)

        for e_ in range(8):
            b_w = {0: cx.v("gwl"), 1: cx.v("gwl2")}
            b_kvw = {0: cx.v("gw3"), 1: cx.v("gw4")}
            b_mm = cx.v("mm")
            b_act = cx.v("act")
            b_mm2 = cx.v("mm2")
            b_vec = cx.v("vec")
            with nc.Block() as blk:
                @blk.gpsimd
                def _(g, e_=e_):
                    g.wait_ge(cx.h("mm"), b_mm)
                    g.wait_ge(cx.h("mm2"), b_mm2)
                    for hj in range(11):
                        nm = "gwl" if hj % 2 == 0 else "gwl2"
                        if hj >= 2:
                            g.wait_ge(cx.h("mm"), b_mm + 2 * (hj - 1))
                        g.dma_start(out=WR[:, (hj % 2) * D:(hj % 2) * D + D],
                                    in_=w15[e_, hj, :, :, :]
                                    ).then_inc(cx.h(nm), 16)
                    for hj in range(11):
                        nm = "gw3" if hj % 2 == 0 else "gw4"
                        g.dma_start(out=w2_sb[:, hj * D:(hj + 1) * D],
                                    in_=w25r[e_, hj, :, :]).then_inc(cx.h(nm), 16)
                cx.inc("gwl", 16 * 6)
                cx.inc("gwl2", 16 * 5)
                cx.inc("gw3", 16 * 6)
                cx.inc("gw4", 16 * 5)

                @blk.tensor
                def _(t, e_=e_):
                    for hj in range(11):
                        nm = "gwl" if hj % 2 == 0 else "gwl2"
                        t.wait_ge(cx.h(nm), b_w[hj % 2] + 16 * (hj // 2 + 1))
                        for tt in range(2):
                            u = hj * 2 + tt
                            if u >= 2:
                                t.wait_ge(cx.h("act"), b_act + (u - 2) + 1)
                            for c in range(DC):
                                mi = t.matmul(pb[u % 2][:, 0:512],
                                              WR[:, (hj % 2) * D + c * 128:
                                                 (hj % 2) * D + c * 128 + 128],
                                              ln4_bf[:, c * T + tt * 512:
                                                     c * T + tt * 512 + 512],
                                              start=(c == 0),
                                              stop=(c == DC - 1))
                            mi.then_inc(cx.h("mm"), 1)
                    t.wait_ge(cx.h("gw3"), b_kvw[0] + 16 * 6)
                    t.wait_ge(cx.h("gw4"), b_kvw[1] + 16 * 5)
                    t.wait_ge(cx.h("act"), b_act + 22)
                    for u2 in range(32):
                        tc, dj = u2 // 4, u2 % 4
                        if u2 >= 2:
                            t.wait_ge(cx.h("vec"), b_vec + (u2 - 1))
                        for hj in range(11):
                            mi = t.matmul(
                                pb[2 + u2 % 2][:, 0:512],
                                h_sb[:, hj * T + tc * 128:
                                     hj * T + tc * 128 + 128],
                                w2_sb[:, hj * D + dj * 512:
                                      hj * D + dj * 512 + 512],
                                start=(hj == 0), stop=(hj == 10))
                        mi.then_inc(cx.h("mm2"), 1)
                cx.inc("mm", 22)
                cx.inc("mm2", 32)

                @blk.scalar
                def _(s, e_=e_):
                    s = EG(s)
                    s.wait_ge(cx.h("wld"), wld_b1)
                    for hj in range(11):
                        for tt in range(2):
                            u = hj * 2 + tt
                            s.wait_ge(cx.h("mm"), b_mm + u + 1)
                            s.activation(h_sb[:, hj * T + tt * 512:
                                              hj * T + tt * 512 + 512],
                                         pb[u % 2][:, 0:512], AF.Silu,
                                         bias=b1_sb[:, e_ * 11 + hj:
                                                    e_ * 11 + hj + 1]
                                         ).then_inc(cx.h("act"), 1)
                cx.inc("act", 22)

                @blk.vector
                def _(v, e_=e_):
                    v = EG(v)
                    for u2 in range(32):
                        tc, dj = u2 // 4, u2 % 4
                        v.wait_ge(cx.h("mm2"), b_mm2 + u2 + 1)
                        for q4 in range(4):
                            dc = dj * 4 + q4
                            v.scalar_tensor_tensor(
                                X[:, dc * T + tc * 128:dc * T + tc * 128 + 128],
                                pb[2 + u2 % 2][:, q4 * 128:q4 * 128 + 128],
                                comb[:, tc * 8 + e_:tc * 8 + e_ + 1],
                                X[:, dc * T + tc * 128:dc * T + tc * 128 + 128],
                                OP.mult, OP.add)
                        v.last.then_inc(cx.h("vec"), 1)
                cx.inc("vec", 32)

        with nc.Block() as blk:
            @blk.sync
            def _(e):
                for nm in ["mm", "mm2", "act", "vec"]:
                    e.wait_ge(cx.h(nm), cx.v(nm))
                for tc in range(8):
                    e.dma_start(
                        out=out[tc * 128:(tc + 1) * 128, :],
                        in_=X[:, :].rearrange("p (c x t) -> p c x t",
                                              c=DC, x=8)[:, :, tc, :],
                    ).then_inc(cx.h("sto"), 16)
                e.wait_ge(cx.h("sto"), cx.v("sto") + 128)
                cx.inc("sto", 128)

        ph.close()
        _audit(nc, cx)

    return nc, es




def _audit(nc, cx):
    totals = {}
    max_wait = {}
    for inst in nc.all_instructions():
        si = getattr(inst, "sync_info", None)
        if si is None:
            continue
        for u in si.on_update:
            if u.sync_type == "semaphore" and u.update_mode in ("sem-inc", "sem-add-imm"):
                totals[u.ant_name] = totals.get(u.ant_name, 0) + u.update_value
        for w in si.on_wait:
            if w.sync_type == "semaphore" and w.wait_mode == "sem-ge-imm":
                max_wait[w.ant_name] = max(max_wait.get(w.ant_name, 0), w.wait_value)
    errs = []
    for name, (h, v) in cx.sem.items():
        t = totals.get(h.name, 0)
        if t != v:
            errs.append(f"sem {name}: python counter {v} != emitted {t}")
        mw = max_wait.get(h.name, 0)
        if mw > t:
            errs.append(f"sem {name}: wait {mw} exceeds total {t} (hang)")
    if errs:
        raise AssertionError("SEM AUDIT FAILED:\n" + "\n".join(errs))


def kernel(**inputs):
    shared = _stage_shared(inputs)
    cores = [_stage_core(inputs, c) for c in range(NCORES)]
    nc, _es = build_nc()
    in_maps = []
    for c in range(NCORES):
        m = {k: shared[k] for k in
             ["wq5", "wk5", "wv5r", "wo5", "wf5", "ffn_bias_t", "ffn_ws",
              "wqkv5", "wd5", "gw_l", "gate_b", "gate_bias", "w15", "b1r",
              "w25r", "b2", "ident_f", "ident_b", "ones_f", "ones_b",
              "onehot_b", "cos1", "sin1"]}
        m.update(cores[c])
        in_maps.append(m)
    res = run_bass_kernel_spmd(nc, in_maps, core_ids=list(range(NCORES)))
    full = np.empty((2, 4096, D), dtype=np.float32)
    for c in range(NCORES):
        o = np.asarray(res.results[c]["out"])
        full[0, JB * c:JB * (c + 1), :] = o[0:JB]
        full[1, JB * c:JB * (c + 1), :] = o[JB:T]
    return full


# version 07
# speedup vs baseline: 1.0735x; 1.0602x over previous
import numpy as np
import ml_dtypes
from contextlib import ExitStack

import concourse.bass as bass
import concourse.mybir as mybir
from concourse.bass_utils import run_bass_kernel_spmd

F32 = mybir.dt.float32
BF16 = mybir.dt.bfloat16
I32 = mybir.dt.int32
AX = mybir.AxisListType.X
OP = mybir.AluOpType
AF = mybir.ActivationFunctionType

NCORES = 8
D = 2048
DC = 16
NH = 16
HD = 128
JB = 512
T = 1024
LN2C = 0.6931471805599453
INV_LN2 = 1.4426950408889634
ISQ128 = float(1.0 / np.sqrt(128.0))
NEG = -1.0e30

DBG = None



def _lhsT5(w):
    K, F = w.shape
    return np.ascontiguousarray(
        w.reshape(K // 128, 128, F // 128, 128).transpose(2, 1, 0, 3))


def _stage_shared(inp):
    s = {}
    s["wq5"] = _lhsT5(inp["L0_attn_wq"])
    s["wk5"] = _lhsT5(inp["L0_attn_wk"])
    s["wv5r"] = np.ascontiguousarray(inp["L0_attn_wv"].reshape(DC, 128, D))
    s["wo5"] = _lhsT5(inp["L0_attn_wo"])
    s["wf5"] = _lhsT5(np.ascontiguousarray(np.asarray(inp["L0_ffn_weight"]).T))
    s["ffn_bias_t"] = np.ascontiguousarray(
        np.asarray(inp["L0_ffn_bias"]).reshape(DC, 128).T).astype(np.float32)
    s["ffn_ws"] = np.asarray(inp["L0_ffn_weight_scale"]).reshape(1, 1).astype(np.float32)

    wqkv = np.asarray(inp["L1_attn_w_qkv"])
    perm = np.concatenate([
        np.concatenate([np.arange(384 * h + 128 * k, 384 * h + 128 * k + 128)
                        for h in range(NH)])
        for k in range(3)])
    s["wqkv5"] = _lhsT5(np.ascontiguousarray(wqkv[:, perm]))
    s["wd5"] = _lhsT5(np.asarray(inp["L1_attn_w_dense"]))

    s["gw_l"] = np.ascontiguousarray(
        np.asarray(inp["L1_ffn_gate_w"]).T.reshape(DC, 128, 8)).astype(np.float32)
    s["gate_b"] = np.asarray(inp["L1_ffn_gate_b"]).reshape(1, 8).astype(np.float32)
    s["gate_bias"] = np.asarray(inp["L1_ffn_bias"]).reshape(1, 8).astype(np.float32)
    w1 = np.asarray(inp["L1_ffn_w1"])
    s["w15"] = np.ascontiguousarray(
        w1.transpose(0, 2, 1).reshape(8, DC, 128, 11, 128).transpose(0, 3, 2, 1, 4))
    s["b1r"] = np.ascontiguousarray(np.asarray(inp["L1_ffn_b1"]).reshape(88, 128).T).astype(np.float32)
    w2 = np.asarray(inp["L1_ffn_w2"])
    s["w25r"] = np.ascontiguousarray(w2.transpose(0, 2, 1).reshape(8, 11, 128, D))
    s["b2"] = np.ascontiguousarray(np.asarray(inp["L1_ffn_b2"]))

    s["ident_f"] = np.eye(128, dtype=np.float32)
    s["ident_b"] = np.eye(128, dtype=np.float32).astype(ml_dtypes.bfloat16)
    s["ones_f"] = np.ones((128, 512), dtype=np.float32)
    s["ones_b"] = np.ones((128, 128), dtype=np.float32).astype(ml_dtypes.bfloat16)
    oh = np.zeros((NH, 128, NH), dtype=np.float32)
    for h in range(NH):
        oh[h, :, h] = 1.0
    s["onehot_b"] = oh.astype(ml_dtypes.bfloat16)
    inv1 = (1.0 / (np.float32(10000.0) **
                   (np.arange(0, 128, 2, dtype=np.float32) / np.float32(128))))
    emb1 = np.concatenate([inv1, inv1]).astype(np.float32)
    s["cos1"] = np.cos(emb1).astype(np.float32).reshape(128, 1)
    s["sin1"] = np.sin(emb1).astype(np.float32).reshape(128, 1)
    return s


def _stage_core(inp, c):
    d = {}
    x = np.asarray(inp["L0_attn_hidden_states"])
    sl = x[:, JB * c: JB * (c + 1), :].reshape(T, D)
    d["x0T"] = np.ascontiguousarray(sl.T).astype(np.float32)

    pos = np.arange(JB * c, JB * (c + 1), dtype=np.float32)
    inv = (1.0 / (np.float32(10000.0) **
                  (np.arange(0, HD, 2, dtype=np.float32) / np.float32(HD))))
    freqs = pos[:, None] * inv[None, :]
    cosT = np.cos(freqs).astype(np.float32).T
    sinT = np.sin(freqs).astype(np.float32).T
    d["cosT"] = np.ascontiguousarray(np.concatenate([cosT, cosT], axis=1))
    d["sinT"] = np.ascontiguousarray(np.concatenate([sinT, sinT], axis=1))

    kt = np.arange(JB)[:, None]
    qt = np.arange(JB)[None, :]
    m0 = np.where(kt > qt, 0.0, NEG).astype(np.float32)
    m1 = np.zeros((JB, JB), dtype=np.float32)
    m2 = np.where(kt <= qt, 0.0, NEG).astype(np.float32)
    if c < 2:
        m0 = np.full((JB, JB), NEG, dtype=np.float32)
    if c < 1:
        m1 = np.full((JB, JB), NEG, dtype=np.float32)
    mm = np.stack([m0, m1, m2]).reshape(3, 4, 128, JB)
    d["negmask"] = np.ascontiguousarray(mm).astype(ml_dtypes.bfloat16)
    return d




_DVE_DATA_OPS = {
    "tensor_tensor", "tensor_scalar", "tensor_copy", "tensor_reduce",
    "reduce_max", "reduce_sum", "reciprocal", "select", "memset",
    "scalar_tensor_tensor", "tensor_add", "tensor_sub", "tensor_mul",
    "activation", "copy", "add", "mul", "sign", "sqrt", "square", "transpose",
}


class EG:

    def __init__(self, eng):
        self._e = eng
        self.last = None

    def __getattr__(self, name):
        f = getattr(self._e, name)
        if name in _DVE_DATA_OPS:
            def g(*a, **k):
                self._e.drain()
                self.last = f(*a, **k)
                return self.last
            return g
        return f


class Ctx:
    def __init__(self, nc, es):
        self.nc = nc
        self.sem = {}
        for name in ["wld", "kv", "mm", "mm2", "act", "vec", "fre", "fr2",
                     "sto", "cc", "gld", "gwl", "gwl2", "kva", "kvb",
                     "sta", "stb", "tra", "trb", "sva", "svb", "gw3", "gw4"]:
            self.sem[name] = [es.enter_context(nc.semaphore(name)), 0]

    def h(self, name):
        return self.sem[name][0]

    def v(self, name):
        return self.sem[name][1]

    def inc(self, name, n=1):
        self.sem[name][1] += n
        return self.sem[name][1]


def build_nc():
    nc = bass.Bass("TRN2", target_bir_lowering=False)
    es = ExitStack()
    RG = [list(range(NCORES))]

    x_in = nc.dram_tensor("x0T", [D, T], F32, kind="ExternalInput")
    out = nc.dram_tensor("out", [T, D], F32, kind="ExternalOutput")

    def din(name, shape, dt=F32):
        return nc.dram_tensor(name, list(shape), dt, kind="ExternalInput")

    wq5 = din("wq5", (NH, 128, DC, 128))
    wk5 = din("wk5", (NH, 128, DC, 128))
    wv5r = din("wv5r", (DC, 128, D))
    wo5 = din("wo5", (DC, 128, DC, 128))
    wf5 = din("wf5", (DC, 128, DC, 128))
    ffn_bias_t = din("ffn_bias_t", (128, DC))
    ffn_ws = din("ffn_ws", (1, 1))
    wqkv5 = din("wqkv5", (48, 128, DC, 128))
    wd5 = din("wd5", (DC, 128, DC, 128))
    gw_l = din("gw_l", (DC, 128, 8))
    gate_b_in = din("gate_b", (1, 8))
    gate_bias_in = din("gate_bias", (1, 8))
    w15 = din("w15", (8, 11, 128, DC, 128))
    b1r = din("b1r", (128, 88))
    w25r = din("w25r", (8, 11, 128, D))
    b2_in = din("b2", (8, D))
    ident_f_in = din("ident_f", (128, 128))
    ident_b_in = din("ident_b", (128, 128), BF16)
    ones_f_in = din("ones_f", (128, 512))
    ones_b_in = din("ones_b", (128, 128), BF16)
    onehot_in = din("onehot_b", (NH, 128, NH), BF16)
    cos1_in = din("cos1", (128, 1))
    sin1_in = din("sin1", (128, 1))
    cosT_in = din("cosT", (64, T))
    sinT_in = din("sinT", (64, T))
    negm_in = din("negmask", (3, 4, 128, JB), BF16)

    k_send = nc.dram_tensor("k_send", [NH, 128, T], BF16)
    v_send = nc.dram_tensor("v_send", [NH, 8, 128, 128], BF16)
    agk = nc.dram_tensor("agk", [NCORES, NH, 128, T], BF16)
    agv = nc.dram_tensor("agv", [NCORES, NH, 8, 128, 128], BF16)
    band_k = nc.dram_tensor("band_k", [2, NH, 128, T], BF16)
    band_v = nc.dram_tensor("band_v", [2, NH, 8, 128, 128], BF16)
    ar_in = nc.dram_tensor("ar_in", [1, 1], F32)
    ar_out = nc.dram_tensor("ar_out", [1, 1], F32)
    dbg = None
    if DBG is not None:
        dbg = nc.dram_tensor("dbg", [128, DC * T], F32, kind="ExternalOutput")

    with es:
        cx = Ctx(nc, es)

        def sb(name, shape, dt):
            return es.enter_context(nc.sbuf_tensor("s_" + name, list(shape), dt))

        X = sb("X", (128, DC * T), F32)
        ident_f = sb("ident_f", (128, 128), F32)
        ident_b = sb("ident_b", (128, 128), BF16)
        ones_f = sb("ones_f", (128, 512), F32)
        ones_b = sb("ones_b", (128, 128), BF16)
        onehot = sb("onehot", (128, NH * NH), BF16)
        cos1 = sb("cos1", (128, 1), F32)
        sin1 = sb("sin1", (128, 1), F32)
        stat = sb("stat", (128, 1024), F32)
        gbias = sb("gbias", (128, 24), F32)
        WR = sb("WR", (128, 2 * D), BF16)
        sq_ring = sb("sq_ring", (128, 2 * 512), BF16)
        xb_ring = sb("xb_ring", (128, 2 * 512), BF16)
        row_sc = sb("row_sc", (1, 3 * 512), F32)
        bc_sc = sb("bc_sc", (128, 512), F32)

        pb = [es.enter_context(nc.psum_tensor(f"pb{i}", [128, 512], F32))
              for i in range(8)]

        ph_q = ExitStack()
        qbuf = ph_q.enter_context(nc.sbuf_tensor("qbuf", [128, NH * T], BF16))
        ph = ExitStack()
        cosT = ph.enter_context(nc.sbuf_tensor("s_cosT", [64, T], F32))
        sinT = ph.enter_context(nc.sbuf_tensor("s_sinT", [64, T], F32))

        def dbg_dump(src, width=DC * T):
            with nc.Block() as blk:
                @blk.vector
                def _(v):
                    v = EG(v)
                    for nm in ["mm", "mm2", "act", "vec", "sto", "wld", "kv"]:
                        v.wait_ge(cx.h(nm), cx.v(nm))
                    step = 2048
                    for i in range(0, width, step):
                        w = min(step, width - i)
                        v.tensor_copy(stat[:, 0:512], stat[:, 0:512])
                        break
                @blk.gpsimd
                def _(g):
                    for nm in ["mm", "mm2", "act", "vec", "sto", "wld", "kv"]:
                        g.wait_ge(cx.h(nm), cx.v(nm))
                    g.dma_start(out=dbg[:, 0:width], in_=src[:, 0:width]
                                ).then_inc(cx.h("gld"), 16)
                    g.wait_ge(cx.h("gld"), cx.v("gld") + 16)
                    cx.inc("gld", 16)

        with nc.Block() as blk:
            @blk.sync
            def _(e):
                e.dma_start(
                    out=X[:, :].rearrange("p (c t) -> p c t", t=T),
                    in_=x_in.ap().rearrange("(c p) t -> p c t", p=128),
                ).then_inc(cx.h("wld"), 16)
                cx.inc("wld", 16)
                for dst, src in [(ident_f, ident_f_in), (ident_b, ident_b_in),
                                 (ones_f, ones_f_in), (ones_b, ones_b_in),
                                 (cos1, cos1_in), (sin1, sin1_in),
                                 (cosT, cosT_in), (sinT, sinT_in)]:
                    e.dma_start(out=dst[:, :], in_=src[:, :]).then_inc(cx.h("wld"), 16)
                    cx.inc("wld", 16)
                e.dma_start(
                    out=onehot[:, :].rearrange("p (h f) -> p h f", h=NH),
                    in_=onehot_in.ap().rearrange("h p f -> p h f"),
                ).then_inc(cx.h("wld"), 16)
                cx.inc("wld", 16)
                e.dma_start(out=gbias[:, 0:DC], in_=ffn_bias_t[:, :]).then_inc(cx.h("wld"), 16)
                cx.inc("wld", 16)
                e.dma_start(out=gbias[0:1, 16:17], in_=ffn_ws[:, :]).then_inc(cx.h("wld"), 16)
                cx.inc("wld", 16)
                e.wait_ge(cx.h("wld"), cx.v("wld"))

            @blk.vector
            def _(v):
                v = EG(v)
                v.memset(gbias[0:1, 17:18], 1e-5)

        def emit_ln(dst, fp32_cb=None):
            for tt in range(2):
                b_mm = cx.v("mm")
                b_act = cx.v("act")
                b_vec = cx.v("vec")
                with nc.Block() as blk:
                    @blk.vector
                    def _(v, tt=tt):
                        v = EG(v)
                        for dc in range(DC):
                            if dc >= 2:
                                v.wait_ge(cx.h("mm"), b_mm + (dc - 1))
                            v.tensor_copy(
                                xb_ring[:, (dc % 2) * 512:(dc % 2) * 512 + 512],
                                X[:, dc * T + tt * 512: dc * T + tt * 512 + 512]
                            ).then_inc(cx.h("vec"), 1)
                        v.wait_ge(cx.h("mm"), b_mm + DC)
                        v.tensor_scalar(row_sc[0:1, 0:512], pb[6][0:1, 0:512],
                                        1.0 / 2048.0, None, OP.mult)
                        v.tensor_scalar(row_sc[0:1, 512:1024], pb[7][0:1, 0:512],
                                        1.0 / 2048.0, None, OP.mult)
                        v.tensor_tensor(row_sc[0:1, 1024:1536],
                                        row_sc[0:1, 0:512], row_sc[0:1, 0:512],
                                        OP.mult)
                        v.tensor_tensor(row_sc[0:1, 1024:1536],
                                        row_sc[0:1, 512:1024],
                                        row_sc[0:1, 1024:1536],
                                        OP.subtract).then_inc(cx.h("vec"), 1)
                        v.wait_ge(cx.h("act"), b_act + DC + 1)
                        v.reciprocal(row_sc[0:1, 512:1024],
                                     row_sc[0:1, 512:1024]).then_inc(cx.h("vec"), 1)
                        v.wait_ge(cx.h("mm"), b_mm + DC + 1)
                        v.tensor_copy(bc_sc[:, 0:512], pb[4][:, 0:512])
                        for dc in range(DC):
                            col = dc * T + tt * 512
                            v.tensor_tensor(stat[:, 0:512], X[:, col:col + 512],
                                            pb[5][:, 0:512], OP.subtract)
                            if fp32_cb is not None:
                                fp32_cb(v, dc, tt)
                            v.tensor_tensor(dst[:, col:col + 512],
                                            stat[:, 0:512], bc_sc[:, 0:512],
                                            OP.mult).then_inc(cx.h("vec"), 1)
                    cx.inc("vec", 2 * DC + 2)

                    @blk.scalar
                    def _(s, tt=tt):
                        s = EG(s)
                        for dc in range(DC):
                            s.wait_ge(cx.h("vec"), b_vec + dc + 1)
                            if dc >= 2:
                                s.wait_ge(cx.h("mm"), b_mm + (dc - 1))
                            s.activation(
                                sq_ring[:, (dc % 2) * 512:(dc % 2) * 512 + 512],
                                xb_ring[:, (dc % 2) * 512:(dc % 2) * 512 + 512],
                                AF.Square).then_inc(cx.h("act"), 1)
                        s.wait_ge(cx.h("vec"), b_vec + DC + 1)
                        s.activation(row_sc[0:1, 512:1024],
                                     row_sc[0:1, 1024:1536], AF.Sqrt,
                                     bias=gbias[0:1, 17:18]).then_inc(cx.h("act"), 1)
                    cx.inc("act", DC + 1)

                    @blk.tensor
                    def _(t, tt=tt):
                        for dc in range(DC):
                            t.wait_ge(cx.h("act"), b_act + dc + 1)
                            t.matmul(pb[6][0:1, 0:512], ones_b[:, 0:1],
                                     xb_ring[:, (dc % 2) * 512:(dc % 2) * 512 + 512],
                                     start=(dc == 0), stop=(dc == DC - 1))
                            t.matmul(pb[7][0:1, 0:512], ones_b[:, 0:1],
                                     sq_ring[:, (dc % 2) * 512:(dc % 2) * 512 + 512],
                                     start=(dc == 0), stop=(dc == DC - 1)
                                     ).then_inc(cx.h("mm"), 1)
                        t.wait_ge(cx.h("vec"), b_vec + DC + 2)
                        t.matmul(pb[5][:, 0:512], ones_f[0:1, 0:128],
                                 row_sc[0:1, 0:512], start=True, stop=True)
                        t.matmul(pb[4][:, 0:512], ones_f[0:1, 0:128],
                                 row_sc[0:1, 512:1024], start=True,
                                 stop=True).then_inc(cx.h("mm"), 1)
                    cx.inc("mm", DC + 1)

        def emit_proj(w5, n_fj, rhs, evac_fn, n_evac_per_unit):
            b_w = {0: cx.v("gwl"), 1: cx.v("gwl2")}
            b_fr2 = cx.v("fr2")
            b_mm = cx.v("mm")
            with nc.Block() as blk:
                @blk.gpsimd
                def _(g):
                    for fj in range(n_fj):
                        nm = "gwl" if fj % 2 == 0 else "gwl2"
                        if fj >= 2:
                            g.wait_ge(cx.h("mm"), b_mm + 2 * (fj - 1))
                        g.dma_start(out=WR[:, (fj % 2) * D:(fj % 2) * D + D],
                                    in_=w5[fj, :, :, :]).then_inc(cx.h(nm), 16)
                cx.inc("gwl", 16 * ((n_fj + 1) // 2))
                cx.inc("gwl2", 16 * (n_fj // 2))

                @blk.tensor
                def _(t):
                    for fj in range(n_fj):
                        nm = "gwl" if fj % 2 == 0 else "gwl2"
                        t.wait_ge(cx.h(nm), b_w[fj % 2] + 16 * (fj // 2 + 1))
                        for tt in range(2):
                            u = fj * 2 + tt
                            if u >= 2:
                                t.wait_ge(cx.h("fr2"), b_fr2 + (u - 1))
                            for c in range(DC):
                                mi = t.matmul(pb[u % 2][:, 0:512],
                                              WR[:, (fj % 2) * D + c * 128:
                                                 (fj % 2) * D + c * 128 + 128],
                                              rhs[:, c * T + tt * 512:
                                                  c * T + tt * 512 + 512],
                                              start=(c == 0), stop=(c == DC - 1))
                            mi.then_inc(cx.h("mm"), 1)
                cx.inc("mm", n_fj * 2)
                evac_fn(blk, b_mm)

        ln1_bf = ph.enter_context(nc.sbuf_tensor("ln1_bf", [128, DC * T], BF16))

        emit_ln(ln1_bf)
        if DBG == "ln1":
            dbg_dump(ln1_bf)
            ph.close(); ph_q.close()
            _audit(nc, cx)
            return nc, es

        rp_t1 = ph.enter_context(nc.sbuf_tensor("rp_t1", [64, 512], F32))
        rp_t2 = ph.enter_context(nc.sbuf_tensor("rp_t2", [64, 512], F32))
        kr = ph.enter_context(nc.sbuf_tensor("kr", [128, 2 * 512], BF16))
        vft = ph.enter_context(nc.sbuf_tensor("vft", [128, 8 * 512], BF16))

        def rope_evac(v, ps, dst_lo, dst_hi, tt):
            cs = slice(tt * 512, tt * 512 + 512)
            v.tensor_tensor(rp_t1[:, :], ps[0:64, 0:512], cosT[:, cs], OP.mult)
            v.tensor_tensor(rp_t2[:, :], ps[64:128, 0:512], sinT[:, cs], OP.mult)
            v.tensor_tensor(dst_lo, rp_t1[:, :], rp_t2[:, :], OP.subtract)
            v.tensor_tensor(rp_t1[:, :], ps[64:128, 0:512], cosT[:, cs], OP.mult)
            pbr = v.tensor_tensor(rp_t2[:, :], ps[0:64, 0:512], sinT[:, cs],
                                  OP.mult)
            fin = v.tensor_tensor(dst_hi, rp_t1[:, :], rp_t2[:, :], OP.add)
            return pbr, fin

        def evac_q(blk, b_mm):
            @blk.vector
            def _(v):
                v = EG(v)
                for fj in range(NH):
                    for tt in range(2):
                        u = fj * 2 + tt
                        v.wait_ge(cx.h("mm"), b_mm + u + 1)
                        col = fj * T + tt * 512
                        pbr, fin = rope_evac(v, pb[u % 2],
                                             qbuf[0:64, col:col + 512],
                                             qbuf[64:128, col:col + 512], tt)
                        fin.then_inc(cx.h("fr2"), 1)
            cx.inc("fr2", NH * 2)

        emit_proj(wq5, NH, ln1_bf, evac_q, 1)

        def evac_k(blk, b_mm):
            b_vec = cx.v("vec")
            b_st = {0: cx.v("sta"), 1: cx.v("stb")}
            @blk.vector
            def _(v):
                v = EG(v)
                for fj in range(NH):
                    for tt in range(2):
                        u = fj * 2 + tt
                        v.wait_ge(cx.h("mm"), b_mm + u + 1)
                        if u >= 2:
                            nm = "sta" if u % 2 == 0 else "stb"
                            v.wait_ge(cx.h(nm), b_st[u % 2] + 16 * ((u - 2) // 2 + 1))
                        kc_ = slice((u % 2) * 512, (u % 2) * 512 + 512)
                        pbr, fin = rope_evac(v, pb[u % 2], kr[0:64, kc_],
                                             kr[64:128, kc_], tt)
                        pbr.then_inc(cx.h("fr2"), 1)
                        fin.then_inc(cx.h("vec"), 1)
            cx.inc("fr2", NH * 2)
            cx.inc("vec", NH * 2)

            @blk.sync
            def _(e):
                for fj in range(NH):
                    for tt in range(2):
                        u = fj * 2 + tt
                        nm = "sta" if u % 2 == 0 else "stb"
                        e.wait_ge(cx.h("vec"), b_vec + u + 1)
                        e.dma_start(
                            out=k_send[fj, :, tt * 512:tt * 512 + 512],
                            in_=kr[:, (u % 2) * 512:(u % 2) * 512 + 512],
                        ).then_inc(cx.h(nm), 16)
            cx.inc("sta", 16 * NH)
            cx.inc("stb", 16 * NH)

        emit_proj(wk5, NH, ln1_bf, evac_k, 2)

        for dht in range(4):
            b_w = {0: cx.v("gwl"), 1: cx.v("gwl2")}
            b_fre = cx.v("fre")
            b_vec = cx.v("vec")
            with nc.Block() as blk:
                @blk.gpsimd
                def _(g, dht=dht):
                    for dc in range(DC):
                        nm = "gwl" if dc % 2 == 0 else "gwl2"
                        if dc >= 2:
                            g.wait_ge(cx.h("fre"), b_fre + (dc - 1))
                        g.dma_start(
                            out=WR[:, (dc % 2) * D:(dc % 2) * D + 512],
                            in_=wv5r[dc, :, dht * 512:dht * 512 + 512],
                        ).then_inc(cx.h(nm), 16)
                cx.inc("gwl", 16 * (DC // 2))
                cx.inc("gwl2", 16 * (DC // 2))

                @blk.tensor
                def _(t, dht=dht):
                    for dc in range(DC):
                        nm = "gwl" if dc % 2 == 0 else "gwl2"
                        t.wait_ge(cx.h(nm), b_w[dc % 2] + 16 * (dc // 2 + 1))
                        for ktc in range(8):
                            mi = t.matmul(pb[ktc][:, 0:512],
                                          ln1_bf[:, dc * T + ktc * 128:
                                                 dc * T + ktc * 128 + 128],
                                          WR[:, (dc % 2) * D:(dc % 2) * D + 512],
                                          start=(dc == 0), stop=(dc == DC - 1))
                        mi.then_inc(cx.h("fre"), 1)
                cx.inc("fre", DC)

                b_sv = {0: cx.v("sva"), 1: cx.v("svb")}

                @blk.vector
                def _(v, dht=dht):
                    v = EG(v)
                    v.wait_ge(cx.h("sva"), b_sv[0])
                    v.wait_ge(cx.h("svb"), b_sv[1])
                    v.wait_ge(cx.h("fre"), b_fre + DC)
                    for ktc in range(8):
                        v.tensor_copy(vft[:, ktc * 512:ktc * 512 + 512],
                                      pb[ktc][:, 0:512]).then_inc(cx.h("vec"), 1)
                cx.inc("vec", 8)

                @blk.sync
                def _(e, dht=dht):
                    for ktc in range(8):
                        p = ktc % 2
                        snm = "sva" if p == 0 else "svb"
                        e.wait_ge(cx.h("vec"), b_vec + ktc + 1)
                        for sub in range(4):
                            hc = dht * 4 + sub
                            e.dma_start(out=v_send[hc, ktc, :, :],
                                        in_=vft[:, ktc * 512 + sub * 128:
                                                ktc * 512 + sub * 128 + 128],
                                        ).then_inc(cx.h(snm), 16)
                cx.inc("sva", 16 * 16)
                cx.inc("svb", 16 * 16)

        if DBG == "qkv":
            dbg_dump(qbuf, NH * T)
            ph.close(); ph_q.close()
            _audit(nc, cx)
            return nc, es

        with nc.Block() as blk:
            @blk.gpsimd
            def _(g):
                for nm in ["sta", "stb", "sva", "svb"]:
                    g.wait_ge(cx.h(nm), cx.v(nm))
                g.collective_compute("AllGather", OP.bypass, replica_groups=RG,
                                     ins=[k_send.ap()], outs=[agk.ap()]
                                     ).then_inc(cx.h("cc"), 1)
                g.collective_compute("AllGather", OP.bypass, replica_groups=RG,
                                     ins=[v_send.ap()], outs=[agv.ap()]
                                     ).then_inc(cx.h("cc"), 1)
                cx.inc("cc", 2)

        ph.close()
        ph = ExitStack()
        ctx_bf = ph.enter_context(nc.sbuf_tensor("ctx_bf", [128, NH * T], BF16))
        kband = ph.enter_context(nc.sbuf_tensor("kband", [128, 2 * 1536], BF16))
        vband = ph.enter_context(nc.sbuf_tensor("vband", [128, 2 * 1536], BF16))
        negm = ph.enter_context(nc.sbuf_tensor("negm", [128, 12 * 512], BF16))
        ptile = ph.enter_context(nc.sbuf_tensor("ptile", [128, 3 * 512], BF16))
        ksq = ph.enter_context(nc.sbuf_tensor("ksq", [128, 1536], F32))
        mneg = ph.enter_context(nc.sbuf_tensor("mneg", [128, 1], F32))
        rcp_row = ph.enter_context(nc.sbuf_tensor("rcp_row", [1, 512], F32))

        with nc.Block() as blk:
            @blk.sync
            def _(e):
                e.dma_start(
                    out=negm[:, :].rearrange("p (m q j) -> p m q j", m=3, q=4),
                    in_=negm_in.ap().rearrange("m q p j -> p m q j"),
                ).then_inc(cx.h("kv"), 16)
                cx.inc("kv", 16)
                e.wait_ge(cx.h("cc"), cx.v("cc"))
                r1 = e.alloc_register()
                r2 = e.alloc_register()
                e.cc_rank_ld(r1, replica_groups=RG)
                e.reg_add(r2, r1, 6)
                e.reg_mod(r2, r2, 8)
                e.reg_add(r1, r1, 7)
                e.reg_mod(r1, r1, 8)
                s2 = e.snap(r2)
                s1 = e.snap(r1)
                e.dma_start(out=band_k[0, :, :, :],
                            in_=agk[bass.ds(s2, 1), :, :, :]).then_inc(cx.h("kv"), 16)
                e.dma_start(out=band_v[0, :, :, :, :],
                            in_=agv[bass.ds(s2, 1), :, :, :, :]).then_inc(cx.h("kv"), 16)
                e.dma_start(out=band_k[1, :, :, :],
                            in_=agk[bass.ds(s1, 1), :, :, :]).then_inc(cx.h("kv"), 16)
                e.dma_start(out=band_v[1, :, :, :, :],
                            in_=agv[bass.ds(s1, 1), :, :, :, :]).then_inc(cx.h("kv"), 16)
                cx.inc("kv", 64)

        kv0 = cx.v("kv")
        kva0 = cx.v("kva")
        kvb0 = cx.v("kvb")
        mm0 = cx.v("mm")
        mm20 = cx.v("mm2")
        act0 = cx.v("act")
        vec0 = cx.v("vec")
        NBH = 2 * NH

        def bh_bases(bh):
            buf = bh % 2
            return {
                "knm": "kva" if buf == 0 else "kvb",
                "kv": (kva0 if buf == 0 else kvb0) + 240 * (bh // 2),
                "mm": mm0 + 12 * bh,
                "mm2": mm20 + 27 * bh,
                "act": act0 + 15 * bh,
                "vec": vec0 + 6 * bh,
            }

        with nc.Block() as blk:
            @blk.sync
            def _(e):
                for bh in range(NBH):
                    b, h = bh // NH, bh % NH
                    buf = bh % 2
                    bb = bh_bases(bh)
                    if bh < 2:
                        e.wait_ge(cx.h("kv"), kv0)
                    else:
                        e.wait_ge(cx.h("mm2"), mm20 + 27 * (bh - 1))
                    for m in range(3):
                        if m < 2:
                            ksrc = band_k[m, h, :, b * 512:b * 512 + 512]
                        else:
                            ksrc = k_send[h, :, b * 512:b * 512 + 512]
                        e.dma_start(
                            out=kband[:, buf * 1536 + m * 512:
                                      buf * 1536 + m * 512 + 512],
                            in_=ksrc).then_inc(cx.h(bb["knm"]), 16)
                        for sub in range(4):
                            if m < 2:
                                vsrc = band_v[m, h, b * 4 + sub, :, :]
                            else:
                                vsrc = v_send[h, b * 4 + sub, :, :]
                            e.dma_start(
                                out=vband[:, buf * 1536 + m * 512 + sub * 128:
                                          buf * 1536 + m * 512 + sub * 128 + 128],
                                in_=vsrc).then_inc(cx.h(bb["knm"]), 16)
            cx.inc("kva", 240 * NH)
            cx.inc("kvb", 240 * NH)

            @blk.scalar
            def _(s):
                s = EG(s)
                for bh in range(NBH):
                    b, h = bh // NH, bh % NH
                    buf = bh % 2
                    bb = bh_bases(bh)
                    qcol = slice(h * T + b * 512, h * T + b * 512 + 512)
                    s.wait_ge(cx.h(bb["knm"]), bb["kv"] + 16 * 15)
                    if bh >= 1:
                        s.wait_ge(cx.h("vec"), bb["vec"] - 5)
                        s.wait_ge(cx.h("mm2"), bb["mm2"] - 26)
                    s.activation(ksq[:, :], kband[:, buf * 1536:buf * 1536 + 1536],
                                 AF.Square).then_inc(cx.h("act"), 1)
                    s.activation(stat[:, 512:1024], qbuf[:, qcol],
                                 AF.Square).then_inc(cx.h("act"), 1)
                    s.wait_ge(cx.h("vec"), bb["vec"] + 2)
                    s.activation(stat[0:1, 2:3], stat[0:1, 1:2], AF.Sqrt,
                                 scale=1.0 / 128.0).then_inc(cx.h("act"), 1)
                    s.wait_ge(cx.h("vec"), bb["vec"] + 4)
                    for j in range(12):
                        s.wait_ge(cx.h("mm"), bb["mm"] + j + 1)
                        if j >= 3:
                            s.wait_ge(cx.h("mm2"), bb["mm2"] + 2 + 2 * (j - 2))
                        s.activation(ptile[:, (j % 3) * 512:(j % 3) * 512 + 512],
                                     pb[j % 2][:, 0:512], AF.Exp,
                                     bias=mneg[:, 0:1],
                                     scale=ISQ128).then_inc(cx.h("act"), 1)
            cx.inc("act", 15 * NBH)

            @blk.vector
            def _(v):
                v = EG(v)
                for bh in range(NBH):
                    b, h = bh // NH, bh % NH
                    bb = bh_bases(bh)
                    qcol = slice(h * T + b * 512, h * T + b * 512 + 512)
                    v.wait_ge(cx.h("act"), bb["act"] + 1)
                    v.reduce_max(stat[:, 0:1], ksq[:, :],
                                 axis=AX).then_inc(cx.h("vec"), 1)
                    v.wait_ge(cx.h("mm2"), bb["mm2"] + 1)
                    v.reduce_max(stat[0:1, 0:1], pb[6][0:1, 0:512], axis=AX)
                    v.tensor_tensor(stat[0:1, 1:2], stat[0:1, 0:1],
                                    pb[7][0:1, 0:1],
                                    OP.mult).then_inc(cx.h("vec"), 1)
                    v.wait_ge(cx.h("act"), bb["act"] + 3)
                    v.tensor_scalar(stat[0:1, 3:4], stat[0:1, 2:3], -1.0, None,
                                    OP.mult).then_inc(cx.h("vec"), 1)
                    v.wait_ge(cx.h("mm2"), bb["mm2"] + 2)
                    v.wait_ge(cx.h("act"), bb["act"])
                    v.tensor_copy(mneg[:, 0:1],
                                  pb[7][:, 4:5]).then_inc(cx.h("vec"), 1)
                    v.wait_ge(cx.h("mm2"), bb["mm2"] + 2 + 24)
                    v.reciprocal(rcp_row[0:1, :],
                                 pb[3][0:1, 0:512]).then_inc(cx.h("vec"), 1)
                    v.wait_ge(cx.h("mm2"), bb["mm2"] + 27)
                    v.tensor_copy(bc_sc[:, 0:512], pb[5][:, 0:512])
                    v.tensor_tensor(ctx_bf[:, qcol], pb[2][:, 0:512],
                                    bc_sc[:, 0:512],
                                    OP.mult).then_inc(cx.h("vec"), 1)
            cx.inc("vec", 6 * NBH)

            @blk.tensor
            def _(t):
                for bh in range(NBH):
                    b, h = bh // NH, bh % NH
                    buf = bh % 2
                    bb = bh_bases(bh)
                    qcol = slice(h * T + b * 512, h * T + b * 512 + 512)
                    t.wait_ge(cx.h("vec"), bb["vec"] + 1)
                    t.matmul(pb[7][0:1, 0:1], ones_f[:, 0:1], stat[:, 0:1],
                             start=True, stop=True)
                    t.wait_ge(cx.h("act"), bb["act"] + 2)
                    t.matmul(pb[6][0:1, 0:512], ones_f[:, 0:1],
                             stat[:, 512:1024], start=True,
                             stop=True).then_inc(cx.h("mm2"), 1)
                    t.wait_ge(cx.h("vec"), bb["vec"] + 3)
                    t.matmul(pb[7][:, 4:5], ones_f[0:1, 0:128], stat[0:1, 3:4],
                             start=True, stop=True).then_inc(cx.h("mm2"), 1)
                    t.wait_ge(cx.h(bb["knm"]), bb["kv"] + 16 * 15)
                    t.wait_ge(cx.h("act"), bb["act"])
                    for j in range(13):
                        if j < 12:
                            if j >= 2:
                                t.wait_ge(cx.h("act"), bb["act"] + 3 + (j - 2) + 1)
                            t.matmul(pb[j % 2][:, 0:512],
                                     kband[:, buf * 1536 + j * 128:
                                           buf * 1536 + j * 128 + 128],
                                     qbuf[:, qcol], start=True, stop=False)
                            t.matmul(pb[j % 2][:, 0:512], ident_b[:, :],
                                     negm[:, j * 512:j * 512 + 512],
                                     start=False, stop=True).then_inc(cx.h("mm"), 1)
                        if j >= 1:
                            jj = j - 1
                            if jj == 0:
                                t.wait_ge(cx.h("vec"), bb["vec"])
                            t.wait_ge(cx.h("act"), bb["act"] + 3 + jj + 1)
                            t.matmul(pb[2][:, 0:512],
                                     vband[:, buf * 1536 + jj * 128:
                                           buf * 1536 + jj * 128 + 128],
                                     ptile[:, (jj % 3) * 512:(jj % 3) * 512 + 512],
                                     start=(jj == 0), stop=(jj == 11))
                            t.matmul(pb[3][0:1, 0:512], ones_b[:, 0:1],
                                     ptile[:, (jj % 3) * 512:(jj % 3) * 512 + 512],
                                     start=(jj == 0), stop=(jj == 11)
                                     ).then_inc(cx.h("mm2"), 2)
                    t.wait_ge(cx.h("vec"), bb["vec"] + 5)
                    t.matmul(pb[5][:, 0:512], ones_f[0:1, 0:128], rcp_row[0:1, :],
                             start=True, stop=True).then_inc(cx.h("mm2"), 1)
                cx.inc("mm", 12 * NBH)
                cx.inc("mm2", 27 * NBH)

        if DBG == "attn0":
            dbg_dump(ctx_bf, NH * T)
            ph.close(); ph_q.close()
            _audit(nc, cx)
            return nc, es

        def evac_resid(blk, b_mm):
            @blk.vector
            def _(v):
                v = EG(v)
                for fj in range(DC):
                    for tt in range(2):
                        u = fj * 2 + tt
                        v.wait_ge(cx.h("mm"), b_mm + u + 1)
                        col = fj * T + tt * 512
                        v.tensor_tensor(X[:, col:col + 512], pb[u % 2][:, 0:512],
                                        X[:, col:col + 512], OP.add
                                        ).then_inc(cx.h("fr2"), 1)
            cx.inc("fr2", DC * 2)

        emit_proj(wo5, DC, ctx_bf, evac_resid, 1)
        if DBG == "x1":
            dbg_dump(X)
            ph.close(); ph_q.close()
            _audit(nc, cx)
            return nc, es

        ph.close()
        ph_q.close()
        ph = ExitStack()
        ln2 = ph.enter_context(nc.sbuf_tensor("ln2", [128, DC * T], F32))
        xq_bf = ph.enter_context(nc.sbuf_tensor("xq_bf", [128, DC * T], BF16))

        emit_ln(ln2)
        if DBG == "ln2":
            dbg_dump(ln2)
            ph.close()
            _audit(nc, cx)
            return nc, es

        amax = ph.enter_context(nc.sbuf_tensor("amax", [128, 1], F32))
        b_vec = cx.v("vec")
        b_mm2 = cx.v("mm2")
        with nc.Block() as blk:
            @blk.vector
            def _(v):
                v = EG(v)
                for dc in range(DC):
                    v.tensor_reduce(stat[:, 0:1], ln2[:, dc * T:dc * T + T],
                                    axis=AX, op=OP.max,
                                    apply_absolute_value=True)
                    if dc == 0:
                        v.tensor_copy(amax[:, :], stat[:, 0:1])
                    else:
                        v.tensor_tensor(amax[:, :], amax[:, :], stat[:, 0:1],
                                        OP.max)
                v.last.then_inc(cx.h("vec"), 1)
                v.wait_ge(cx.h("mm2"), b_mm2 + 1)
                v.reduce_max(stat[0:1, 0:1], pb[6][0:1, 0:128], axis=AX)
                v.last.then_inc(cx.h("vec"), 1)
            cx.inc("vec", 2)

            @blk.tensor
            def _(t):
                t.wait_ge(cx.h("vec"), b_vec + 1)
                t.transpose(pb[6][0:1, 0:128], amax[:, 0:1],
                            ident_f[:, :]).then_inc(cx.h("mm2"), 1)
            cx.inc("mm2", 1)

            @blk.gpsimd
            def _(g):
                g.wait_ge(cx.h("vec"), b_vec + 2)
                g.dma_start(out=ar_in[:, :], in_=stat[0:1, 0:1]).then_inc(cx.h("gld"), 16)
                g.wait_ge(cx.h("gld"), cx.v("gld") + 16)
                g.collective_compute("AllReduce", OP.max, replica_groups=RG,
                                     ins=[ar_in.ap()], outs=[ar_out.ap()]
                                     ).then_inc(cx.h("cc"), 1)
                g.wait_ge(cx.h("cc"), cx.v("cc") + 1)
                g.dma_start(out=stat[0:1, 4:5], in_=ar_out[:, :]).then_inc(cx.h("gld"), 16)
                cx.inc("gld", 32)
                cx.inc("cc", 1)

        b_gld = cx.v("gld")
        b_vec = cx.v("vec")
        b_mm2 = cx.v("mm2")
        with nc.Block() as blk:
            @blk.vector
            def _(v):
                v = EG(v)
                v.wait_ge(cx.h("gld"), b_gld)
                v.tensor_scalar(stat[0:1, 5:6], stat[0:1, 4:5], 1.0 / 448.0,
                                1e-7, OP.mult, OP.max)
                v.reciprocal(stat[0:1, 6:7], stat[0:1, 5:6])
                v.last.then_inc(cx.h("vec"), 1)
                v.wait_ge(cx.h("mm2"), b_mm2 + 1)
                v.tensor_copy(gbias[:, 18:19], pb[6][:, 0:1])
                v.tensor_copy(gbias[:, 19:20], pb[6][:, 1:2])
                v.last.then_inc(cx.h("vec"), 1)
            cx.inc("vec", 2)

            @blk.tensor
            def _(t):
                t.wait_ge(cx.h("vec"), b_vec + 1)
                t.matmul(pb[6][:, 0:1], ones_f[0:1, 0:128], stat[0:1, 5:6],
                         start=True, stop=True)
                t.matmul(pb[6][:, 1:2], ones_f[0:1, 0:128], stat[0:1, 6:7],
                         start=True, stop=True).then_inc(cx.h("mm2"), 1)
            cx.inc("mm2", 1)

        qs1 = ph.enter_context(nc.sbuf_tensor("qs1", [128, T], F32))
        qs2 = ph.enter_context(nc.sbuf_tensor("qs2", [128, T], F32))
        qs3 = ph.enter_context(nc.sbuf_tensor("qs3", [128, T], F32))
        qsi = ph.enter_context(nc.sbuf_tensor("qsi", [128, T], I32))

        for dc in range(DC):
            b_act = cx.v("act")
            b_vec = cx.v("vec")
            with nc.Block() as blk:
                @blk.scalar
                def _(s, dc=dc, b_act=b_act, b_vec=b_vec):
                    s = EG(s)
                    s.activation(qs1[:, :], ln2[:, dc * T:dc * T + T], AF.Abs,
                                 scale=gbias[:, 19:20]).then_inc(cx.h("act"), 1)
                    s.wait_ge(cx.h("vec"), b_vec + 1)
                    s.activation(qs2[:, :], qs1[:, :], AF.Ln).then_inc(cx.h("act"), 1)
                    s.wait_ge(cx.h("vec"), b_vec + 2)
                    s.activation(qs2[:, :], qs3[:, :], AF.Exp,
                                 scale=LN2C).then_inc(cx.h("act"), 1)
                    s.activation(qs3[:, :], qs3[:, :], AF.Exp,
                                 scale=-LN2C).then_inc(cx.h("act"), 1)
                    s.wait_ge(cx.h("vec"), b_vec + 3)
                    s.activation(qs3[:, :], ln2[:, dc * T:dc * T + T],
                                 AF.Sign).then_inc(cx.h("act"), 1)
                cx.inc("act", 5)

                @blk.vector
                def _(v, dc=dc, b_act=b_act, b_vec=b_vec):
                    v = EG(v)
                    v.wait_ge(cx.h("act"), b_act + 1)
                    v.tensor_scalar(qs1[:, :], qs1[:, :], 1e-30, None, OP.max)
                    v.last.then_inc(cx.h("vec"), 1)
                    v.wait_ge(cx.h("act"), b_act + 2)
                    v.tensor_scalar(qs2[:, :], qs2[:, :], INV_LN2, 131.0,
                                    OP.mult, OP.add)
                    v.tensor_copy(qsi[:, :], qs2[:, :])
                    v.tensor_copy(qs3[:, :], qsi[:, :])
                    v.tensor_tensor(stat[:, 0:512], qs3[:, 0:512],
                                    qs2[:, 0:512], OP.is_gt)
                    v.tensor_tensor(stat[:, 512:1024], qs3[:, 512:1024],
                                    qs2[:, 512:1024], OP.is_gt)
                    v.tensor_tensor(qs3[:, 0:512], qs3[:, 0:512],
                                    stat[:, 0:512], OP.subtract)
                    v.tensor_tensor(qs3[:, 512:1024], qs3[:, 512:1024],
                                    stat[:, 512:1024], OP.subtract)
                    v.tensor_scalar(qs3[:, :], qs3[:, :], -134.0, -9.0,
                                    OP.add, OP.max)
                    v.last.then_inc(cx.h("vec"), 1)
                    v.wait_ge(cx.h("act"), b_act + 4)
                    v.tensor_tensor(stat[:, 0:1024], qs1[:, :],
                                    qs3[:, :], OP.mult)
                    v.last.then_inc(cx.h("vec"), 1)
                    v.tensor_copy(qsi[:, :], stat[:, 0:1024])
                    v.tensor_copy(stat[:, 0:1024], qsi[:, :])
                    v.tensor_tensor(stat[:, 0:1024], stat[:, 0:1024],
                                    qs2[:, :], OP.mult)
                    v.tensor_tensor(qs1[:, :], qs1[:, :], stat[:, 0:1024],
                                    OP.subtract)
                    v.tensor_scalar(qs2[:, :], qs2[:, :], 0.5, None, OP.mult)
                    v.tensor_tensor(qs1[:, :], qs1[:, :], qs2[:, :], OP.is_gt)
                    v.tensor_scalar(qs2[:, :], qs2[:, :], 2.0, None, OP.mult)
                    v.tensor_tensor(qs1[:, :], qs1[:, :], qs2[:, :], OP.mult)
                    v.tensor_tensor(stat[:, 0:1024], stat[:, 0:1024],
                                    qs1[:, :], OP.add)
                    v.wait_ge(cx.h("act"), b_act + 5)
                    v.tensor_tensor(xq_bf[:, dc * T:dc * T + T],
                                    stat[:, 0:1024], qs3[:, :],
                                    OP.mult).then_inc(cx.h("vec"), 1)
                cx.inc("vec", 4)

        if DBG == "xq":
            dbg_dump(xq_bf)
            ph.close()
            _audit(nc, cx)
            return nc, es

        b_vec = cx.v("vec")
        b_mm2 = cx.v("mm2")
        with nc.Block() as blk:
            @blk.tensor
            def _(t):
                t.matmul(pb[6][:, 3:4], ones_f[0:1, 0:128], gbias[0:1, 16:17],
                         start=True, stop=True).then_inc(cx.h("mm2"), 1)
            cx.inc("mm2", 1)

            @blk.vector
            def _(v):
                v = EG(v)
                v.wait_ge(cx.h("mm2"), b_mm2 + 1)
                v.tensor_copy(gbias[:, 20:21], pb[6][:, 3:4]).then_inc(cx.h("vec"), 1)
            cx.inc("vec", 1)

        def evac_ffn(blk, b_mm):
            b_act = cx.v("act")
            @blk.scalar
            def _(s):
                s = EG(s)
                for fj in range(DC):
                    for tt in range(2):
                        u = fj * 2 + tt
                        s.wait_ge(cx.h("mm"), b_mm + u + 1)
                        if u >= 1:
                            s.wait_ge(cx.h("fr2"), cx_fr2_base + u)
                        s.activation(bc_sc[:, 0:512], pb[u % 2][:, 0:512],
                                     AF.Identity, bias=gbias[:, fj:fj + 1],
                                     scale=gbias[:, 20:21]).then_inc(cx.h("act"), 1)
            cx.inc("act", DC * 2)

            @blk.vector
            def _(v):
                v = EG(v)
                for fj in range(DC):
                    for tt in range(2):
                        u = fj * 2 + tt
                        v.wait_ge(cx.h("act"), b_act + u + 1)
                        col = fj * T + tt * 512
                        v.tensor_tensor(X[:, col:col + 512], bc_sc[:, 0:512],
                                        X[:, col:col + 512], OP.add
                                        ).then_inc(cx.h("fr2"), 1)
            cx.inc("fr2", DC * 2)

        cx_fr2_base = cx.v("fr2")
        emit_proj(wf5, DC, xq_bf, evac_ffn, 1)
        if DBG == "x2":
            dbg_dump(X)
            ph.close()
            _audit(nc, cx)
            return nc, es

        ph.close()
        ph = ExitStack()
        qbuf1 = ph.enter_context(nc.sbuf_tensor("qbuf1", [128, NH * 512], BF16))
        kbuf1 = ph.enter_context(nc.sbuf_tensor("kbuf1", [128, NH * T], BF16))
        vbuf1 = ph.enter_context(nc.sbuf_tensor("vbuf1", [128, NH * T], BF16))
        rp1 = ph.enter_context(nc.sbuf_tensor("rp1", [64, 2 * 512], F32))
        ph3 = ExitStack()
        ln3_bf = ph3.enter_context(nc.sbuf_tensor("ln3_bf", [128, DC * T], BF16))

        emit_ln(ln3_bf)
        if DBG == "ln3":
            dbg_dump(ln3_bf)
            ph3.close(); ph.close()
            _audit(nc, cx)
            return nc, es

        units = []
        upref = {}
        for fj in range(48):
            upref[fj] = len(units)
            tts = (1,) if fj < NH else (0, 1)
            for tt in tts:
                units.append((fj, tt))
        b_w = {0: cx.v("gwl"), 1: cx.v("gwl2")}
        b_mm = cx.v("mm")
        b_fr2 = cx.v("fr2")
        with nc.Block() as blk:
            @blk.gpsimd
            def _(g):
                for fj in range(48):
                    nm = "gwl" if fj % 2 == 0 else "gwl2"
                    if fj >= 2:
                        g.wait_ge(cx.h("mm"), b_mm + upref[fj - 1])
                    g.dma_start(out=WR[:, (fj % 2) * D:(fj % 2) * D + D],
                                in_=wqkv5[fj, :, :, :]).then_inc(cx.h(nm), 16)
            cx.inc("gwl", 16 * 24)
            cx.inc("gwl2", 16 * 24)

            @blk.tensor
            def _(t):
                for u, (fj, tt) in enumerate(units):
                    nm = "gwl" if fj % 2 == 0 else "gwl2"
                    t.wait_ge(cx.h(nm), b_w[fj % 2] + 16 * (fj // 2 + 1))
                    if u >= 2:
                        t.wait_ge(cx.h("fr2"), b_fr2 + (u - 1))
                    for c in range(DC):
                        mi = t.matmul(pb[u % 2][:, 0:512],
                                      WR[:, (fj % 2) * D + c * 128:
                                         (fj % 2) * D + c * 128 + 128],
                                      ln3_bf[:, c * T + tt * 512:
                                             c * T + tt * 512 + 512],
                                      start=(c == 0), stop=(c == DC - 1))
                    mi.then_inc(cx.h("mm"), 1)
            cx.inc("mm", len(units))

            @blk.vector
            def _(v):
                v = EG(v)

                def rope1(ps, dst_lo, dst_hi):
                    c1 = cos1[0:64, 0:1]
                    s1_ = sin1[0:64, 0:1]
                    v.tensor_scalar(rp1[:, 0:512], ps[0:64, 0:512], c1, None,
                                    OP.mult)
                    v.tensor_scalar(rp1[:, 512:1024], ps[64:128, 0:512], s1_,
                                    None, OP.mult)
                    v.tensor_tensor(dst_lo, rp1[:, 0:512], rp1[:, 512:1024],
                                    OP.subtract)
                    v.tensor_scalar(rp1[:, 0:512], ps[64:128, 0:512], c1, None,
                                    OP.mult)
                    v.tensor_scalar(rp1[:, 512:1024], ps[0:64, 0:512], s1_,
                                    None, OP.mult)
                    return v.tensor_tensor(dst_hi, rp1[:, 0:512],
                                           rp1[:, 512:1024], OP.add)

                for u, (fj, tt) in enumerate(units):
                    v.wait_ge(cx.h("mm"), b_mm + u + 1)
                    ps = pb[u % 2]
                    if fj < NH:
                        col = fj * 512
                        fin = rope1(ps, qbuf1[0:64, col:col + 512],
                                    qbuf1[64:128, col:col + 512])
                    elif fj < 2 * NH:
                        h = fj - NH
                        col = h * T + tt * 512
                        if tt == 0:
                            fin = v.tensor_copy(kbuf1[:, col:col + 512],
                                                ps[:, 0:512])
                        else:
                            fin = rope1(ps, kbuf1[0:64, col:col + 512],
                                        kbuf1[64:128, col:col + 512])
                    else:
                        h = fj - 2 * NH
                        col = h * T + tt * 512
                        fin = v.tensor_copy(vbuf1[:, col:col + 512],
                                            ps[:, 0:512])
                    fin.then_inc(cx.h("fr2"), 1)
            cx.inc("fr2", len(units))

        ph3.close()
        if DBG == "qkv1":
            dbg_dump(kbuf1, NH * T)
            ph.close()
            _audit(nc, cx)
            return nc, es

        zb = ph.enter_context(nc.sbuf_tensor("zb", [128, 2 * 1024], BF16))
        pp = ph.enter_context(nc.sbuf_tensor("pp", [1, 2 * 512], F32))
        sc0 = ph.enter_context(nc.sbuf_tensor("sc0", [1, 3 * 512], F32))
        ct1 = ph.enter_context(nc.sbuf_tensor("ct1", [128, 2 * 512], F32))

        b_mm2 = cx.v("mm2")
        b_vec = cx.v("vec")
        b_act = cx.v("act")
        with nc.Block() as blk:
            @blk.vector
            def _(v):
                v = EG(v)
                for h in range(NH):
                    if h >= 2:
                        v.wait_ge(cx.h("mm2"), b_mm2 + 4 * (h - 2) + 2)
                    v.tensor_tensor(zb[:, (h % 2) * 1024:(h % 2) * 1024 + 512],
                                    qbuf1[:, h * 512:h * 512 + 512],
                                    kbuf1[:, h * T:h * T + 512], OP.mult)
                    v.tensor_tensor(zb[:, (h % 2) * 1024 + 512:
                                       (h % 2) * 1024 + 1024],
                                    qbuf1[:, h * 512:h * 512 + 512],
                                    kbuf1[:, h * T + 512:h * T + 1024],
                                    OP.mult).then_inc(cx.h("vec"), 1)
                    v.wait_ge(cx.h("mm2"), b_mm2 + 4 * h + 2)
                    v.tensor_copy(sc0[0:1, 0:512],
                                  pb[4][0:1, 0:512])
                    v.tensor_tensor(sc0[0:1, 512:1024], sc0[0:1, 0:512],
                                    pb[5][0:1, 0:512], OP.max)
                    v.tensor_tensor(sc0[0:1, 1024:1536], sc0[0:1, 0:512],
                                    sc0[0:1, 512:1024], OP.subtract)
                    v.tensor_tensor(sc0[0:1, 0:512], pb[5][0:1, 0:512],
                                    sc0[0:1, 512:1024],
                                    OP.subtract).then_inc(cx.h("vec"), 1)
                    v.wait_ge(cx.h("act"), b_act + 2 * h + 2)
                    v.tensor_tensor(sc0[0:1, 512:1024], pp[0:1, 0:512],
                                    pp[0:1, 512:1024], OP.add)
                    v.reciprocal(sc0[0:1, 512:1024], sc0[0:1, 512:1024])
                    v.tensor_tensor(pp[0:1, 0:512], pp[0:1, 0:512],
                                    sc0[0:1, 512:1024], OP.mult)
                    v.tensor_tensor(pp[0:1, 512:1024], pp[0:1, 512:1024],
                                    sc0[0:1, 512:1024],
                                    OP.mult).then_inc(cx.h("vec"), 1)
                    v.wait_ge(cx.h("mm2"), b_mm2 + 4 * h + 4)
                    v.tensor_tensor(ct1[:, 0:512], pb[6][:, 0:512],
                                    vbuf1[:, h * T:h * T + 512], OP.mult)
                    v.tensor_tensor(ct1[:, 512:1024], pb[7][:, 0:512],
                                    vbuf1[:, h * T + 512:h * T + 1024],
                                    OP.mult)
                    v.tensor_tensor(vbuf1[:, h * T + 512:h * T + 1024],
                                    ct1[:, 0:512], ct1[:, 512:1024],
                                    OP.add).then_inc(cx.h("vec"), 1)
            cx.inc("vec", 4 * NH)

            @blk.tensor
            def _(t):
                for h in range(NH):
                    t.wait_ge(cx.h("vec"), b_vec + 4 * h + 1)
                    t.matmul(pb[4][0:16, 0:512],
                             onehot[:, h * NH:(h + 1) * NH],
                             zb[:, (h % 2) * 1024:(h % 2) * 1024 + 512],
                             start=True, stop=True).then_inc(cx.h("mm2"), 1)
                    t.matmul(pb[5][0:16, 0:512],
                             onehot[:, h * NH:(h + 1) * NH],
                             zb[:, (h % 2) * 1024 + 512:(h % 2) * 1024 + 1024],
                             start=True, stop=True).then_inc(cx.h("mm2"), 1)
                    t.wait_ge(cx.h("vec"), b_vec + 4 * h + 3)
                    t.matmul(pb[6][:, 0:512], ones_f[0:1, 0:128],
                             pp[0:1, 0:512], start=True,
                             stop=True).then_inc(cx.h("mm2"), 1)
                    t.matmul(pb[7][:, 0:512], ones_f[0:1, 0:128],
                             pp[0:1, 512:1024], start=True,
                             stop=True).then_inc(cx.h("mm2"), 1)
            cx.inc("mm2", 4 * NH)

            @blk.scalar
            def _(s):
                s = EG(s)
                for h in range(NH):
                    s.wait_ge(cx.h("vec"), b_vec + 4 * h + 2)
                    s.activation(pp[0:1, 0:512], sc0[0:1, 1024:1536], AF.Exp,
                                 scale=ISQ128).then_inc(cx.h("act"), 1)
                    s.activation(pp[0:1, 512:1024], sc0[0:1, 0:512], AF.Exp,
                                 scale=ISQ128).then_inc(cx.h("act"), 1)
            cx.inc("act", 2 * NH)

        if DBG == "attn1":
            dbg_dump(vbuf1, NH * T)
            ph.close()
            _audit(nc, cx)
            return nc, es

        emit_proj(wd5, DC, vbuf1, evac_resid, 1)
        if DBG == "x3":
            dbg_dump(X)
            ph.close()
            _audit(nc, cx)
            return nc, es

        ph.close()
        ph = ExitStack()
        ln4_bf = ph.enter_context(nc.sbuf_tensor("ln4_bf", [128, DC * T], BF16))
        comb = ph.enter_context(nc.sbuf_tensor("compb", [128, 8 * 8], F32))
        combT = ph.enter_context(nc.sbuf_tensor("combT", [8, T], BF16))
        ph4 = ExitStack()
        ln4r = ph4.enter_context(nc.sbuf_tensor("ln4r", [128, 2 * 512], F32))
        rsd4 = ph4.enter_context(nc.sbuf_tensor("rsd4", [1, T], F32))
        gsc = ph4.enter_context(nc.sbuf_tensor("gsc", [8, T], F32))
        gw_sb = ph4.enter_context(nc.sbuf_tensor("gw_sb", [128, DC * 8], F32))
        gb_sb = ph4.enter_context(nc.sbuf_tensor("gb_sb", [1, 24], F32))

        b_wld = cx.v("wld")
        with nc.Block() as blk:
            @blk.sync
            def _(e):
                e.dma_start(out=gw_sb[:, :].rearrange("p (c e) -> p c e", e=8),
                            in_=gw_l.ap().rearrange("c p e -> p c e")
                            ).then_inc(cx.h("wld"), 16)
                e.dma_start(out=gb_sb[0:1, 0:8], in_=gate_b_in[:, :]
                            ).then_inc(cx.h("wld"), 16)
                e.dma_start(out=gb_sb[0:1, 8:16], in_=gate_bias_in[:, :]
                            ).then_inc(cx.h("wld"), 16)
                cx.inc("wld", 48)

            @blk.vector
            def _(v):
                v = EG(v)
                v.wait_ge(cx.h("wld"), b_wld + 48)
                v.tensor_tensor(gb_sb[0:1, 16:24], gb_sb[0:1, 0:8],
                                gb_sb[0:1, 8:16], OP.add).then_inc(cx.h("vec"), 1)
            cx.inc("vec", 1)

        for tt in range(2):
            b_mm = cx.v("mm")
            b_act = cx.v("act")
            b_vec = cx.v("vec")
            b_mm2 = cx.v("mm2")
            with nc.Block() as blk:
                @blk.vector
                def _(v, tt=tt):
                    v = EG(v)
                    for dc in range(DC):
                        if dc >= 2:
                            v.wait_ge(cx.h("mm"), b_mm + (dc - 1))
                        v.tensor_copy(
                            xb_ring[:, (dc % 2) * 512:(dc % 2) * 512 + 512],
                            X[:, dc * T + tt * 512: dc * T + tt * 512 + 512]
                        ).then_inc(cx.h("vec"), 1)
                    v.wait_ge(cx.h("mm"), b_mm + DC)
                    v.tensor_scalar(row_sc[0:1, 0:512], pb[6][0:1, 0:512],
                                    1.0 / 2048.0, None, OP.mult)
                    v.tensor_scalar(row_sc[0:1, 512:1024], pb[7][0:1, 0:512],
                                    1.0 / 2048.0, None, OP.mult)
                    v.tensor_tensor(row_sc[0:1, 1024:1536],
                                    row_sc[0:1, 0:512], row_sc[0:1, 0:512],
                                    OP.mult)
                    v.tensor_tensor(row_sc[0:1, 1024:1536],
                                    row_sc[0:1, 512:1024],
                                    row_sc[0:1, 1024:1536],
                                    OP.subtract).then_inc(cx.h("vec"), 1)
                    v.wait_ge(cx.h("act"), b_act + DC + 1)
                    v.reciprocal(row_sc[0:1, 512:1024],
                                 row_sc[0:1, 512:1024]).then_inc(cx.h("vec"), 1)
                    v.tensor_copy(rsd4[0:1, tt * 512:tt * 512 + 512],
                                  row_sc[0:1, 512:1024])
                    v.wait_ge(cx.h("mm"), b_mm + DC + 1)
                    v.tensor_copy(bc_sc[:, 0:512], pb[4][:, 0:512])
                    for dc in range(DC):
                        col = dc * T + tt * 512
                        if dc >= 2:
                            v.wait_ge(cx.h("mm2"), b_mm2 + (dc - 1))
                        v.tensor_tensor(stat[:, 0:512], X[:, col:col + 512],
                                        pb[5][:, 0:512], OP.subtract)
                        v.tensor_tensor(ln4r[:, (dc % 2) * 512:
                                             (dc % 2) * 512 + 512],
                                        stat[:, 0:512], bc_sc[:, 0:512],
                                        OP.mult)
                        v.tensor_copy(ln4_bf[:, col:col + 512],
                                      ln4r[:, (dc % 2) * 512:
                                           (dc % 2) * 512 + 512]
                                      ).then_inc(cx.h("vec"), 1)
                cx.inc("vec", 2 * DC + 2)

                @blk.scalar
                def _(s, tt=tt):
                    s = EG(s)
                    for dc in range(DC):
                        s.wait_ge(cx.h("vec"), b_vec + dc + 1)
                        if dc >= 2:
                            s.wait_ge(cx.h("mm"), b_mm + (dc - 1))
                        s.activation(
                            sq_ring[:, (dc % 2) * 512:(dc % 2) * 512 + 512],
                            xb_ring[:, (dc % 2) * 512:(dc % 2) * 512 + 512],
                            AF.Square).then_inc(cx.h("act"), 1)
                    s.wait_ge(cx.h("vec"), b_vec + DC + 1)
                    s.activation(row_sc[0:1, 512:1024],
                                 row_sc[0:1, 1024:1536], AF.Sqrt,
                                 bias=gbias[0:1, 17:18]).then_inc(cx.h("act"), 1)
                cx.inc("act", DC + 1)

                @blk.tensor
                def _(t, tt=tt):
                    for dc in range(DC):
                        t.wait_ge(cx.h("act"), b_act + dc + 1)
                        t.matmul(pb[6][0:1, 0:512], ones_b[:, 0:1],
                                 xb_ring[:, (dc % 2) * 512:(dc % 2) * 512 + 512],
                                 start=(dc == 0), stop=(dc == DC - 1))
                        t.matmul(pb[7][0:1, 0:512], ones_b[:, 0:1],
                                 sq_ring[:, (dc % 2) * 512:(dc % 2) * 512 + 512],
                                 start=(dc == 0), stop=(dc == DC - 1)
                                 ).then_inc(cx.h("mm"), 1)
                    t.wait_ge(cx.h("vec"), b_vec + DC + 2)
                    t.matmul(pb[5][:, 0:512], ones_f[0:1, 0:128],
                             row_sc[0:1, 0:512], start=True, stop=True)
                    t.matmul(pb[4][:, 0:512], ones_f[0:1, 0:128],
                             row_sc[0:1, 512:1024], start=True,
                             stop=True).then_inc(cx.h("mm"), 1)
                    for dc in range(DC):
                        t.wait_ge(cx.h("vec"), b_vec + DC + 3 + dc)
                        t.matmul(pb[2 + tt][0:8, 0:512],
                                 gw_sb[:, dc * 8:(dc + 1) * 8],
                                 ln4r[:, (dc % 2) * 512:(dc % 2) * 512 + 512],
                                 start=(dc == 0), stop=False
                                 ).then_inc(cx.h("mm2"), 1)
                    t.matmul(pb[2 + tt][0:8, 0:512], gb_sb[0:1, 16:24],
                             ones_f[0:1, 0:512], start=False,
                             stop=True).then_inc(cx.h("mm2"), 1)
                cx.inc("mm", DC + 1)
                cx.inc("mm2", DC + 1)

        b_mm2 = cx.v("mm2")
        b_vec = cx.v("vec")
        b_act = cx.v("act")
        with nc.Block() as blk:
            @blk.vector
            def _(v):
                v = EG(v)
                for tt in range(2):
                    v.tensor_copy(gsc[0:8, tt * 512:tt * 512 + 512],
                                  pb[2 + tt][0:8, 0:512])
                v.last.then_inc(cx.h("vec"), 1)
                v.wait_ge(cx.h("mm2"), b_mm2 + 2)
                for tt in range(2):
                    v.tensor_tensor(gsc[0:8, tt * 512:tt * 512 + 512],
                                    gsc[0:8, tt * 512:tt * 512 + 512],
                                    pb[2 + tt][0:8, 0:512], OP.mult)
                v.last.then_inc(cx.h("vec"), 1)
                for tc in range(8):
                    v.wait_ge(cx.h("mm2"), b_mm2 + 2 + tc + 1)
                    g0 = stat[:, 0:8]
                    v.tensor_copy(g0, pb[tc % 2][0:128, 0:8])
                    v.tensor_reduce(stat[:, 8:9], g0, axis=AX, op=OP.max)
                    v.tensor_scalar(stat[:, 16:24], g0, stat[:, 8:9], None,
                                    OP.is_ge)
                    v.scalar_tensor_tensor(stat[:, 24:32], stat[:, 16:24],
                                           NEG, g0, OP.mult, OP.add)
                    v.tensor_reduce(stat[:, 9:10], stat[:, 24:32], axis=AX,
                                    op=OP.max)
                    v.tensor_scalar(stat[:, 32:40], stat[:, 24:32],
                                    stat[:, 9:10], None, OP.is_ge)
                    v.tensor_tensor(stat[:, 10:11], stat[:, 9:10],
                                    stat[:, 8:9],
                                    OP.subtract).then_inc(cx.h("vec"), 1)
                    v.wait_ge(cx.h("act"), b_act + tc + 1)
                    v.tensor_scalar(stat[:, 12:13], stat[:, 11:12], -1.0, 1.0,
                                    OP.mult, OP.add)
                    v.tensor_scalar(stat[:, 32:40], stat[:, 32:40],
                                    stat[:, 11:12], None, OP.mult)
                    v.scalar_tensor_tensor(comb[:, tc * 8:(tc + 1) * 8],
                                           stat[:, 16:24], stat[:, 12:13],
                                           stat[:, 32:40], OP.mult,
                                           OP.add).then_inc(cx.h("vec"), 1)
                for tc in range(8):
                    v.wait_ge(cx.h("mm2"), b_mm2 + 10 + tc + 1)
                    v.tensor_copy(combT[0:8, tc * 128:(tc + 1) * 128],
                                  pb[4 + tc % 2][0:8, 0:128]
                                  ).then_inc(cx.h("vec"), 1)
            cx.inc("vec", 2 + 16 + 8)

            @blk.tensor
            def _(t):
                t.wait_ge(cx.h("vec"), b_vec + 1)
                for tt in range(2):
                    t.matmul(pb[2 + tt][0:8, 0:512], ones_f[0:1, 0:8],
                             rsd4[0:1, tt * 512:tt * 512 + 512],
                             start=True, stop=True).then_inc(cx.h("mm2"), 1)
                for tc in range(8):
                    t.wait_ge(cx.h("vec"), b_vec + 2)
                    if tc >= 2:
                        t.wait_ge(cx.h("vec"), b_vec + 2 + 2 * (tc - 2) + 1)
                    tti = tc * 128 // 512
                    t.transpose(pb[tc % 2][0:128, 0:8],
                                gsc[0:8, tc * 128:(tc + 1) * 128],
                                ident_f[0:8, 0:8]).then_inc(cx.h("mm2"), 1)
                for tc in range(8):
                    t.wait_ge(cx.h("vec"), b_vec + 2 + 2 * tc + 2)
                    if tc >= 2:
                        t.wait_ge(cx.h("vec"), b_vec + 18 + (tc - 2) + 1)
                    t.transpose(pb[4 + tc % 2][0:8, 0:128],
                                comb[:, tc * 8:(tc + 1) * 8],
                                ident_f[:, :]).then_inc(cx.h("mm2"), 1)
            cx.inc("mm2", 18)

            @blk.scalar
            def _(s):
                s = EG(s)
                for tc in range(8):
                    s.wait_ge(cx.h("vec"), b_vec + 2 + 2 * tc + 1)
                    s.activation(stat[:, 11:12], stat[:, 10:11],
                                 AF.Sigmoid).then_inc(cx.h("act"), 1)
            cx.inc("act", 8)

        if DBG == "gates":
            dbg_dump(comb, 64)
            ph.close()
            _audit(nc, cx)
            return nc, es

        ph4.close()
        h_sb = ph.enter_context(nc.sbuf_tensor("h_sb", [128, 11 * T], BF16))
        w2_sb = ph.enter_context(nc.sbuf_tensor("w2_sb", [128, 11 * D], BF16))
        b1_sb = ph.enter_context(nc.sbuf_tensor("b1_sb", [128, 88], F32))
        b2_sb = ph.enter_context(nc.sbuf_tensor("b2_sb", [8, D], BF16))

        b_mm2 = cx.v("mm2")
        b_vec = cx.v("vec")
        b_wld = cx.v("wld")
        b_gld = cx.v("gld")
        with nc.Block() as blk:
            @blk.sync
            def _(e):
                e.dma_start(out=b1_sb[:, :], in_=b1r[:, :]
                            ).then_inc(cx.h("wld"), 16)
                cx.inc("wld", 16)
            wld_b1 = cx.v("wld")

            @blk.gpsimd
            def _(g):
                g.dma_start(out=b2_sb[:, :], in_=b2_in[:, :]
                            ).then_inc(cx.h("gld"), 16)
                cx.inc("gld", 16)

            @blk.tensor
            def _(t):
                for i in range(128):
                    dc, tc = i // 8, i % 8
                    if i >= 4:
                        t.wait_ge(cx.h("vec"), b_vec + (i - 4) + 1)
                    t.transpose(pb[i % 4][:, 0:128],
                                X[:, dc * T + tc * 128:dc * T + tc * 128 + 128],
                                ident_f[:, :]).then_inc(cx.h("mm2"), 1)
                t.wait_ge(cx.h("gld"), b_gld + 16)
                for u in range(32):
                    tc, dj = u // 4, u % 4
                    if u >= 2:
                        t.wait_ge(cx.h("vec"), b_vec + 128 + (u - 1))
                    t.matmul(pb[4 + u % 2][:, 0:512],
                             combT[0:8, tc * 128:(tc + 1) * 128],
                             b2_sb[:, dj * 512:dj * 512 + 512],
                             start=True, stop=True).then_inc(cx.h("mm2"), 1)
            cx.inc("mm2", 128 + 32)

            @blk.vector
            def _(v):
                v = EG(v)
                for i in range(128):
                    dc, tc = i // 8, i % 8
                    v.wait_ge(cx.h("mm2"), b_mm2 + i + 1)
                    v.tensor_copy(X[:, dc * T + tc * 128:dc * T + tc * 128 + 128],
                                  pb[i % 4][:, 0:128]).then_inc(cx.h("vec"), 1)
                for u in range(32):
                    tc, dj = u // 4, u % 4
                    v.wait_ge(cx.h("mm2"), b_mm2 + 128 + u + 1)
                    for q4 in range(4):
                        dc = dj * 4 + q4
                        v.scalar_tensor_tensor(
                            X[:, dc * T + tc * 128:dc * T + tc * 128 + 128],
                            pb[4 + u % 2][:, q4 * 128:q4 * 128 + 128],
                            1.0,
                            X[:, dc * T + tc * 128:dc * T + tc * 128 + 128],
                            OP.mult, OP.add)
                    v.last.then_inc(cx.h("vec"), 1)
            cx.inc("vec", 160)

        for e_ in range(8):
            b_w = {0: cx.v("gwl"), 1: cx.v("gwl2")}
            b_kvw = {0: cx.v("gw3"), 1: cx.v("gw4")}
            b_mm = cx.v("mm")
            b_act = cx.v("act")
            b_mm2 = cx.v("mm2")
            b_vec = cx.v("vec")
            with nc.Block() as blk:
                @blk.gpsimd
                def _(g, e_=e_):
                    g.wait_ge(cx.h("mm"), b_mm)
                    g.wait_ge(cx.h("mm2"), b_mm2)
                    for hj in range(11):
                        nm = "gwl" if hj % 2 == 0 else "gwl2"
                        if hj >= 2:
                            g.wait_ge(cx.h("mm"), b_mm + 2 * (hj - 1))
                        g.dma_start(out=WR[:, (hj % 2) * D:(hj % 2) * D + D],
                                    in_=w15[e_, hj, :, :, :]
                                    ).then_inc(cx.h(nm), 16)
                    for hj in range(11):
                        nm = "gw3" if hj % 2 == 0 else "gw4"
                        g.dma_start(out=w2_sb[:, hj * D:(hj + 1) * D],
                                    in_=w25r[e_, hj, :, :]).then_inc(cx.h(nm), 16)
                cx.inc("gwl", 16 * 6)
                cx.inc("gwl2", 16 * 5)
                cx.inc("gw3", 16 * 6)
                cx.inc("gw4", 16 * 5)

                @blk.tensor
                def _(t, e_=e_):
                    for hj in range(11):
                        nm = "gwl" if hj % 2 == 0 else "gwl2"
                        t.wait_ge(cx.h(nm), b_w[hj % 2] + 16 * (hj // 2 + 1))
                        for tt in range(2):
                            u = hj * 2 + tt
                            if u >= 2:
                                t.wait_ge(cx.h("act"), b_act + (u - 2) + 1)
                            for c in range(DC):
                                mi = t.matmul(pb[u % 2][:, 0:512],
                                              WR[:, (hj % 2) * D + c * 128:
                                                 (hj % 2) * D + c * 128 + 128],
                                              ln4_bf[:, c * T + tt * 512:
                                                     c * T + tt * 512 + 512],
                                              start=(c == 0),
                                              stop=(c == DC - 1))
                            mi.then_inc(cx.h("mm"), 1)
                    t.wait_ge(cx.h("gw3"), b_kvw[0] + 16 * 6)
                    t.wait_ge(cx.h("gw4"), b_kvw[1] + 16 * 5)
                    t.wait_ge(cx.h("act"), b_act + 22)
                    for u2 in range(32):
                        tc, dj = u2 // 4, u2 % 4
                        if u2 >= 2:
                            t.wait_ge(cx.h("vec"), b_vec + (u2 - 1))
                        for hj in range(11):
                            mi = t.matmul(
                                pb[2 + u2 % 2][:, 0:512],
                                h_sb[:, hj * T + tc * 128:
                                     hj * T + tc * 128 + 128],
                                w2_sb[:, hj * D + dj * 512:
                                      hj * D + dj * 512 + 512],
                                start=(hj == 0), stop=(hj == 10))
                        mi.then_inc(cx.h("mm2"), 1)
                cx.inc("mm", 22)
                cx.inc("mm2", 32)

                @blk.scalar
                def _(s, e_=e_):
                    s = EG(s)
                    s.wait_ge(cx.h("wld"), wld_b1)
                    for hj in range(11):
                        for tt in range(2):
                            u = hj * 2 + tt
                            s.wait_ge(cx.h("mm"), b_mm + u + 1)
                            s.activation(h_sb[:, hj * T + tt * 512:
                                              hj * T + tt * 512 + 512],
                                         pb[u % 2][:, 0:512], AF.Silu,
                                         bias=b1_sb[:, e_ * 11 + hj:
                                                    e_ * 11 + hj + 1]
                                         ).then_inc(cx.h("act"), 1)
                cx.inc("act", 22)

                @blk.vector
                def _(v, e_=e_):
                    v = EG(v)
                    for u2 in range(32):
                        tc, dj = u2 // 4, u2 % 4
                        v.wait_ge(cx.h("mm2"), b_mm2 + u2 + 1)
                        for q4 in range(4):
                            dc = dj * 4 + q4
                            v.scalar_tensor_tensor(
                                X[:, dc * T + tc * 128:dc * T + tc * 128 + 128],
                                pb[2 + u2 % 2][:, q4 * 128:q4 * 128 + 128],
                                comb[:, tc * 8 + e_:tc * 8 + e_ + 1],
                                X[:, dc * T + tc * 128:dc * T + tc * 128 + 128],
                                OP.mult, OP.add)
                        v.last.then_inc(cx.h("vec"), 1)
                cx.inc("vec", 32)

        with nc.Block() as blk:
            @blk.sync
            def _(e):
                for nm in ["mm", "mm2", "act", "vec"]:
                    e.wait_ge(cx.h(nm), cx.v(nm))
                for tc in range(8):
                    e.dma_start(
                        out=out[tc * 128:(tc + 1) * 128, :],
                        in_=X[:, :].rearrange("p (c x t) -> p c x t",
                                              c=DC, x=8)[:, :, tc, :],
                    ).then_inc(cx.h("sto"), 16)
                e.wait_ge(cx.h("sto"), cx.v("sto") + 128)
                cx.inc("sto", 128)

        ph.close()
        _audit(nc, cx)

    return nc, es




def _audit(nc, cx):
    totals = {}
    max_wait = {}
    for inst in nc.all_instructions():
        si = getattr(inst, "sync_info", None)
        if si is None:
            continue
        for u in si.on_update:
            if u.sync_type == "semaphore" and u.update_mode in ("sem-inc", "sem-add-imm"):
                totals[u.ant_name] = totals.get(u.ant_name, 0) + u.update_value
        for w in si.on_wait:
            if w.sync_type == "semaphore" and w.wait_mode == "sem-ge-imm":
                max_wait[w.ant_name] = max(max_wait.get(w.ant_name, 0), w.wait_value)
    errs = []
    for name, (h, v) in cx.sem.items():
        t = totals.get(h.name, 0)
        if t != v:
            errs.append(f"sem {name}: python counter {v} != emitted {t}")
        mw = max_wait.get(h.name, 0)
        if mw > t:
            errs.append(f"sem {name}: wait {mw} exceeds total {t} (hang)")
    if errs:
        raise AssertionError("SEM AUDIT FAILED:\n" + "\n".join(errs))


def kernel(**inputs):
    shared = _stage_shared(inputs)
    cores = [_stage_core(inputs, c) for c in range(NCORES)]
    nc, _es = build_nc()
    in_maps = []
    for c in range(NCORES):
        m = {k: shared[k] for k in
             ["wq5", "wk5", "wv5r", "wo5", "wf5", "ffn_bias_t", "ffn_ws",
              "wqkv5", "wd5", "gw_l", "gate_b", "gate_bias", "w15", "b1r",
              "w25r", "b2", "ident_f", "ident_b", "ones_f", "ones_b",
              "onehot_b", "cos1", "sin1"]}
        m.update(cores[c])
        in_maps.append(m)
    res = run_bass_kernel_spmd(nc, in_maps, core_ids=list(range(NCORES)))
    full = np.empty((2, 4096, D), dtype=np.float32)
    for c in range(NCORES):
        o = np.asarray(res.results[c]["out"])
        full[0, JB * c:JB * (c + 1), :] = o[0:JB]
        full[1, JB * c:JB * (c + 1), :] = o[JB:T]
    return full
